# version 08
# speedup vs baseline: 1.6045x; 1.0934x over previous
import contextlib
import numpy as np
import ml_dtypes
import concourse.bass as bass
import concourse.mybir as mybir
from concourse.bass_utils import run_bass_kernel_spmd

F32 = mybir.dt.float32
BF16 = mybir.dt.bfloat16
AL = mybir.AluOpType
AF = mybir.ActivationFunctionType
AX = mybir.AxisListType.X

NC = 8
B_, L_, D, DI, DS, DTR, H = 2, 16384, 2048, 4096, 16, 128, 8192
E, R = 8, 64
TOK = B_ * L_
TPC = TOK // NC
CPC = DI // NC
NKD = D // 128
NKI = DI // 128
EPS = 1e-5
TT_A = 512
TT_C = 256
FRONT_TILES = TPC // TT_A
LC = 1024
SCAN_CTS = TOK // LC
SCAN_DEBUG = False
BACK_TILES = TPC // TT_C
MODE = "mega"
BACK_SPAN = TPC

ENGS = ("sync", "tensor", "vector", "scalar", "gpsimd")
DMAQ = ("sync", "scalar", "gpsimd")


class Prog:

    def __init__(self, nc, es):
        self.nc = nc
        self.es = es
        self.ops = {e: [] for e in ENGS}
        self.sem = {e: es.enter_context(nc.semaphore(f"es_{e}")) for e in ENGS}
        self.cnt = {e: 0 for e in ENGS}
        self.lastwait = {e: {} for e in ENGS}
        self.ch = {}

    def _channel(self, name, eng):
        if name not in self.ch:
            self.ch[name] = dict(sem=self.es.enter_context(self.nc.semaphore(f"dc_{name}")),
                                 eng=eng, total=0, closed=0, open=False)
        c = self.ch[name]
        assert c["eng"] == eng, f"channel {name} is owned by {c['eng']}"
        return c

    def dma(self, name, eng, out, in_):
        c = self._channel(name, eng)
        if not c["open"]:
            if c["closed"] > 0:
                self.ops[eng].append(
                    lambda ifc, sem=c["sem"], v=c["closed"]: ifc.wait_ge(sem, v))
            c["open"] = True
        c["total"] += 16
        sem = c["sem"]
        def cl(ifc, out=out, in_=in_, sem=sem):
            ifc.dma_start(out=out, in_=in_).then_inc(sem, 16)
        self.ops[eng].append(cl)

    def close(self, name):
        c = self.ch[name]
        c["open"] = False
        c["closed"] = c["total"]
        return ("d", name, c["closed"])

    def run(self, eng, fn, inc=False):
        sem = self.sem[eng]
        if eng in ("vector", "scalar", "gpsimd"):
            self.ops[eng].append(lambda ifc: ifc.drain())
        if inc:
            self.cnt[eng] += 1

            def cl(ifc, fn=fn, sem=sem):
                fn(ifc).then_inc(sem, 1)
        else:
            def cl(ifc, fn=fn):
                fn(ifc)
        self.ops[eng].append(cl)
        return ("e", eng, self.cnt[eng]) if inc else None

    def wait(self, eng, tok):
        if tok is None:
            return
        kind, src, val = tok
        if val <= 0:
            return
        if kind == "e" and src == eng:
            return
        key = (kind, src)
        if self.lastwait[eng].get(key, 0) >= val:
            return
        self.lastwait[eng][key] = val
        if kind == "d":
            c = self.ch[src]
            assert val <= c["closed"], f"wait on open batch of channel {src}"
            sem = c["sem"]
        else:
            sem = self.sem[src]
        self.ops[eng].append(lambda ifc, sem=sem, val=val: ifc.wait_ge(sem, val))

    def waits(self, eng, *toks):
        for t in toks:
            self.wait(eng, t)

    def mark(self, eng):
        sem = self.sem[eng]
        prev = self.cnt[eng]
        if eng == "gpsimd" and prev > 0:
            self.ops[eng].append(lambda ifc, sem=sem, v=prev: ifc.wait_ge(sem, v))
        self.cnt[eng] += 1
        self.ops[eng].append(lambda ifc: ifc.drain())
        self.ops[eng].append(lambda ifc, sem=sem: ifc.sem_inc(sem, 1))
        return ("e", eng, self.cnt[eng])

    def barrier(self, engs=ENGS):
        toks = {e: self.mark(e) for e in engs}
        for e in engs:
            for o in engs:
                if o != e:
                    self.wait(e, toks[o])

    def drain_all_channels(self, eng):
        for name, c in self.ch.items():
            if c["total"] > 0:
                self.ops[eng].append(
                    lambda ifc, sem=c["sem"], v=c["total"]: ifc.wait_ge(sem, v))

    def finalize(self, block):
        for e in ENGS:
            if not self.ops[e]:
                continue
            deco = getattr(block, e)
            ops = self.ops[e]

            def body(ifc, ops=ops):
                for cl in ops:
                    cl(ifc)
            deco(body)



def emit_ln(p, al, N, x32c, ready_tok, out16c=None, out32c=None, affine=None,
            tmp32=None):
    nc = p.nc
    psA = al["psA"][0:1, 0:N]
    psB = al["psB"][0:1, 0:N]
    p.waits("scalar", ready_tok)
    cons = [None, None]
    for k in range(NKD):
        b = k % 2
        xbt = al["xbt"][:, b * al["ln_n"]: b * al["ln_n"] + N]
        p.waits("scalar", cons[b])
        tk = p.run("scalar", lambda s, o=xbt, i=x32c[k]: s.copy(o, i), inc=True)
        p.waits("tensor", tk)
        p.run("tensor", lambda t, o=psA, w=al["ones_col16"], r=xbt,
              s=(k == 0), e=(k == NKD - 1): t.matmul(o, w[:], r, start=s, stop=e))
        cons[b] = p.mark("tensor")
    for k in range(NKD):
        b = k % 2
        sqt = al["sqt"][:, b * al["ln_n"]: b * al["ln_n"] + N]
        p.waits("scalar", cons[b])
        tk = p.run("scalar", lambda s, o=sqt, i=x32c[k]: s.activation(o, i, AF.Square), inc=True)
        p.waits("tensor", tk)
        p.run("tensor", lambda t, o=psB, w=al["ones_col16"], r=sqt,
              s=(k == 0), e=(k == NKD - 1): t.matmul(o, w[:], r, start=s, stop=e))
        cons[b] = p.mark("tensor")
    tok_mm = cons[(NKD - 1) % 2]

    m = al["st_m"][:, 0:N]
    q = al["st_q"][:, 0:N]
    var = al["st_var"][:, 0:N]
    sd = al["st_sd"][:, 0:N]
    r_ = al["st_r"][:, 0:N]
    nmr = al["st_nmr"][:, 0:N]
    t1 = al["st_t1"][:, 0:N]
    t2 = al["st_t2"][:, 0:N]
    p.waits("vector", tok_mm)
    p.run("vector", lambda v, o=m, i=psA: v.tensor_scalar(o, i, 1.0 / D, None, AL.mult))
    p.run("vector", lambda v, o=q, i=psB: v.tensor_scalar(o, i, 1.0 / D, None, AL.mult))
    p.run("vector", lambda v, o=t1, a=m: v.tensor_tensor(o, a, a, AL.mult))
    p.run("vector", lambda v, o=var, a=q, b=t1: v.tensor_tensor(o, a, b, AL.subtract))
    tv = p.run("vector", lambda v, o=t2, i=var: v.tensor_scalar(o, i, EPS, None, AL.add), inc=True)
    p.waits("scalar", tv)
    ts_ = p.run("scalar", lambda s, o=sd, i=t2: s.activation(o, i, AF.Sqrt), inc=True)
    p.waits("vector", ts_)
    p.run("vector", lambda v, o=r_, i=sd: v.reciprocal(o, i))
    p.run("vector", lambda v, o=t1, a=m, b=r_: v.tensor_tensor(o, a, b, AL.mult))
    tv2 = p.run("vector", lambda v, o=nmr, i=t1: v.tensor_scalar(o, i, -1.0, None, AL.mult), inc=True)

    psR = al["psR"][:, 0:N]
    psN = al["psN"][:, 0:N]
    p.waits("tensor", tv2)
    p.run("tensor", lambda t, o=psR, w=al["ones_row32"], r=r_: t.matmul(o, w[:], r, start=True, stop=True))
    tok_rep = p.run("tensor", lambda t, o=psN, w=al["ones_row32"], r=nmr: t.matmul(o, w[:], r, start=True, stop=True), inc=True)

    p.waits("vector", tok_rep)
    for k in range(NKD):
        p.run("vector", lambda v, o=tmp32[0:128, 0:N], a=x32c[k], b=psR: v.tensor_tensor(o, a, b, AL.mult))
        if out32c is not None:
            p.run("vector", lambda v, o=out32c[k], a=tmp32[0:128, 0:N], b=psN: v.tensor_tensor(o, a, b, AL.add))
        if out16c is not None and affine is None:
            src = out32c[k] if out32c is not None else None
            if src is not None:
                p.run("vector", lambda v, o=out16c[k], i=src: v.tensor_copy(o, i))
            else:
                p.run("vector", lambda v, o=out16c[k], a=tmp32[0:128, 0:N], b=psN: v.tensor_tensor(o, a, b, AL.add))
        if out16c is not None and affine is not None:
            wcol, bcol = affine
            p.run("vector", lambda v, o=al["tmp32b"][0:128, 0:N], a=tmp32[0:128, 0:N], b=psN: v.tensor_tensor(o, a, b, AL.add))
            p.run("vector", lambda v, o=out16c[k], i=al["tmp32b"][0:128, 0:N], w=wcol[k], b=bcol[k]:
                  v.tensor_scalar(o, i, w, b, AL.mult, AL.add))
    return p.mark("vector")


def emit_moe(p, al, N, v16c, vtok, x32c, xtok, out32c, scale, pre_cb=None, post_cb=None):
    nc = p.nc
    nst = (N + 127) // 128
    psR = al["psRt"]
    lgs = al["lgs"]
    mx = al["mx"]
    wsc = al["wsc"]
    sm = al["sm"]
    maskb = al["maskb"]
    p.waits("tensor", vtok)
    for st in range(nst):
        Ns = min(128, N - 128 * st)
        out = psR[0:Ns, st * 8:(st + 1) * 8]
        for k in range(NKD):
            p.run("tensor", lambda t, o=out, w=v16c[k][:, st * 128: st * 128 + Ns],
                  r=al["wrT"][:, k * 8:(k + 1) * 8], s=(k == 0):
                  t.matmul(o, w, r, start=s, stop=False))
        p.run("tensor", lambda t, o=out, w=al["ones_row16"][:, 0:Ns], r=al["brT"][:]:
              t.matmul(o, w, r, start=False, stop=True))
    tok_r = p.mark("tensor")
    p.waits("scalar", tok_r)
    ts_ = None
    for st in range(nst):
        Ns = min(128, N - 128 * st)
        ts_ = p.run("scalar", lambda s, o=lgs[0:Ns, st * 8:(st + 1) * 8],
                    i=psR[0:Ns, st * 8:(st + 1) * 8]: s.copy(o, i), inc=True)
    p.waits("vector", ts_)
    for st in range(nst):
        Ns = min(128, N - 128 * st)
        p.run("vector", lambda v, o=mx[0:Ns, st:st + 1], i=lgs[0:Ns, st * 8:(st + 1) * 8]:
              v.reduce_max(o, i, axis=AX))
    for st in range(nst):
        Ns = min(128, N - 128 * st)
        p.run("vector", lambda v, o=al["nmx"][0:Ns, st:st + 1], i=mx[0:Ns, st:st + 1]:
              v.tensor_scalar(o, i, -1.0, None, AL.mult))
    tv = p.mark("vector")
    p.waits("scalar", tv)
    for st in range(nst):
        Ns = min(128, N - 128 * st)
        p.run("scalar", lambda s, o=al["exps"][0:Ns, st * 8:(st + 1) * 8], i=lgs[0:Ns, st * 8:(st + 1) * 8],
              b=al["nmx"][0:Ns, st:st + 1]: s.activation(o, i, AF.Exp, bias=b))
    ts2 = p.mark("scalar")
    p.waits("vector", ts2)
    for st in range(nst):
        Ns = min(128, N - 128 * st)
        p.run("vector", lambda v, o=sm[0:Ns, st:st + 1], i=al["exps"][0:Ns, st * 8:(st + 1) * 8]:
              v.reduce_sum(o, i, axis=AX))
    for st in range(nst):
        Ns = min(128, N - 128 * st)
        p.run("vector", lambda v, o=al["w0"][0:Ns, st:st + 1], i=sm[0:Ns, st:st + 1]: v.reciprocal(o, i))
    for st in range(nst):
        Ns = min(128, N - 128 * st)
        p.run("vector", lambda v, o=wsc[0:Ns, st:st + 1], i=al["w0"][0:Ns, st:st + 1]:
              v.tensor_scalar(o, i, scale / R, None, AL.mult))
    for st in range(nst):
        Ns = min(128, N - 128 * st)
        p.run("vector", lambda v, o=maskb[0:Ns, st * 8:(st + 1) * 8], i=lgs[0:Ns, st * 8:(st + 1) * 8],
              a=mx[0:Ns, st:st + 1], b=wsc[0:Ns, st:st + 1]:
              v.tensor_scalar(o, i, a, b, AL.is_equal, AL.mult))
    tv2 = p.mark("vector")
    psT = al["psT"]
    p.waits("tensor", tv2)
    for st in range(nst):
        Ns = min(128, N - 128 * st)
        p.run("tensor", lambda t, o=psT[0:8, st * 128: st * 128 + Ns],
              i=maskb[0:Ns, st * 8:(st + 1) * 8], id_=al["ident"], n=Ns:
              t.transpose(o, i, id_[0:n, 0:n]))
    tok_t = p.mark("tensor")
    p.waits("scalar", tok_t)
    ts3 = p.run("scalar", lambda s, o=al["maskT"][0:8, 0:N], i=psT[0:8, 0:N]: s.copy(o, i), inc=True)
    psM = al["psM"][:, 0:N]
    psH = al["psH"][:, 0:N]
    hm = al["hm16"]
    tok_hm = []
    for c in range(4):
        p.waits("tensor", ts3)
        tmr = p.run("tensor", lambda t, o=psM, w=al["repm"][:, c * 128:(c + 1) * 128],
              r=al["maskT"][0:8, 0:N]: t.matmul(o, w, r, start=True, stop=True), inc=True)
        p.waits("scalar", tmr)
        tmc = p.run("scalar", lambda s, o=al["mrep16"][:, 0:N], i=psM: s.copy(o, i), inc=True)
        for k in range(NKD):
            p.run("tensor", lambda t, o=psH, w=al["aT"][:, (k * 4 + c) * 128:(k * 4 + c) * 128 + 128],
                  r=v16c[k], s=(k == 0), e=(k == NKD - 1): t.matmul(o, w, r, start=s, stop=e))
        tmh = p.mark("tensor")
        p.waits("scalar", tmh)
        tch = p.run("scalar", lambda s, o=al["h16"][:, 0:N], i=psH: s.copy(o, i), inc=True)
        p.waits("vector", tch, tmc)
        thm = p.run("vector", lambda v, o=hm[:, c * al["ffn_n"]: c * al["ffn_n"] + N],
              a=al["h16"][:, 0:N], b=al["mrep16"][:, 0:N]: v.tensor_tensor(o, a, b, AL.mult), inc=True)
        tok_hm.append(thm)
        p.waits("tensor", thm, tch, tmc)
    psS = al["psS"][:, 0:N]
    p.waits("tensor", tok_hm[-1], xtok)
    out_tok = None
    for mchunk in range(NKD):
        for c in range(4):
            p.run("tensor", lambda t, o=psS, w=al["bT"][:, (c * NKD + mchunk) * 128:(c * NKD + mchunk) * 128 + 128],
                  r=hm[:, c * al["ffn_n"]: c * al["ffn_n"] + N], s=(c == 0), e=(c == 3):
                  t.matmul(o, w, r, start=s, stop=e))
        tms = p.mark("tensor")
        p.waits("vector", tms)
        if pre_cb is not None:
            p.waits("vector", pre_cb(mchunk))
        out_tok = p.run("vector", lambda v, o=out32c[mchunk], a=x32c[mchunk], b=psS:
                        v.tensor_tensor(o, a, b, AL.add), inc=True)
        p.waits("tensor", out_tok)
        if post_cb is not None:
            post_cb(mchunk, out_tok)
    return out_tok


def host_f32(a):
    return np.ascontiguousarray(np.asarray(a, dtype=np.float32))


def colsT(vec, nchunk):
    return np.ascontiguousarray(np.asarray(vec, dtype=np.float32).reshape(nchunk, 128).T)



def build_front(nc=None, p=None, io=None, finalize=True, mid_cb=None):
    if nc is None:
        nc = bass.Bass("TRN2", target_bir_lowering=False, debug=False)
    dram = nc.dram_tensor
    if io is None:
        io = {}
        for nm, shp in (("x_in", [D, TPC + 3]), ("halo_mask", [128, 3]),
                        ("w_in_T", [D, 2 * DI]), ("w_xp_T", [DI, 160]),
                        ("a_T", [D, E * R]), ("b_T", [E * R, D]), ("wr_T", [D, E]),
                        ("br_r", [1, E]), ("conv_w_t", [128, NKI * 4]),
                        ("conv_b_t", [128, NKI]), ("ln1_w_t", [128, NKD]),
                        ("ln1_b_t", [128, NKD]), ("repm_in", [8, 512])):
            io[nm] = dram(nm, shp, F32, kind="ExternalInput").ap()
        io["xc_o"] = dram("xc_o", [DI, TPC], BF16, kind="ExternalOutput").ap()
        io["z_o"] = dram("z_o", [DI, TPC], BF16, kind="ExternalOutput").ap()
        io["xd_o"] = dram("xd_o", [160, TPC], BF16, kind="ExternalOutput").ap()
        io["x1_o"] = dram("x1_o", [D, TPC], F32, kind="ExternalOutput").ap()
        io["u_o"] = dram("u_o", [D, TPC], F32, kind="ExternalOutput").ap()
    (x_in, halo_mask, w_in_T, w_xp_T, a_T, b_T, wr_T, br_r, conv_w_t, conv_b_t,
     ln1_w_t, ln1_b_t, repm_in, xc_o, z_o, xd_o, x1_o, u_o) = (
        io["x_in"], io["halo_mask"], io["w_in_T"], io["w_xp_T"], io["a_T"],
        io["b_T"], io["wr_T"], io["br_r"], io["conv_w_t"], io["conv_b_t"],
        io["ln1_w_t"], io["ln1_b_t"], io["repm_in"], io["xc_o"], io["z_o"],
        io["xd_o"], io["x1_o"], io["u_o"])

    w_in_b = dram("f_w_in_b", [D, 2 * DI], BF16)
    w_xp_b = dram("f_w_xp_b", [DI, 160], BF16)
    a_b = dram("f_a_b", [D, E * R], BF16)
    b_b = dram("f_b_b", [E * R, D], BF16)
    wr_b = dram("f_wr_b", [D, E], BF16)
    br_b = dram("f_br_b", [1, E], BF16)
    t_s = dram("f_t_s", [D, TPC + 3], BF16).ap()

    PF = "f_"
    es = contextlib.ExitStack()
    with es:
        if p is None:
            p = Prog(nc, es)
        p.dma("prep", "gpsimd", w_in_b.ap(), w_in_T)
        p.dma("prep", "gpsimd", w_xp_b.ap(), w_xp_T)
        p.dma("prep", "gpsimd", a_b.ap(), a_T)
        p.dma("prep", "gpsimd", b_b.ap(), b_T)
        p.dma("prep", "gpsimd", wr_b.ap(), wr_T)
        p.dma("prep", "gpsimd", al_repm_dummy := None or br_b.ap(), br_r)
        tok_prep = p.close("prep")

        N = TT_A
        with contextlib.ExitStack() as e0:
            sb0 = lambda name, shape, dt=F32: e0.enter_context(nc.sbuf_tensor(PF + "p0" + name, shape, dt))[:]
            pst0 = lambda name, shape, dt=F32: e0.enter_context(nc.psum_tensor(PF + name, shape, dt))[:]
            al = {}
            bufA = sb0("bufA", [128, NKD * N])
            bufB = sb0("bufB", [128, NKD * N])
            al["xbt"] = sb0("xbt", [128, 2 * N], BF16)
            al["sqt"] = sb0("sqt", [128, 2 * N], BF16)
            al["ln_n"] = N
            vt16 = sb0("vt16", [128, NKD * N], BF16)
            al["tmp32"] = sb0("tmp32", [128, N])
            al["tmp32b"] = sb0("tmp32b", [128, N])
            for nm in ("st_m", "st_q", "st_var", "st_sd", "st_r", "st_nmr", "st_t1", "st_t2"):
                al[nm] = sb0(nm, [1, N])
            al["ones_col16"] = e0.enter_context(nc.sbuf_tensor(PF + "p0ones_c", [128, 1], BF16))
            al["ones_row32"] = e0.enter_context(nc.sbuf_tensor(PF + "p0ones_r", [1, 128], F32))
            al["ones_row16"] = sb0("ones_r16", [1, 128], BF16)
            al["ident"] = e0.enter_context(nc.sbuf_tensor(PF + "p0ident", [128, 128], BF16))
            zeros128 = e0.enter_context(nc.sbuf_tensor(PF + "p0zeros", [128, 128], BF16))
            al["lgs"] = sb0("lgs", [128, 32])
            al["mx"] = sb0("mx", [128, 4])
            al["nmx"] = sb0("nmx", [128, 4])
            al["sm"] = sb0("sm", [128, 4])
            al["wsc"] = sb0("wsc", [128, 4])
            al["w0"] = sb0("w0", [128, 4])
            al["exps"] = sb0("exps", [128, 32])
            al["maskb"] = sb0("maskb", [128, 32], BF16)
            al["maskT"] = sb0("maskT", [8, N], BF16)
            al["repm"] = sb0("repm", [8, 512], BF16)
            al["mrep16"] = sb0("mrep16", [128, N], BF16)
            al["h16"] = sb0("h16", [128, N], BF16)
            al["hm16"] = sb0("hm16", [128, 4 * N], BF16)
            al["ffn_n"] = N
            al["aT"] = sb0("aT", [128, NKD * 4 * 128], BF16)
            al["bT"] = sb0("bT", [128, 4 * NKD * 128], BF16)
            al["wrT"] = sb0("wrT", [128, NKD * 8], BF16)
            al["brT"] = e0.enter_context(nc.sbuf_tensor(PF + "p0brT", [1, 8], BF16))
            al["psA"] = pst0("p0psA", [1, 512])
            al["psB"] = pst0("p0psB", [1, 512])
            al["psR"] = pst0("p0psR", [128, 512])
            al["psN"] = pst0("p0psN", [128, 512])
            al["psS"] = pst0("p0psS", [128, 512])
            al["psRt"] = al["psS"][:, 0:32]
            al["psM"] = al["psR"]
            al["psH"] = al["psN"]
            psX0 = pst0("p0psX", [128, 512])
            al["psT"] = psX0[0:8, :].bitcast(BF16)
            ln1w = sb0("ln1w", [128, NKD])
            ln1b = sb0("ln1b", [128, NKD])
            hmask = sb0("hmask", [128, 3])
            tok_z = p.run("vector", lambda v: v.memset(zeros128[:], 0.0), inc=True)
            p.waits("gpsimd", tok_z)
            tok_id = p.run("gpsimd", lambda g: g.affine_select(
                out=al["ident"][:], in_=zeros128[:], compare_op=AL.not_equal,
                fill=1.0, base=0, pattern=[[-1, 128]], channel_multiplier=1), inc=True)
            tok_c1 = p.run("gpsimd", lambda g: g.memset(al["ones_col16"][:], 1.0), inc=True)
            tok_c2 = p.run("gpsimd", lambda g: g.memset(al["ones_row32"][:], 1.0), inc=True)
            tok_c3 = p.run("gpsimd", lambda g: g.memset(al["ones_row16"][:], 1.0), inc=True)
            p.waits("sync", tok_prep)
            for k in range(NKD):
                p.dma("res", "sync", al["aT"][:, k * 512:(k + 1) * 512],
                      a_b.ap()[k * 128:(k + 1) * 128, :])
            for c in range(4):
                p.dma("res", "sync", al["bT"][:, c * 2048:(c + 1) * 2048],
                      b_b.ap()[c * 128:(c + 1) * 128, :])
            for k in range(NKD):
                p.dma("res", "sync", al["wrT"][:, k * 8:(k + 1) * 8],
                      wr_b.ap()[k * 128:(k + 1) * 128, :])
            p.dma("res", "sync", al["brT"][:], br_b.ap())
            p.dma("res", "sync", ln1w, ln1_w_t)
            p.dma("res", "sync", ln1b, ln1_b_t)
            p.dma("res", "sync", hmask, halo_mask)
            tok_res = p.close("res")
            p.dma("prep0", "gpsimd", al["repm"], repm_in)
            tok_rp = p.close("prep0")
            for e in ENGS:
                p.waits(e, tok_res, tok_rp, tok_id, tok_c1, tok_c2, tok_c3)
            p.barrier()
            ln1w_c = [ln1w[:, k:k + 1] for k in range(NKD)]
            ln1b_c = [ln1b[:, k:k + 1] for k in range(NKD)]
            store_toks = {}

            def p0_pass(N_, xoff, halo_pass):
                xc = [bufA[:, k * TT_A: k * TT_A + N_] for k in range(NKD)]
                x1c = [bufB[:, k * TT_A: k * TT_A + N_] for k in range(NKD)]
                vc = [vt16[:, k * TT_A: k * TT_A + N_] for k in range(NKD)]
                p.waits("vector", store_toks.get("x1"), store_toks.get("u"), store_toks.get("t"))
                p.waits("sync", store_toks.get("x1"), store_toks.get("u"), store_toks.get("t"),
                        store_toks.get("tln"))
                for k in range(NKD):
                    p.dma("x", "sync", xc[k], x_in[k * 128:(k + 1) * 128, xoff: xoff + N_])
                tok_x = p.close("x")
                tok_v = emit_ln(p, al, N_, xc, tok_x, out16c=vc, tmp32=al["tmp32"])
                tok_x1 = emit_moe(p, al, N_, vc, tok_v, xc, tok_x, x1c, 0.5)
                if not halo_pass:
                    p.waits("sync", tok_x1)
                    for k in range(NKD):
                        p.dma("ox1", "sync", x1_o[k * 128:(k + 1) * 128, xoff - 3: xoff - 3 + N_], x1c[k])
                    store_toks["x1"] = p.close("ox1")
                uc = xc
                tok_u = emit_ln(p, al, N_, x1c, tok_x1, out32c=uc, tmp32=al["tmp32"])
                if not halo_pass:
                    p.waits("sync", tok_u)
                    for k in range(NKD):
                        p.dma("ou", "sync", u_o[k * 128:(k + 1) * 128, xoff - 3: xoff - 3 + N_], uc[k])
                    store_toks["u"] = p.close("ou")
                tok_t = emit_ln(p, al, N_, uc, tok_u, out16c=vc, affine=(ln1w_c, ln1b_c),
                                tmp32=al["tmp32"])
                if halo_pass:
                    p.waits("vector", tok_t)
                    for k in range(NKD):
                        tok_t = p.run("vector", lambda v, o=al["tmp32"][:, 0:3],
                                      a=vt16[:, k * TT_A: k * TT_A + 3]:
                                      v.tensor_tensor(o, a, hmask, AL.mult), inc=True)
                        tok_t = p.run("vector", lambda v, o=vt16[:, k * TT_A: k * TT_A + 3],
                                      i=al["tmp32"][:, 0:3]: v.tensor_copy(o, i), inc=True)
                    p.waits("sync", tok_t)
                    for k in range(NKD):
                        p.dma("ot", "sync", t_s[k * 128:(k + 1) * 128, 0:3],
                              vt16[:, k * TT_A: k * TT_A + 3])
                    store_toks["t"] = p.close("ot")
                else:
                    p.waits("sync", tok_t)
                    for k in range(NKD):
                        p.dma("ot", "sync", t_s[k * 128:(k + 1) * 128, xoff: xoff + N_], vc[k])
                    store_toks["t"] = p.close("ot")
                store_toks["tln"] = tok_t

            p0_pass(4, 0, True)
            for t in range(TPC // N):
                p0_pass(N, 3 + t * N, False)
            for name in list(p.ch):
                if p.ch[name]["open"]:
                    p.close(name)
            p.drain_all_channels("sync")
            p.barrier()

        PT = 2048
        with contextlib.ExitStack() as e1:
            sb1 = lambda name, shape, dt=F32: e1.enter_context(nc.sbuf_tensor(PF + "p1" + name, shape, dt))[:]
            pst1 = lambda name, shape, dt=F32: e1.enter_context(nc.psum_tensor(PF + name, shape, dt))[:]
            t16 = sb1("t16", [128, NKD * (PT + 3)], BF16)
            wbuf = [sb1(f"wbuf{b}", [128, NKD * 512], BF16) for b in range(2)]
            work = sb1("work", [128, 4 * (PT + 3)], BF16)
            xc16 = sb1("xc16", [128, 4 * PT], BF16)
            z16 = sb1("z16", [128, 2 * PT], BF16)
            cva = sb1("cva", [128, PT])
            cvb = sb1("cvb", [128, PT])
            convw = sb1("convw", [128, NKI * 4])
            convb = sb1("convb", [128, NKI])
            psMM = [pst1("p1psMM", [128, 512]), pst1("p1psMM2", [128, 512])]
            p.dma("res1", "sync", convw, conv_w_t)
            p.dma("res1", "sync", convb, conv_b_t)
            tok_res1 = p.close("res1")
            for e in ("sync", "tensor", "vector", "scalar"):
                p.waits(e, tok_res1)
            wfree = [None, None]
            evac = [None, None]
            ldone = {}
            for t in range(TPC // PT):
                xoff = t * PT
                p.waits("sync", ldone.get("t_cons"))
                for k in range(NKD):
                    p.dma("tl", "sync", t16[:, k * (PT + 3): k * (PT + 3) + PT + 3],
                          t_s[k * 128:(k + 1) * 128, xoff: xoff + PT + 3])
                tok_t = p.close("tl")
                p.waits("tensor", tok_t)
                for m_abs in range(2 * NKI):
                    g, mc = m_abs // 4, m_abs % 4
                    par = g % 2
                    if mc == 0:
                        p.waits("sync", wfree[par])
                        for k in range(NKD):
                            p.dma(f"w{par}", "sync", wbuf[par][:, k * 512:(k + 1) * 512],
                                  w_in_b.ap()[k * 128:(k + 1) * 128, g * 512:(g + 1) * 512])
                        tok_w = p.close(f"w{par}")
                        p.waits("tensor", tok_w)
                    is_xm = m_abs < NKI
                    cslot = m_abs % 4
                    zslot = m_abs % 2
                    npieces = PT // 512
                    dst = work[:, cslot * (PT + 3): cslot * (PT + 3) + PT + 3] if is_xm else None
                    if is_xm:
                        p.waits("scalar", ldone.get(("work", cslot)))
                    else:
                        p.waits("scalar", ldone.get(("z", zslot)))
                    total = PT + 3 if is_xm else PT
                    toff = 0 if is_xm else 3
                    for h in range((total + 511) // 512):
                        c0 = h * 512
                        cn = min(512, total - c0)
                        bank = (m_abs * 5 + h) % 2
                        ps = psMM[bank]
                        p.waits("tensor", evac[bank])
                        for k in range(NKD):
                            p.run("tensor", lambda tt, o=ps[:, 0:cn],
                                  w=wbuf[par][:, k * 512 + mc * 128: k * 512 + mc * 128 + 128],
                                  r=t16[:, k * (PT + 3) + toff + c0: k * (PT + 3) + toff + c0 + cn],
                                  st=(k == 0), e_=(k == NKD - 1): tt.matmul(o, w, r, start=st, stop=e_))
                        tmm = p.mark("tensor")
                        p.waits("scalar", tmm)
                        if is_xm:
                            tev = p.run("scalar", lambda sc, o=dst[:, c0:c0 + cn], i=ps[:, 0:cn]:
                                        sc.copy(o, i), inc=True)
                        else:
                            tev = p.run("scalar", lambda sc, o=z16[:, zslot * PT + c0: zslot * PT + c0 + cn],
                                        i=ps[:, 0:cn]: sc.copy(o, i), inc=True)
                        evac[bank] = tev
                    if is_xm:
                        cw = [convw[:, m_abs * 4 + kk: m_abs * 4 + kk + 1] for kk in range(4)]
                        p.waits("vector", tev, ldone.get(("cv", 0)))
                        p.run("vector", lambda v, o=cva[:, 0:PT], i=dst[:, 0:PT], w=cw[0]:
                              v.tensor_scalar(o, i, w, None, AL.mult))
                        for kk in range(1, 4):
                            src, dsx = (cva, cvb) if kk % 2 == 1 else (cvb, cva)
                            p.run("vector", lambda v, o=dsx[:, 0:PT], i=dst[:, kk:kk + PT], w=cw[kk], b=src[:, 0:PT]:
                                  v.scalar_tensor_tensor(o, i, w, b, AL.mult, AL.add))
                        tok_cv = p.mark("vector")
                        ldone[("work", cslot)] = tok_cv
                        xslot = m_abs % 4
                        p.waits("scalar", tok_cv, ldone.get(("xc", xslot)))
                        tok_xc = p.run("scalar", lambda sc, o=xc16[:, xslot * PT: xslot * PT + PT],
                                       i=cvb[:, 0:PT], b=convb[:, m_abs:m_abs + 1]:
                                       sc.activation(o, i, AF.Silu, bias=b), inc=True)
                        ldone[("cv", 0)] = tok_xc
                        p.waits("sync", tok_xc)
                        p.dma(f"oxc{xslot}", "sync",
                              xc_o[m_abs * 128:(m_abs + 1) * 128, xoff: xoff + PT],
                              xc16[:, xslot * PT: xslot * PT + PT])
                        ldone[("xc", xslot)] = p.close(f"oxc{xslot}")
                    else:
                        p.waits("sync", tev)
                        p.dma(f"oz{zslot}", "sync",
                              z_o[(m_abs - NKI) * 128:(m_abs - NKI + 1) * 128, xoff: xoff + PT],
                              z16[:, zslot * PT:(zslot + 1) * PT])
                        ldone[("z", zslot)] = p.close(f"oz{zslot}")
                    if mc == 3:
                        wfree[par] = p.mark("tensor")
                ldone["t_cons"] = p.mark("tensor")
            for name in list(p.ch):
                if p.ch[name]["open"]:
                    p.close(name)
            p.drain_all_channels("sync")
            p.barrier()

        if mid_cb is not None:
            mid_cb()
        with contextlib.ExitStack() as e2:
            sb2 = lambda name, shape, dt=F32: e2.enter_context(nc.sbuf_tensor(PF + "p2" + name, shape, dt))[:]
            pst2 = lambda name, shape, dt=F32: e2.enter_context(nc.psum_tensor(PF + name, shape, dt))[:]
            xpT = sb2("xpT", [128, NKI * 160], BF16)
            xcr = sb2("xcr", [128, 2 * 512], BF16)
            xd16 = sb2("xd16", [128, 2 * 512], BF16)
            xd16b = sb2("xd16b", [32, 2 * 512], BF16)
            psD = pst2("p2psD", [128, 512])
            psD2 = pst2("p2psD2", [32, 512])
            for kk in range(NKI):
                p.dma("res2", "sync", xpT[:, kk * 160:(kk + 1) * 160],
                      w_xp_b.ap()[kk * 128:(kk + 1) * 128, :])
            tok_res2 = p.close("res2")
            p.waits("tensor", tok_res2)
            ldone = {}
            for piece in range(TPC // 512):
                xoff = piece * 512
                psl = piece % 2
                for k in range(NKI):
                    ksl = k % 2
                    p.waits("sync", ldone.get(("xcr", ksl)))
                    p.dma("xcl", "sync", xcr[:, ksl * 512:(ksl + 1) * 512],
                          xc_o[k * 128:(k + 1) * 128, xoff: xoff + 512])
                    tok_xc = p.close("xcl")
                    p.waits("tensor", tok_xc, ldone.get("evac"))
                    p.run("tensor", lambda tt, o=psD[:, 0:512], w=xpT[:, k * 160: k * 160 + 128],
                          r=xcr[:, ksl * 512:(ksl + 1) * 512], st=(k == 0), e_=(k == NKI - 1):
                          tt.matmul(o, w, r, start=st, stop=e_))
                    p.run("tensor", lambda tt, o=psD2[:, 0:512], w=xpT[:, k * 160 + 128: k * 160 + 160],
                          r=xcr[:, ksl * 512:(ksl + 1) * 512], st=(k == 0), e_=(k == NKI - 1):
                          tt.matmul(o, w, r, start=st, stop=e_))
                    ldone[("xcr", ksl)] = p.mark("tensor")
                tok_mm = ldone[("xcr", (NKI - 1) % 2)]
                p.waits("scalar", tok_mm, ldone.get(("xd", psl)))
                p.run("scalar", lambda sc, o=xd16[:, psl * 512:(psl + 1) * 512], i=psD:
                      sc.copy(o, i[:, 0:512]))
                tok_xd = p.run("scalar", lambda sc, o=xd16b[:, psl * 512:(psl + 1) * 512], i=psD2:
                               sc.copy(o, i[:, 0:512]), inc=True)
                ldone["evac"] = tok_xd
                p.waits("sync", tok_xd)
                p.dma(f"oxd{psl}", "sync", xd_o[0:128, xoff: xoff + 512],
                      xd16[:, psl * 512:(psl + 1) * 512])
                p.dma(f"oxd{psl}", "sync", xd_o[128:160, xoff: xoff + 512],
                      xd16b[:, psl * 512:(psl + 1) * 512])
                ldone[("xd", psl)] = p.close(f"oxd{psl}")

        if finalize:
            for name in list(p.ch):
                if p.ch[name]["open"]:
                    p.close(name)
            p.drain_all_channels("sync")
            block = es.enter_context(nc.Block())
            p.finalize(block)
    return nc



def build_scan(nc=None, p=None, io=None, finalize=True):
    if nc is None:
        nc = bass.Bass("TRN2", target_bir_lowering=False, debug=False)
    dram = nc.dram_tensor
    if io is None:
        io = {}
        xc_in = dram("xc_in", [CPC, TOK], BF16, kind="ExternalInput").ap()
        z_in = dram("z_in", [CPC, TOK], BF16, kind="ExternalInput").ap()
        xd_in = dram("xd_in", [160, TOK], BF16, kind="ExternalInput").ap()
        io["wdt_j"] = dram("wdt_j", [DTR, CPC], F32, kind="ExternalInput").ap()
        io["db_j"] = dram("db_j", [128, CPC // 128], F32, kind="ExternalInput").ap()
        io["alog_j"] = dram("alog_j", [CPC, DS], F32, kind="ExternalInput").ap()
        io["dsk_j"] = dram("dsk_j", [128, CPC // 128], F32, kind="ExternalInput").ap()
        yg_ = dram("yg_o", [CPC, TOK], BF16, kind="ExternalOutput").ap()
        io["xc_ap"] = lambda r0, nr, co, ln: xc_in[r0:r0 + nr, co:co + ln]
        io["z_ap"] = lambda r0, nr, co, ln: z_in[r0:r0 + nr, co:co + ln]
        io["xd_ap"] = lambda r0, nr, co, ln: xd_in[r0:r0 + nr, co:co + ln]
        io["yg_ap"] = lambda r0, nr, co, ln: yg_[r0:r0 + nr, co:co + ln]
    wdt_j, db_j, alog_j, dsk_j = io["wdt_j"], io["db_j"], io["alog_j"], io["dsk_j"]
    xc_ap, z_ap, xd_ap, yg_ap = io["xc_ap"], io["z_ap"], io["xd_ap"], io["yg_ap"]

    ND = CPC // 128
    NCT = SCAN_CTS
    RESET_CT = L_ // LC
    PF = "s_"
    es = contextlib.ExitStack()
    with es:
        sb = lambda name, shape, dt=F32: es.enter_context(nc.sbuf_tensor(PF + name, shape, dt))[:]
        pst = lambda name, shape, dt=F32: es.enter_context(nc.psum_tensor(PF + name, shape, dt))[:]
        wdt16 = sb("wdt16", [128, CPC], BF16)
        atmp = sb("atmp", [128, ND * DS])
        acol = sb("acol", [128, ND * DS])
        db = sb("db", [128, ND])
        dsk = sb("dsk", [128, ND])
        dtr16 = sb("dtr16", [128, 2 * LC], BF16)
        Bs = sb("Bs", [128, 3 * LC], BF16)
        Cs = sb("Cs", [128, 3 * LC], BF16)
        xcd = sb("xcd", [128, 2 * ND * LC], BF16)
        zd = sb("zd", [128, 2 * ND * LC], BF16)
        dt32 = sb("dt32", [128, ND * LC])
        dtx = sb("dtx", [128, ND * LC], BF16)
        Pt = sb("Pt", [128, ND * LC])
        Qt = sb("Qt", [128, ND * LC])
        Rt = sb("Rt", [128, ND * LC])
        dA = [sb(f"dA{i}", [128, LC]) for i in range(3)]
        u_all = [sb(f"u_all{i}", [128, ND * LC], BF16) for i in range(2)]
        h8 = sb("h8", [128, 8 * LC], BF16)
        prod_all = sb("prod_all", [128, ND * LC], BF16)
        yacc = [sb(f"yacc{i}", [128, ND * LC], BF16) for i in range(2)]
        t1g = sb("t1g", [128, 2 * LC], BF16)
        szg = sb("szg", [128, 2 * LC], BF16)
        ygd = sb("ygd", [128, ND * LC], BF16)
        state_all = sb("state_all", [128, DS * ND])
        psdt = [pst(f"psdt{i}", [128, 1024]) for i in range(2)]

        p = p if p is not None else Prog(nc, es)
        p.dma("prep", "gpsimd", wdt16, wdt_j)
        tok_prep = p.close("prep")
        for d in range(ND):
            p.dma("res", "sync", atmp[:, d * DS:(d + 1) * DS],
                  alog_j[d * 128:(d + 1) * 128, :])
        p.dma("res", "sync", db, db_j)
        p.dma("res", "sync", dsk, dsk_j)
        tok_res = p.close("res")
        p.waits("scalar", tok_res)
        tok_e = p.run("scalar", lambda sc: sc.activation(acol, atmp, AF.Exp), inc=True)
        p.waits("vector", tok_e)
        tok_a = p.run("vector", lambda v: v.tensor_scalar(atmp, acol, -1.0, None, AL.mult), inc=True)
        for e in ENGS:
            p.waits(e, tok_prep, tok_res, tok_a)
        p.barrier()

        prev_cons = {}
        slotS = [None, None, None]
        hcons = [None, None]
        bcons = [None, None, None]
        ucons = [None, None]
        ycons = [None, None]
        yread = [None, None]
        oy_tok = [None] * ND
        for ct in range(NCT):
            co = ct * LC
            pc = ct % 2
            dsl = lambda d: slice(d * LC, (d + 1) * LC)
            xsl = lambda d: slice((pc * ND + d) * LC, (pc * ND + d + 1) * LC)
            for tk in (prev_cons.get(("ld", pc)) or []):
                p.waits("sync", tk)
            p.dma(f"dtr{pc}", "sync", dtr16[:, pc * LC:(pc + 1) * LC], xd_ap(0, DTR, co, LC))
            tok_dtr = p.close(f"dtr{pc}")
            for d in range(ND):
                p.dma(f"xcz{pc}", "sync", xcd[:, xsl(d)], xc_ap(d * 128, 128, co, LC))
                p.dma(f"xcz{pc}", "sync", zd[:, xsl(d)], z_ap(d * 128, 128, co, LC))
            tok_xcz = p.close(f"xcz{pc}")

            p.waits("tensor", tok_dtr, prev_cons.get("dtpre"))
            mmtok = [None] * ND
            for d in range(ND):
                bank = d % 2
                p.waits("tensor", prev_cons.get(("psdt", bank)))
                for h in range(LC // 512):
                    t_ = p.run("tensor", lambda tt, o=psdt[bank][:, h * 512:(h + 1) * 512],
                               w=wdt16[:, d * 128:(d + 1) * 128],
                               r=dtr16[:, pc * LC + h * 512: pc * LC + (h + 1) * 512],
                               st=True, e_=True: tt.matmul(o, w, r, start=st, stop=e_),
                               inc=(h == LC // 512 - 1))
                mmtok[d] = t_
                p.waits("scalar", t_)
                s1 = p.run("scalar", lambda sc, o=Pt[:, dsl(d)], i=psdt[bank][:, 0:LC],
                           b=db[:, d:d + 1]: sc.add(o, i, b), inc=True)
                prev_cons[("psdt", bank)] = s1
            for d in range(ND):
                p.run("scalar", lambda sc, o=Qt[:, dsl(d)], i=Pt[:, dsl(d)]:
                      sc.activation(o, i, AF.Abs))
            t2_ = p.mark("scalar")
            p.waits("vector", t2_)
            for d in range(ND):
                p.run("vector", lambda v, o=Rt[:, dsl(d)], i=Qt[:, dsl(d)]:
                      v.tensor_scalar(o, i, -1.0, None, AL.mult))
            t3_ = p.mark("vector")
            p.waits("scalar", t3_)
            for d in range(ND):
                p.run("scalar", lambda sc, o=Qt[:, dsl(d)], i=Rt[:, dsl(d)]:
                      sc.activation(o, i, AF.Exp))
            t4_ = p.mark("scalar")
            p.waits("vector", t4_)
            for d in range(ND):
                p.run("vector", lambda v, o=Rt[:, dsl(d)], i=Qt[:, dsl(d)]:
                      v.tensor_scalar(o, i, 1.0, None, AL.add))
            t5_ = p.mark("vector")
            p.waits("scalar", t5_)
            for d in range(ND):
                p.run("scalar", lambda sc, o=Qt[:, dsl(d)], i=Rt[:, dsl(d)]:
                      sc.activation(o, i, AF.Ln))
            t6_ = p.mark("scalar")
            p.waits("vector", t6_)
            for d in range(ND):
                p.run("vector", lambda v, o=Rt[:, dsl(d)], i=Pt[:, dsl(d)]:
                      v.tensor_scalar(o, i, 0.0, None, AL.max))
            for d in range(ND):
                p.run("vector", lambda v, o=dt32[:, dsl(d)], a=Qt[:, dsl(d)], b=Rt[:, dsl(d)]:
                      v.tensor_tensor(o, a, b, AL.add))
            p.waits("vector", tok_xcz)
            for d in range(ND):
                p.run("vector", lambda v, o=dtx[:, dsl(d)], a=dt32[:, dsl(d)], b=xcd[:, xsl(d)]:
                      v.tensor_tensor(o, a, b, AL.mult))
            tok_dt = p.mark("vector")
            prev_cons["dtpre"] = tok_dt
            p.waits("scalar", tok_dt)

            for s_ in range(DS):
                bs = s_ % 3
                usl = s_ % 2
                ysl = s_ % 2
                hbase = (s_ % 2) * ND
                p.waits("sync", bcons[bs])
                p.dma(f"bc{bs}", "sync", Bs[:, bs * LC:(bs + 1) * LC],
                      xd_ap(DTR + s_, 1, co, LC).partition_broadcast(128))
                p.dma(f"bc{bs}", "sync", Cs[:, bs * LC:(bs + 1) * LC],
                      xd_ap(DTR + DS + s_, 1, co, LC).partition_broadcast(128))
                tok_bc = p.close(f"bc{bs}")
                p.waits("vector", ucons[usl], tok_bc)
                tu = p.run("vector", lambda v, o=u_all[usl], a=dtx,
                           b=Bs[:, bs * LC:(bs + 1) * LC].unsqueeze(1).broadcast_to([128, ND, LC]):
                           v.tensor_tensor(o.rearrange("p (n l) -> p n l", l=LC), a.rearrange("p (n l) -> p n l", l=LC), b, AL.mult), inc=True)
                vlast = None
                for d in range(ND):
                    step = s_ * ND + d
                    sl = step % 3
                    hsl = hbase + d
                    p.waits("scalar", slotS[sl])
                    tda = p.run("scalar", lambda sc, o=dA[sl], i=dt32[:, dsl(d)],
                                a=atmp[:, d * DS + s_: d * DS + s_ + 1]:
                                sc.activation(o, i, AF.Exp, scale=a), inc=True)
                    if ct == 0 or ct == RESET_CT:
                        init = 0.0
                    else:
                        init = state_all[:, s_ * ND + d: s_ * ND + d + 1]
                    p.waits("vector", tda, tu, hcons[s_ % 2])
                    vlast = p.run("vector", lambda v, o=h8[:, hsl * LC:(hsl + 1) * LC],
                                  a=dA[sl], b=u_all[usl][:, d * LC:(d + 1) * LC], i=init:
                                  v.tensor_tensor_scan(o, a, b, i, AL.mult, AL.add), inc=True)
                    slotS[sl] = vlast
                tpm = p.run("vector", lambda v, o=prod_all,
                            a=h8[:, hbase * LC: (hbase + ND) * LC],
                            c=Cs[:, bs * LC:(bs + 1) * LC].unsqueeze(1).broadcast_to([128, ND, LC]):
                            v.tensor_tensor(o.rearrange("p (n l) -> p n l", l=LC),
                                            a.rearrange("p (n l) -> p n l", l=LC), c, AL.mult), inc=True)
                ucons[usl] = tpm
                bcons[bs] = tpm
                p.waits("vector", yread[ysl], ycons[ysl])
                if s_ == 0:
                    ty = p.run("vector", lambda v, o=yacc[ysl], i=prod_all:
                               v.tensor_copy(o, i), inc=True)
                else:
                    ty = p.run("vector", lambda v, o=yacc[ysl], a=yacc[1 - ysl], b=prod_all:
                               v.tensor_tensor(o, a, b, AL.add), inc=True)
                ycons[ysl] = ty
                yread[ysl] = ty
                p.waits("scalar", vlast)
                tcap = p.run("scalar", lambda sc, o=state_all[:, s_ * ND: s_ * ND + ND],
                             i=h8.rearrange("p (n l) -> p n l", l=LC)[:, hbase: hbase + ND, LC - 1]:
                             sc.copy(o, i), inc=True)
                hcons[s_ % 2] = tcap

            yfin = yacc[(DS - 1) % 2]
            for d in range(ND):
                zsl = d % 2
                p.waits("scalar", prev_cons.get(("sz", zsl)))
                tsz = p.run("scalar", lambda sc, o=szg[:, zsl * LC:(zsl + 1) * LC],
                            i=zd[:, xsl(d)]: sc.activation(o, i, AF.Silu), inc=True)
                p.waits("vector", tsz, oy_tok[d], ycons[(DS - 1) % 2])
                p.run("vector", lambda v, o=t1g[:, zsl * LC:(zsl + 1) * LC], i=xcd[:, xsl(d)],
                      a=dsk[:, d:d + 1], b=yfin[:, dsl(d)]:
                      v.scalar_tensor_tensor(o, i, a, b, AL.mult, AL.add))
                tyg = p.run("vector", lambda v, o=ygd[:, dsl(d)], a=t1g[:, zsl * LC:(zsl + 1) * LC],
                            b=szg[:, zsl * LC:(zsl + 1) * LC]:
                            v.tensor_tensor(o, a, b, AL.mult), inc=True)
                prev_cons[("sz", zsl)] = tyg
                yread[(DS - 1) % 2] = tyg
                p.waits("sync", tyg)
                p.dma(f"oy{d}", "sync", yg_ap(d * 128, 128, co, LC), ygd[:, dsl(d)])
                oy_tok[d] = p.close(f"oy{d}")
            prev_cons[("ld", pc)] = [p.mark("vector"), p.mark("gpsimd"), p.mark("tensor")]

        if finalize:
            for name in list(p.ch):
                if p.ch[name]["open"]:
                    p.close(name)
            p.drain_all_channels("sync")
            block = es.enter_context(nc.Block())
            p.finalize(block)
    return nc


def build_back(nc=None, p=None, io=None, finalize=True):
    if nc is None:
        nc = bass.Bass("TRN2", target_bir_lowering=False, debug=False)
    dram = nc.dram_tensor
    if io is None:
        io = {}
        yg_in = dram("yg_in", [DI, TPC], BF16, kind="ExternalInput").ap()
        io["ygb_ap"] = lambda r0, nr, xoff, n: yg_in[r0:r0 + nr, xoff:xoff + n]
        for nm, shp in (("x1_in", [D, TPC]), ("u_in", [D, TPC])):
            io[nm] = dram(nm, shp, F32, kind="ExternalInput").ap()
        for nm, shp in (("w_out_T", [DI, D]), ("w_ffp_T", [D, H]), ("w_ffo_T", [DI, D]),
                        ("a_T", [D, E * R]), ("b_T", [E * R, D]), ("wr_T", [D, E]),
                        ("br_r", [1, E]), ("ln2_w_t", [128, NKD]), ("ln2_b_t", [128, NKD]),
                        ("repm_in", [8, 512])):
            io[nm] = dram(nm, shp, F32, kind="ExternalInput").ap()
        io["xo_o"] = dram("xo_o", [D, TPC], F32, kind="ExternalOutput").ap()
    (ygb_ap, x1_in, u_in, w_out_T, w_ffp_T, w_ffo_T, a_T, b_T, wr_T, br_r,
     ln2_w_t, ln2_b_t, repm_in, xo_o) = (
        io["ygb_ap"], io["x1_in"], io["u_in"], io["w_out_T"], io["w_ffp_T"],
        io["w_ffo_T"], io["a_T"], io["b_T"], io["wr_T"], io["br_r"],
        io["ln2_w_t"], io["ln2_b_t"], io["repm_in"], io["xo_o"])

    if "w_out_b" in io:
        (w_out_b, w_ffp_b, w_ffo_b, a_b, b_b, wr_b, br_b) = (
            io["w_out_b"], io["w_ffp_b"], io["w_ffo_b"], io["a_b"], io["b_b"],
            io["wr_b"], io["br_b"])
        prep_here = False
    else:
        w_out_b = dram("k_w_out_b", [DI, D], BF16)
        w_ffp_b = dram("k_w_ffp_b", [D, H], BF16)
        w_ffo_b = dram("k_w_ffo_b", [DI, D], BF16)
        a_b = dram("k_a_b", [D, E * R], BF16)
        b_b = dram("k_b_b", [E * R, D], BF16)
        wr_b = dram("k_wr_b", [D, E], BF16)
        br_b = dram("k_br_b", [1, E], BF16)
        prep_here = True
    v_s = dram("k_v_s", [D, TPC], F32).ap()
    t2_s = dram("k_t2_s", [D, TPC], BF16).ap()
    ff_s = dram("k_ff_s", [DI, TPC], BF16).ap()
    x2_s = dram("k_x2_s", [D, TPC], F32).ap()

    PF = "k_"
    es = contextlib.ExitStack()
    with es:
        sb = lambda name, shape, dt=F32: es.enter_context(nc.sbuf_tensor(PF + name, shape, dt))[:]
        pst = lambda name, shape, dt=F32: es.enter_context(nc.psum_tensor(PF + name, shape, dt))[:]
        if p is None:
            p = Prog(nc, es)
        if prep_here:
            p.dma("prep", "gpsimd", w_out_b.ap(), w_out_T)
            p.dma("prep", "gpsimd", w_ffp_b.ap(), w_ffp_T)
            p.dma("prep", "gpsimd", w_ffo_b.ap(), w_ffo_T)
            p.dma("prep", "gpsimd", a_b.ap(), a_T)
            p.dma("prep", "gpsimd", b_b.ap(), b_T)
            p.dma("prep", "gpsimd", wr_b.ap(), wr_T)
            p.dma("prep", "gpsimd", br_b.ap(), br_r)
            tok_prep = p.close("prep")
        else:
            tok_prep = io["prepk_tok"]

        PT1 = 1024
        with contextlib.ExitStack() as e1:
            sb1 = lambda name, shape, dt=F32: e1.enter_context(nc.sbuf_tensor(PF + "p1" + name, shape, dt))[:]
            pst1 = lambda name, shape, dt=F32: e1.enter_context(nc.psum_tensor(PF + name, shape, dt))[:]
            ygt = sb1("ygt", [128, NKI * PT1], BF16)
            ut = sb1("ut", [128, 2 * PT1])
            vt = sb1("vt", [128, 2 * PT1])
            vb = sb1("vb", [128, 2 * PT1], BF16)
            vrb = sb1("vrb", [128, 2 * 512])
            sqb = sb1("sqb", [128, 2 * PT1], BF16)
            t2t = sb1("t2t", [128, 2 * PT1], BF16)
            tmpA = sb1("tmpA", [128, PT1])
            tmpB = sb1("tmpB", [128, PT1])
            wbuf = [sb1(f"wbuf{b}", [128, NKI * 512], BF16) for b in range(2)]
            stats = {nm: sb1(nm, [1, PT1]) for nm in
                     ("m", "q", "var", "sd", "r", "nmr", "t1", "t2x")}
            ones_c = e1.enter_context(nc.sbuf_tensor(PF + "p1ones_c", [128, 1], BF16))
            ones_r = e1.enter_context(nc.sbuf_tensor(PF + "p1ones_r", [1, 128], F32))
            ln2w = sb1("ln2w", [128, NKD])
            ln2b = sb1("ln2b", [128, NKD])
            psMM = [pst1("p1psMM", [128, 512]), pst1("p1psMM2", [128, 512])]
            psA = pst1("p1psA", [1, 1024])
            psB = pst1("p1psB", [1, 1024])
            psR = pst1("p1psR", [128, 512])
            psN = pst1("p1psN", [128, 512])
            tok_c = p.run("vector", lambda v: v.memset(ones_c[:], 1.0), inc=True)
            tok_c2 = p.run("vector", lambda v: v.memset(ones_r[:], 1.0), inc=True)
            p.dma("res", "sync", ln2w, ln2_w_t)
            p.dma("res", "sync", ln2b, ln2_b_t)
            tok_res = p.close("res")
            for e in ENGS:
                p.waits(e, tok_prep, tok_res, tok_c, tok_c2)
            wfree = [None, None]
            evac = [None, None]
            ldone = {}
            for t in range(BACK_SPAN // PT1):
                xoff = t * PT1
                p.waits("sync", ldone.get("yg_cons"))
                for k in range(NKI):
                    p.dma("yg", "sync", ygt[:, k * PT1:(k + 1) * PT1],
                          ygb_ap(k * 128, 128, xoff, PT1))
                tok_yg = p.close("yg")
                p.waits("tensor", tok_yg)
                for m in range(NKD):
                    par = m % 2
                    g, mc = m // 4, m % 4
                    if mc == 0:
                        p.waits("sync", wfree[par_g := (g % 2)])
                        for k in range(NKI):
                            p.dma(f"w{par_g}", "sync", wbuf[par_g][:, k * 512:(k + 1) * 512],
                                  w_out_b.ap()[k * 128:(k + 1) * 128, g * 512:(g + 1) * 512])
                        tok_w = p.close(f"w{par_g}")
                        p.waits("tensor", tok_w)
                    usl = m % 2
                    for _tk in (ldone.get(("u", usl)) or []):
                        p.waits("sync", _tk)
                    p.dma("u", "sync", ut[:, usl * PT1:(usl + 1) * PT1],
                          u_in[m * 128:(m + 1) * 128, xoff: xoff + PT1])
                    tok_u = p.close("u")
                    vsl = m % 2
                    for h in range(PT1 // 512):
                        bank = (m * (PT1 // 512) + h) % 2
                        ps = psMM[bank]
                        p.waits("tensor", evac[bank])
                        for k in range(NKI):
                            p.run("tensor", lambda tt, o=ps[:, 0:512],
                                  w=wbuf[g % 2][:, k * 512 + (m % 4) * 128: k * 512 + (m % 4) * 128 + 128],
                                  r=ygt[:, k * PT1 + h * 512: k * PT1 + (h + 1) * 512],
                                  st=(k == 0), e_=(k == NKI - 1):
                                  tt.matmul(o, w, r, start=st, stop=e_))
                        tmm = p.mark("tensor")
                        p.waits("vector", tmm, tok_u)
                        tv = p.run("vector", lambda v, o=vt[:, vsl * PT1 + h * 512: vsl * PT1 + (h + 1) * 512],
                                   a=ut[:, usl * PT1 + h * 512: usl * PT1 + (h + 1) * 512],
                                   b=ps[:, 0:512]: v.tensor_tensor(o, a, b, AL.add), inc=True)
                        evac[bank] = tv
                    p.waits("sync", tv)
                    p.dma("v", "sync", v_s[m * 128:(m + 1) * 128, xoff: xoff + PT1],
                          vt[:, vsl * PT1:(vsl + 1) * PT1])
                    vstok = p.close("v")
                    ldone[("u", usl)] = [tv, vstok]
                    p.waits("scalar", tv)
                    tvb = p.run("scalar", lambda sc, o=vb[:, vsl * PT1:(vsl + 1) * PT1],
                                i=vt[:, vsl * PT1:(vsl + 1) * PT1]: sc.copy(o, i), inc=True)
                    tsq = p.run("scalar", lambda sc, o=sqb[:, vsl * PT1:(vsl + 1) * PT1],
                                i=vt[:, vsl * PT1:(vsl + 1) * PT1]: sc.activation(o, i, AF.Square), inc=True)
                    p.waits("tensor", tvb, tsq)
                    for h2 in range(PT1 // 512):
                        p.run("tensor", lambda tt, o=psA[0:1, h2 * 512:(h2 + 1) * 512], w=ones_c[:],
                              r=vb[:, vsl * PT1 + h2 * 512: vsl * PT1 + (h2 + 1) * 512],
                              st=(m == 0), e_=(m == NKD - 1):
                              tt.matmul(o, w, r, start=st, stop=e_, skip_group_check=True))
                        p.run("tensor", lambda tt, o=psB[0:1, h2 * 512:(h2 + 1) * 512], w=ones_c[:],
                              r=sqb[:, vsl * PT1 + h2 * 512: vsl * PT1 + (h2 + 1) * 512],
                              st=(m == 0), e_=(m == NKD - 1):
                              tt.matmul(o, w, r, start=st, stop=e_, skip_group_check=True))
                    ldone[("v", vsl)] = p.mark("tensor")
                    if mc == 3:
                        wfree[g % 2] = ldone[("v", vsl)]
                    if m == NKD - 1:
                        ldone["yg_cons"] = ldone[("v", vsl)]
                tok_mm = p.mark("tensor")
                st_ = stats
                p.waits("vector", tok_mm)
                p.run("vector", lambda v: v.tensor_scalar(st_["m"][:, 0:PT1], psA[0:1, 0:PT1], 1.0 / D, None, AL.mult))
                p.run("vector", lambda v: v.tensor_scalar(st_["q"][:, 0:PT1], psB[0:1, 0:PT1], 1.0 / D, None, AL.mult))
                p.run("vector", lambda v: v.tensor_tensor(st_["t1"][:, 0:PT1], st_["m"][:, 0:PT1], st_["m"][:, 0:PT1], AL.mult))
                p.run("vector", lambda v: v.tensor_tensor(st_["var"][:, 0:PT1], st_["q"][:, 0:PT1], st_["t1"][:, 0:PT1], AL.subtract))
                tv1 = p.run("vector", lambda v: v.tensor_scalar(st_["t2x"][:, 0:PT1], st_["var"][:, 0:PT1], EPS, None, AL.add), inc=True)
                p.waits("scalar", tv1)
                ts1 = p.run("scalar", lambda sc: sc.activation(st_["sd"][:, 0:PT1], st_["t2x"][:, 0:PT1], AF.Sqrt), inc=True)
                p.waits("vector", ts1)
                p.run("vector", lambda v: v.reciprocal(st_["r"][:, 0:PT1], st_["sd"][:, 0:PT1]))
                p.run("vector", lambda v: v.tensor_tensor(st_["t1"][:, 0:PT1], st_["m"][:, 0:PT1], st_["r"][:, 0:PT1], AL.mult))
                tv2 = p.run("vector", lambda v: v.tensor_scalar(st_["nmr"][:, 0:PT1], st_["t1"][:, 0:PT1], -1.0, None, AL.mult), inc=True)
                p.waits("tensor", tv2)
                p.waits("sync", vstok)
                for h in range(PT1 // 512):
                    p.run("tensor", lambda tt, o=psR[:, 0:512], r=st_["r"][0:1, h * 512:(h + 1) * 512]:
                          tt.matmul(o, ones_r[:], r, start=True, stop=True))
                    p.run("tensor", lambda tt, o=psN[:, 0:512], r=st_["nmr"][0:1, h * 512:(h + 1) * 512]:
                          tt.matmul(o, ones_r[:], r, start=True, stop=True))
                    trp = p.mark("tensor")
                    for m in range(NKD):
                        vsl2 = m % 2
                        p.waits("sync", ldone.get(("vr", vsl2)))
                        p.dma("vr", "sync", vrb[:, vsl2 * 512:(vsl2 + 1) * 512],
                              v_s[m * 128:(m + 1) * 128, xoff + h * 512: xoff + (h + 1) * 512])
                        tok_vr = p.close("vr")
                        p.waits("vector", trp, tok_vr, ldone.get(("t2s", vsl2)))
                        p.run("vector", lambda v, o=tmpA[:, 0:512], a=vrb[:, vsl2 * 512:(vsl2 + 1) * 512],
                              b=psR[:, 0:512]: v.tensor_tensor(o, a, b, AL.mult))
                        p.run("vector", lambda v, o=tmpB[:, 0:512], a=tmpA[:, 0:512],
                              b=psN[:, 0:512]: v.tensor_tensor(o, a, b, AL.add))
                        tt2 = p.run("vector", lambda v, o=t2t[:, vsl2 * 512:(vsl2 + 1) * 512], i=tmpB[:, 0:512],
                                    w=ln2w[:, m:m + 1], b=ln2b[:, m:m + 1]:
                                    v.tensor_scalar(o, i, w, b, AL.mult, AL.add), inc=True)
                        ldone[("vr", vsl2)] = tt2
                        p.waits("sync", tt2)
                        p.dma(f"t2{vsl2}", "sync",
                              t2_s[m * 128:(m + 1) * 128, xoff + h * 512: xoff + (h + 1) * 512],
                              t2t[:, vsl2 * 512:(vsl2 + 1) * 512])
                        ldone[("t2s", vsl2)] = p.close(f"t2{vsl2}")
                    p.waits("tensor", p.mark("vector"))
        for name in list(p.ch):
            if p.ch[name]["open"]:
                p.close(name)
        p.drain_all_channels("sync")
        p.barrier()

        PT2 = 2048
        with contextlib.ExitStack() as e2:
            sb2 = lambda name, shape, dt=F32: e2.enter_context(nc.sbuf_tensor(PF + "p2" + name, shape, dt))[:]
            pst2 = lambda name, shape, dt=F32: e2.enter_context(nc.psum_tensor(PF + name, shape, dt))[:]
            t2r = sb2("t2r", [128, NKD * PT2], BF16)
            a16 = sb2("a16", [128, 2 * 512], BF16)
            sg = sb2("sg", [128, 2 * 512], BF16)
            ff16 = sb2("ff16", [128, 2 * 512], BF16)
            wbufa = [sb2(f"wbufa{b}", [128, NKD * 512], BF16) for b in range(2)]
            wbufg = [sb2(f"wbufg{b}", [128, NKD * 512], BF16) for b in range(2)]
            psMM = [pst2("p2psMM", [128, 512]), pst2("p2psMM2", [128, 512])]
            evac = [None, None]
            wfree = [None, None, None, None]
            ldone = {}
            for t in range(BACK_SPAN // PT2):
                xoff = t * PT2
                for k in range(NKD):
                    p.waits("sync", ldone.get(("t2", k)))
                    p.dma("t2r", "sync", t2r[:, k * PT2:(k + 1) * PT2],
                          t2_s[k * 128:(k + 1) * 128, xoff: xoff + PT2])
                tok_t2 = p.close("t2r")
                for m in range(NKI):
                    gw, mw = m // 4, m % 4
                    par = gw % 2
                    if mw == 0:
                        p.waits("sync", wfree[par])
                        for k in range(NKD):
                            p.dma(f"wa{par}", "sync", wbufa[par][:, k * 512:(k + 1) * 512],
                                  w_ffp_b.ap()[k * 128:(k + 1) * 128, gw * 512:(gw + 1) * 512])
                        for k in range(NKD):
                            p.dma(f"wa{par}", "sync", wbufg[par][:, k * 512:(k + 1) * 512],
                                  w_ffp_b.ap()[k * 128:(k + 1) * 128, DI + gw * 512: DI + (gw + 1) * 512])
                        tok_w = p.close(f"wa{par}")
                    p.waits("tensor", tok_w, tok_t2)
                    for h in range(PT2 // 512):
                        bank = h % 2
                        psa = psMM[bank]
                        psg = psMM[1 - bank]
                        p.waits("tensor", evac[0], evac[1])
                        for k in range(NKD):
                            p.run("tensor", lambda tt, o=psa[:, 0:512],
                                  w=wbufa[par][:, k * 512 + mw * 128: k * 512 + mw * 128 + 128],
                                  r=t2r[:, k * PT2 + h * 512: k * PT2 + (h + 1) * 512],
                                  st=(k == 0), e_=(k == NKD - 1): tt.matmul(o, w, r, start=st, stop=e_))
                        tma = p.mark("tensor")
                        for k in range(NKD):
                            p.run("tensor", lambda tt, o=psg[:, 0:512],
                                  w=wbufg[par][:, k * 512 + mw * 128: k * 512 + mw * 128 + 128],
                                  r=t2r[:, k * PT2 + h * 512: k * PT2 + (h + 1) * 512],
                                  st=(k == 0), e_=(k == NKD - 1): tt.matmul(o, w, r, start=st, stop=e_))
                        tmg = p.mark("tensor")
                        hsl = h % 2
                        p.waits("scalar", tma, tmg, ldone.get(("ff", hsl)))
                        ta = p.run("scalar", lambda sc, o=a16[:, hsl * 512:(hsl + 1) * 512], i=psa:
                                   sc.copy(o, i[:, 0:512]), inc=True)
                        tg = p.run("scalar", lambda sc, o=sg[:, hsl * 512:(hsl + 1) * 512], i=psg:
                                   sc.activation(o, i[:, 0:512], AF.Silu), inc=True)
                        evac[0] = ta
                        evac[1] = tg
                        p.waits("vector", tg)
                        tf = p.run("vector", lambda v, o=ff16[:, hsl * 512:(hsl + 1) * 512],
                                   a=a16[:, hsl * 512:(hsl + 1) * 512], b=sg[:, hsl * 512:(hsl + 1) * 512]:
                                   v.tensor_tensor(o, a, b, AL.mult), inc=True)
                        p.waits("sync", tf)
                        p.dma(f"ff{hsl}", "sync", ff_s[m * 128:(m + 1) * 128, xoff + h * 512: xoff + (h + 1) * 512],
                              ff16[:, hsl * 512:(hsl + 1) * 512])
                        ldone[("ff", hsl)] = p.close(f"ff{hsl}")
                    if mw == 3:
                        wfree[par] = p.mark("tensor")
        for name in list(p.ch):
            if p.ch[name]["open"]:
                p.close(name)
        p.drain_all_channels("sync")
        p.barrier()

        PT3 = 1024
        with contextlib.ExitStack() as e3:
            sb3 = lambda name, shape, dt=F32: e3.enter_context(nc.sbuf_tensor(PF + "p3" + name, shape, dt))[:]
            pst3 = lambda name, shape, dt=F32: e3.enter_context(nc.psum_tensor(PF + name, shape, dt))[:]
            ffr = sb3("ffr", [128, NKI * PT3], BF16)
            x1t = sb3("x1t", [128, 2 * PT3])
            vtt = sb3("vtt", [128, 2 * PT3])
            e32 = sb3("e32", [128, 2 * PT3])
            x2t = sb3("x2t", [128, 2 * PT3])
            wbuf = [sb3(f"wbuf{b}", [128, NKI * 512], BF16) for b in range(2)]
            psMM = [pst3("p3psMM", [128, 512]), pst3("p3psMM2", [128, 512])]
            evac = [None, None]
            wfree = [None, None]
            ldone = {}
            for t in range(BACK_SPAN // PT3):
                xoff = t * PT3
                for k in range(NKI):
                    p.waits("sync", ldone.get(("ff", k)))
                    p.dma("ffr", "sync", ffr[:, k * PT3:(k + 1) * PT3],
                          ff_s[k * 128:(k + 1) * 128, xoff: xoff + PT3])
                tok_ff = p.close("ffr")
                for m in range(NKD):
                    gw, mw = m // 4, m % 4
                    par = gw % 2
                    if mw == 0:
                        p.waits("sync", wfree[par])
                        for k in range(NKI):
                            p.dma(f"w{par}", "sync", wbuf[par][:, k * 512:(k + 1) * 512],
                                  w_ffo_b.ap()[k * 128:(k + 1) * 128, gw * 512:(gw + 1) * 512])
                        tok_w = p.close(f"w{par}")
                    msl = m % 2
                    p.waits("sync", ldone.get(("x2", msl)))
                    p.dma("x1l", "sync", x1t[:, msl * PT3:(msl + 1) * PT3],
                          x1_in[m * 128:(m + 1) * 128, xoff: xoff + PT3])
                    p.dma("x1l", "sync", vtt[:, msl * PT3:(msl + 1) * PT3],
                          v_s[m * 128:(m + 1) * 128, xoff: xoff + PT3])
                    tok_x1 = p.close("x1l")
                    p.waits("tensor", tok_w, tok_ff)
                    tx2 = None
                    for h in range(PT3 // 512):
                        bank = h % 2
                        ps = psMM[bank]
                        p.waits("tensor", evac[bank])
                        for k in range(NKI):
                            p.run("tensor", lambda tt, o=ps[:, 0:512],
                                  w=wbuf[par][:, k * 512 + mw * 128: k * 512 + mw * 128 + 128],
                                  r=ffr[:, k * PT3 + h * 512: k * PT3 + (h + 1) * 512],
                                  st=(k == 0), e_=(k == NKI - 1): tt.matmul(o, w, r, start=st, stop=e_))
                        tmm = p.mark("tensor")
                        p.waits("vector", tmm, tok_x1)
                        p.run("vector", lambda v, o=e32[:, msl * PT3 + h * 512: msl * PT3 + (h + 1) * 512],
                              a=vtt[:, msl * PT3 + h * 512: msl * PT3 + (h + 1) * 512],
                              b=ps[:, 0:512]: v.tensor_tensor(o, a, b[:, 0:512], AL.add))
                        tx2 = p.run("vector", lambda v, o=x2t[:, msl * PT3 + h * 512: msl * PT3 + (h + 1) * 512],
                                    a=e32[:, msl * PT3 + h * 512: msl * PT3 + (h + 1) * 512],
                                    b=x1t[:, msl * PT3 + h * 512: msl * PT3 + (h + 1) * 512]:
                                    v.tensor_tensor(o, a, b, AL.add), inc=True)
                        evac[bank] = tx2
                    p.waits("sync", tx2)
                    p.dma(f"x2{msl}", "sync", x2_s[m * 128:(m + 1) * 128, xoff: xoff + PT3],
                          x2t[:, msl * PT3:(msl + 1) * PT3])
                    ldone[("x2", msl)] = p.close(f"x2{msl}")
                    if mw == 3:
                        wfree[par] = p.mark("tensor")
        for name in list(p.ch):
            if p.ch[name]["open"]:
                p.close(name)
        p.drain_all_channels("sync")
        p.barrier()

        N = 512
        with contextlib.ExitStack() as e4:
            sb4 = lambda name, shape, dt=F32: e4.enter_context(nc.sbuf_tensor(PF + "p4" + name, shape, dt))[:]
            pst4 = lambda name, shape, dt=F32: e4.enter_context(nc.psum_tensor(PF + name, shape, dt))[:]
            al = {}
            al["xbt"] = sb4("xbt", [128, 2 * N], BF16)
            al["sqt"] = sb4("sqt", [128, 2 * N], BF16)
            al["ln_n"] = N
            al["tmp32"] = sb4("tmp32", [128, N])
            al["tmp32b"] = sb4("tmp32b", [128, N])
            for nm in ("st_m", "st_q", "st_var", "st_sd", "st_r", "st_nmr", "st_t1", "st_t2"):
                al[nm] = sb4(nm, [1, N])
            al["ones_col16"] = e4.enter_context(nc.sbuf_tensor(PF + "p4ones_c", [128, 1], BF16))
            al["ones_row32"] = e4.enter_context(nc.sbuf_tensor(PF + "p4ones_r", [1, 128], F32))
            al["ones_row16"] = sb4("ones_r16", [1, 128], BF16)
            al["ident"] = e4.enter_context(nc.sbuf_tensor(PF + "p4ident", [128, 128], BF16))
            zeros128 = e4.enter_context(nc.sbuf_tensor(PF + "p4zeros", [128, 128], BF16))
            al["lgs"] = sb4("lgs", [128, 32])
            al["mx"] = sb4("mx", [128, 4])
            al["nmx"] = sb4("nmx", [128, 4])
            al["sm"] = sb4("sm", [128, 4])
            al["wsc"] = sb4("wsc", [128, 4])
            al["w0"] = sb4("w0", [128, 4])
            al["exps"] = sb4("exps", [128, 32])
            al["maskb"] = sb4("maskb", [128, 32], BF16)
            al["maskT"] = sb4("maskT", [8, N], BF16)
            al["repm"] = sb4("repm", [8, 512], BF16)
            al["mrep16"] = sb4("mrep16", [128, N], BF16)
            al["h16"] = sb4("h16", [128, N], BF16)
            al["hm16"] = sb4("hm16", [128, 4 * N], BF16)
            al["ffn_n"] = N
            al["aT"] = sb4("aT", [128, NKD * 4 * 128], BF16)
            al["bT"] = sb4("bT", [128, 4 * NKD * 128], BF16)
            al["wrT"] = sb4("wrT", [128, NKD * 8], BF16)
            al["brT"] = e4.enter_context(nc.sbuf_tensor(PF + "p4brT", [1, 8], BF16))
            al["psA"] = pst4("p4psA", [1, 512])
            al["psB"] = pst4("p4psB", [1, 512])
            al["psR"] = pst4("p4psR", [128, 512])
            al["psN"] = pst4("p4psN", [128, 512])
            al["psS"] = pst4("p4psS", [128, 512])
            al["psRt"] = al["psS"][:, 0:32]
            al["psM"] = al["psR"]
            al["psH"] = al["psN"]
            psX4 = pst4("p4psX", [128, 512])
            al["psT"] = psX4[0:8, :].bitcast(BF16)
            x2b = sb4("x2b", [128, NKD * N])
            v2b = sb4("v2b", [128, NKD * N], BF16)
            xob = sb4("xob", [128, 4 * N])
            tok_z = p.run("vector", lambda v: v.memset(zeros128[:], 0.0), inc=True)
            p.waits("gpsimd", tok_z)
            tok_id = p.run("gpsimd", lambda g: g.affine_select(
                out=al["ident"][:], in_=zeros128[:], compare_op=AL.not_equal,
                fill=1.0, base=0, pattern=[[-1, 128]], channel_multiplier=1), inc=True)
            tok_c1 = p.run("gpsimd", lambda g: g.memset(al["ones_col16"][:], 1.0), inc=True)
            tok_c2 = p.run("gpsimd", lambda g: g.memset(al["ones_row32"][:], 1.0), inc=True)
            tok_c3 = p.run("gpsimd", lambda g: g.memset(al["ones_row16"][:], 1.0), inc=True)
            for k in range(NKD):
                p.dma("res4", "sync", al["aT"][:, k * 512:(k + 1) * 512],
                      a_b.ap()[k * 128:(k + 1) * 128, :])
            for c in range(4):
                p.dma("res4", "sync", al["bT"][:, c * 2048:(c + 1) * 2048],
                      b_b.ap()[c * 128:(c + 1) * 128, :])
            for k in range(NKD):
                p.dma("res4", "sync", al["wrT"][:, k * 8:(k + 1) * 8],
                      wr_b.ap()[k * 128:(k + 1) * 128, :])
            p.dma("res4", "sync", al["brT"][:], br_b.ap())
            tok_res = p.close("res4")
            p.dma("prep4", "gpsimd", al["repm"], repm_in)
            tok_rp = p.close("prep4")
            for e in ENGS:
                p.waits(e, tok_res, tok_rp, tok_id, tok_c1, tok_c2, tok_c3)
            p.barrier()
            oslot = {}
            prev_x2 = None
            for t in range(BACK_SPAN // N):
                xoff = t * N
                x2c = [x2b[:, k * N:(k + 1) * N] for k in range(NKD)]
                v2c = [v2b[:, k * N:(k + 1) * N] for k in range(NKD)]
                xoc = [xob[:, (k % 4) * N:(k % 4 + 1) * N] for k in range(NKD)]
                p.waits("sync", prev_x2)
                for k in range(NKD):
                    p.dma("x2r", "sync", x2c[k], x2_s[k * 128:(k + 1) * 128, xoff: xoff + N])
                tok_x2 = p.close("x2r")
                tok_v2 = emit_ln(p, al, N, x2c, tok_x2, out16c=v2c, tmp32=al["tmp32"])

                def pre_cb(m, oslot=oslot, t=t):
                    return oslot.get((t, m - 4)) or oslot.get((t - 1, m + NKD - 4))

                def post_cb(m, tok, xoff=xoff, t=t, xoc=xoc):
                    p.waits("sync", tok)
                    p.dma(f"o{m % 4}", "sync", xo_o[m * 128:(m + 1) * 128, xoff: xoff + N], xoc[m])
                    oslot[(t, m)] = p.close(f"o{m % 4}")

                tok_out = emit_moe(p, al, N, v2c, tok_v2, x2c, tok_x2, xoc, 0.5,
                                   pre_cb=pre_cb, post_cb=post_cb)
                prev_x2 = tok_out

        if finalize:
            for name in list(p.ch):
                if p.ch[name]["open"]:
                    p.close(name)
            p.drain_all_channels("sync")
            block = es.enter_context(nc.Block())
            p.finalize(block)
    return nc



def build_mega():
    from concourse.aot_env import allow_ici_all_to_all
    nc = bass.Bass("TRN2", target_bir_lowering=False, debug=False)
    dram = nc.dram_tensor
    io_f = {}
    for nm, shp in (("x_in", [D, TPC + 3]), ("halo_mask", [128, 3]),
                    ("w_in_T", [D, 2 * DI]), ("w_xp_T", [DI, 160]),
                    ("a_T", [D, E * R]), ("b_T", [E * R, D]), ("wr_T", [D, E]),
                    ("br_r", [1, E]), ("conv_w_t", [128, NKI * 4]),
                    ("conv_b_t", [128, NKI]), ("ln1_w_t", [128, NKD]),
                    ("ln1_b_t", [128, NKD]), ("repm_in", [8, 512])):
        io_f[nm] = dram(nm, shp, F32, kind="ExternalInput").ap()
    io_s = {}
    for nm, shp in (("wdt_j", [DTR, CPC]), ("db_j", [128, CPC // 128]),
                    ("alog_j", [CPC, DS]), ("dsk_j", [128, CPC // 128])):
        io_s[nm] = dram(nm, shp, F32, kind="ExternalInput").ap()
    io_b = {}
    for nm, shp in (("w_out_T", [DI, D]), ("w_ffp_T", [D, H]), ("w_ffo_T", [DI, D]),
                    ("ln2_w_t", [128, NKD]), ("ln2_b_t", [128, NKD])):
        io_b[nm] = dram(nm, shp, F32, kind="ExternalInput").ap()
    io_b["a_T"] = io_f["a_T"]; io_b["b_T"] = io_f["b_T"]
    io_b["wr_T"] = io_f["wr_T"]; io_b["br_r"] = io_f["br_r"]
    io_b["repm_in"] = io_f["repm_in"]
    io_b["xo_o"] = dram("xo_o", [D, TPC], F32, kind="ExternalOutput").ap()

    xc_x = dram("xc_x", [DI, TPC], BF16)
    z_x = dram("z_x", [DI, TPC], BF16)
    xd_g = dram("xd_g", [160, TPC], BF16)
    xc_sh = dram("xc_sh", [NC * CPC, TPC], BF16)
    z_sh = dram("z_sh", [NC * CPC, TPC], BF16)
    xd_sh = dram("xd_sh", [NC * 160, TPC], BF16)
    yg_x = dram("yg_x", [NC * CPC, TPC], BF16)
    yg_sh = dram("yg_sh", [NC * CPC, TPC], BF16)
    x1_s = dram("x1_s", [D, TPC], F32)
    u_s = dram("u_s", [D, TPC], F32)

    io_f["xc_o"] = xc_x.ap()
    io_f["z_o"] = z_x.ap()
    io_f["xd_o"] = xd_g.ap()
    io_f["x1_o"] = x1_s.ap()
    io_f["u_o"] = u_s.ap()

    xcs, zs, xds, ygs = xc_sh.ap(), z_sh.ap(), xd_sh.ap(), yg_x.ap()
    io_s["xc_ap"] = lambda r0, nr, co, ln: xcs[(co // TPC) * CPC + r0:(co // TPC) * CPC + r0 + nr, co % TPC: co % TPC + ln]
    io_s["z_ap"] = lambda r0, nr, co, ln: zs[(co // TPC) * CPC + r0:(co // TPC) * CPC + r0 + nr, co % TPC: co % TPC + ln]
    io_s["xd_ap"] = lambda r0, nr, co, ln: xds[(co // TPC) * 160 + r0:(co // TPC) * 160 + r0 + nr, co % TPC: co % TPC + ln]
    io_s["yg_ap"] = lambda r0, nr, co, ln: ygs[(co // TPC) * CPC + r0:(co // TPC) * CPC + r0 + nr, co % TPC: co % TPC + ln]
    ygshr = yg_sh.ap()
    io_b["ygb_ap"] = lambda r0, nr, xoff, n: ygshr[r0:r0 + nr, xoff: xoff + n]
    io_b["x1_in"] = x1_s.ap()
    io_b["u_in"] = u_s.ap()

    es = contextlib.ExitStack()
    with es:
        p = Prog(nc, es)
        kw = {}
        for nm, shp in (("w_out_b", [DI, D]), ("w_ffp_b", [D, H]), ("w_ffo_b", [DI, D]),
                        ("a_b", [D, E * R]), ("b_b", [E * R, D]), ("wr_b", [D, E]),
                        ("br_b", [1, E])):
            kw[nm] = dram("k_" + nm, shp, BF16)
        for nm, src in (("w_out_b", "w_out_T"), ("w_ffp_b", "w_ffp_T"),
                        ("w_ffo_b", "w_ffo_T"), ("a_b", "a_T"), ("b_b", "b_T"),
                        ("wr_b", "wr_T"), ("br_b", "br_r")):
            p.dma("prepk", "gpsimd", kw[nm].ap(), io_b[src])
        io_b.update(kw)
        io_b["prepk_tok"] = p.close("prepk")
        rg = [list(range(NC))]

        def cc(kind, i, o):
            def f(g, kind=kind, i=i, o=o):
                with allow_ici_all_to_all():
                    return g.collective_compute(kind, AL.bypass, replica_groups=rg,
                                                ins=[i], outs=[o])
            return p.run("gpsimd", f, inc=True)

        def mid_cb():
            for name in list(p.ch):
                if p.ch[name]["open"]:
                    p.close(name)
            p.drain_all_channels("gpsimd")
            cc("AllToAll", xc_x.ap().opt(), xc_sh.ap().opt())
            cc("AllToAll", z_x.ap().opt(), z_sh.ap().opt())

        build_front(nc=nc, p=p, io=io_f, finalize=False, mid_cb=mid_cb)
        for name in list(p.ch):
            if p.ch[name]["open"]:
                p.close(name)
        p.barrier()
        p.drain_all_channels("gpsimd")
        tok_fwd = cc("AllGather", xd_g.ap().opt(), xd_sh.ap().opt())
        for e in ENGS:
            p.waits(e, tok_fwd)
        p.barrier()
        build_scan(nc=nc, p=p, io=io_s, finalize=False)
        for name in list(p.ch):
            if p.ch[name]["open"]:
                p.close(name)
        p.barrier()
        p.drain_all_channels("gpsimd")
        tok_bwd = cc("AllToAll", yg_x.ap().opt(), yg_sh.ap().opt())
        for e in ENGS:
            p.waits(e, tok_bwd)
        p.barrier()
        build_back(nc=nc, p=p, io=io_b, finalize=False)
        for name in list(p.ch):
            if p.ch[name]["open"]:
                p.close(name)
        p.drain_all_channels("sync")
        block = es.enter_context(nc.Block())
        p.finalize(block)
    return nc


def _mega_inputs(inputs):
    front = _front_inputs(inputs)
    wdt = host_f32(inputs["L0_attn_dt_proj_w"]).T.copy()
    db = host_f32(inputs["L0_attn_dt_proj_b"])
    alog = host_f32(inputs["L0_attn_A_log"])
    dsk = host_f32(inputs["L0_attn_D_skip"])
    w_out_T = host_f32(inputs["L0_attn_out_proj_w"]).T.copy()
    w_ffp_T = host_f32(inputs["L0_attn_ff_proj_w"]).T.copy()
    w_ffo_T = host_f32(inputs["L0_attn_ff_out_w"]).T.copy()
    ln2_w_t = colsT(inputs["L0_attn_ln2_w"], NKD)
    ln2_b_t = colsT(inputs["L0_attn_ln2_b"], NKD)
    in_maps = []
    for i in range(NC):
        im = dict(front[i])
        c0 = i * CPC
        im.update(dict(
            wdt_j=np.ascontiguousarray(wdt[:, c0:c0 + CPC]),
            db_j=np.ascontiguousarray(db[c0:c0 + CPC].reshape(CPC // 128, 128).T),
            alog_j=np.ascontiguousarray(alog[c0:c0 + CPC]),
            dsk_j=np.ascontiguousarray(dsk[c0:c0 + CPC].reshape(CPC // 128, 128).T),
            w_out_T=w_out_T, w_ffp_T=w_ffp_T, w_ffo_T=w_ffo_T,
            ln2_w_t=ln2_w_t, ln2_b_t=ln2_b_t))
        in_maps.append(im)
    return in_maps


def _front_inputs(inputs):
    x = host_f32(inputs["L0_attn_x"]).reshape(TOK, D)
    w_in_T = host_f32(inputs["L0_attn_in_proj_w"]).T.copy()
    w_xp_T = host_f32(inputs["L0_attn_x_proj_w"]).T.copy()
    A = host_f32(inputs["L0_ffn_A"])
    Bw = host_f32(inputs["L0_ffn_B"])
    a_T = np.ascontiguousarray(A.transpose(2, 0, 1).reshape(D, E * R))
    b_T = np.ascontiguousarray(Bw.transpose(0, 2, 1).reshape(E * R, D))
    wr_T = host_f32(inputs["L0_ffn_Wr"]).T.copy()
    br_r = host_f32(inputs["L0_ffn_br"]).reshape(1, E)
    conv_w = host_f32(inputs["L0_attn_conv_w"]).reshape(DI, 4)
    conv_w_t = np.ascontiguousarray(conv_w.reshape(NKI, 128, 4).transpose(1, 0, 2).reshape(128, NKI * 4))
    conv_b_t = colsT(inputs["L0_attn_conv_b"], NKI)
    ln1_w_t = colsT(inputs["L0_attn_ln1_w"], NKD)
    ln1_b_t = colsT(inputs["L0_attn_ln1_b"], NKD)
    repm = np.zeros((8, 512), np.float32)
    for col in range(512):
        repm[col // 64, col] = 1.0
    shared = dict(w_in_T=w_in_T, w_xp_T=w_xp_T, a_T=a_T, b_T=b_T, wr_T=wr_T,
                  br_r=br_r, conv_w_t=conv_w_t, conv_b_t=conv_b_t,
                  ln1_w_t=ln1_w_t, ln1_b_t=ln1_b_t, repm_in=repm)
    in_maps = []
    for i in range(NC):
        s0 = i * TPC
        x_ext = np.zeros((TPC + 3, D), np.float32)
        if i % 4 != 0:
            x_ext[0:3] = x[s0 - 3: s0]
        x_ext[3:] = x[s0: s0 + TPC]
        hm = np.zeros((128, 3), np.float32) if i % 4 == 0 else np.ones((128, 3), np.float32)
        im = dict(shared)
        im["x_in"] = np.ascontiguousarray(x_ext.T)
        im["halo_mask"] = hm
        in_maps.append(im)
    return in_maps


def _scan_inputs(inputs, xc, z, xd):
    wdt = host_f32(inputs["L0_attn_dt_proj_w"]).T.copy()
    db = host_f32(inputs["L0_attn_dt_proj_b"])
    alog = host_f32(inputs["L0_attn_A_log"])
    dsk = host_f32(inputs["L0_attn_D_skip"])
    in_maps = []
    for j in range(NC):
        c0 = j * CPC
        in_maps.append(dict(
            xc_in=np.ascontiguousarray(xc[c0:c0 + CPC]),
            z_in=np.ascontiguousarray(z[c0:c0 + CPC]),
            xd_in=xd,
            wdt_j=np.ascontiguousarray(wdt[:, c0:c0 + CPC]),
            db_j=np.ascontiguousarray(db[c0:c0 + CPC].reshape(CPC // 128, 128).T),
            alog_j=np.ascontiguousarray(alog[c0:c0 + CPC]),
            dsk_j=np.ascontiguousarray(dsk[c0:c0 + CPC].reshape(CPC // 128, 128).T),
        ))
    return in_maps


def _back_inputs(inputs, yg, x1s, us):
    w_out_T = host_f32(inputs["L0_attn_out_proj_w"]).T.copy()
    w_ffp_T = host_f32(inputs["L0_attn_ff_proj_w"]).T.copy()
    w_ffo_T = host_f32(inputs["L0_attn_ff_out_w"]).T.copy()
    A = host_f32(inputs["L0_ffn_A"])
    Bw = host_f32(inputs["L0_ffn_B"])
    a_T = np.ascontiguousarray(A.transpose(2, 0, 1).reshape(D, E * R))
    b_T = np.ascontiguousarray(Bw.transpose(0, 2, 1).reshape(E * R, D))
    wr_T = host_f32(inputs["L0_ffn_Wr"]).T.copy()
    br_r = host_f32(inputs["L0_ffn_br"]).reshape(1, E)
    ln2_w_t = colsT(inputs["L0_attn_ln2_w"], NKD)
    ln2_b_t = colsT(inputs["L0_attn_ln2_b"], NKD)
    repm = np.zeros((8, 512), np.float32)
    for col in range(512):
        repm[col // 64, col] = 1.0
    shared = dict(w_out_T=w_out_T, w_ffp_T=w_ffp_T, w_ffo_T=w_ffo_T, a_T=a_T,
                  b_T=b_T, wr_T=wr_T, br_r=br_r, ln2_w_t=ln2_w_t,
                  ln2_b_t=ln2_b_t, repm_in=repm)
    in_maps = []
    for i in range(NC):
        im = dict(shared)
        im["yg_in"] = np.ascontiguousarray(yg[:, i * TPC:(i + 1) * TPC])
        im["x1_in"] = x1s[i]
        im["u_in"] = us[i]
        in_maps.append(im)
    return in_maps


_NC_CACHE = {}


def kernel(**inputs):
    cores = list(range(NC))
    if MODE == "mega":
        if "mega" not in _NC_CACHE:
            _NC_CACHE["mega"] = build_mega()
        res = run_bass_kernel_spmd(_NC_CACHE["mega"], _mega_inputs(inputs), core_ids=cores)
        out = np.concatenate([np.asarray(r["xo_o"]) for r in res.results], axis=1)
        return np.ascontiguousarray(out.T).reshape(B_, L_, D)
    if "front" not in _NC_CACHE:
        _NC_CACHE["front"] = build_front()
        _NC_CACHE["scan"] = build_scan()
        _NC_CACHE["back"] = build_back()
    resA = run_bass_kernel_spmd(_NC_CACHE["front"], _front_inputs(inputs), core_ids=cores)
    xc = np.concatenate([np.asarray(r["xc_o"]) for r in resA.results], axis=1)
    z = np.concatenate([np.asarray(r["z_o"]) for r in resA.results], axis=1)
    xd = np.ascontiguousarray(np.concatenate([np.asarray(r["xd_o"]) for r in resA.results], axis=1))
    x1s = [np.asarray(r["x1_o"]) for r in resA.results]
    us = [np.asarray(r["u_o"]) for r in resA.results]
    resB = run_bass_kernel_spmd(_NC_CACHE["scan"], _scan_inputs(inputs, xc, z, xd), core_ids=cores)
    yg = np.concatenate([np.asarray(r["yg_o"]) for r in resB.results], axis=0)
    resC = run_bass_kernel_spmd(_NC_CACHE["back"], _back_inputs(inputs, yg, x1s, us), core_ids=cores)
    out = np.concatenate([np.asarray(r["xo_o"]) for r in resC.results], axis=1)
    return np.ascontiguousarray(out.T).reshape(B_, L_, D)


# version 09
# speedup vs baseline: 1.6442x; 1.0248x over previous
import contextlib
import numpy as np
import ml_dtypes
import concourse.bass as bass
import concourse.mybir as mybir
from concourse.bass_utils import run_bass_kernel_spmd

F32 = mybir.dt.float32
BF16 = mybir.dt.bfloat16
BF16_NP = ml_dtypes.bfloat16
AL = mybir.AluOpType
AF = mybir.ActivationFunctionType
AX = mybir.AxisListType.X

NC = 8
B_, L_, D, DI, DS, DTR, H = 2, 16384, 2048, 4096, 16, 128, 8192
E, R = 8, 64
TOK = B_ * L_
TPC = TOK // NC
CPC = DI // NC
NKD = D // 128
NKI = DI // 128
EPS = 1e-5
TT_A = 512
TT_C = 256
FRONT_TILES = TPC // TT_A
LC = 1024
SCAN_CTS = TOK // LC
SCAN_DEBUG = False
BACK_TILES = TPC // TT_C
MODE = "mega"
BACK_SPAN = TPC

ENGS = ("sync", "tensor", "vector", "scalar", "gpsimd")
DMAQ = ("sync", "scalar", "gpsimd")


class Prog:

    def __init__(self, nc, es):
        self.nc = nc
        self.es = es
        self.ops = {e: [] for e in ENGS}
        self.sem = {e: es.enter_context(nc.semaphore(f"es_{e}")) for e in ENGS}
        self.cnt = {e: 0 for e in ENGS}
        self.lastwait = {e: {} for e in ENGS}
        self.ch = {}

    def _channel(self, name, eng):
        if name not in self.ch:
            self.ch[name] = dict(sem=self.es.enter_context(self.nc.semaphore(f"dc_{name}")),
                                 eng=eng, total=0, closed=0, open=False)
        c = self.ch[name]
        assert c["eng"] == eng, f"channel {name} is owned by {c['eng']}"
        return c

    def dma(self, name, eng, out, in_):
        c = self._channel(name, eng)
        if not c["open"]:
            if c["closed"] > 0:
                self.ops[eng].append(
                    lambda ifc, sem=c["sem"], v=c["closed"]: ifc.wait_ge(sem, v))
            c["open"] = True
        c["total"] += 16
        sem = c["sem"]
        def cl(ifc, out=out, in_=in_, sem=sem):
            ifc.dma_start(out=out, in_=in_).then_inc(sem, 16)
        self.ops[eng].append(cl)

    def close(self, name):
        c = self.ch[name]
        c["open"] = False
        c["closed"] = c["total"]
        return ("d", name, c["closed"])

    def run(self, eng, fn, inc=False):
        sem = self.sem[eng]
        if eng in ("vector", "scalar", "gpsimd"):
            self.ops[eng].append(lambda ifc: ifc.drain())
        if inc:
            self.cnt[eng] += 1

            def cl(ifc, fn=fn, sem=sem):
                fn(ifc).then_inc(sem, 1)
        else:
            def cl(ifc, fn=fn):
                fn(ifc)
        self.ops[eng].append(cl)
        return ("e", eng, self.cnt[eng]) if inc else None

    def wait(self, eng, tok):
        if tok is None:
            return
        kind, src, val = tok
        if val <= 0:
            return
        if kind == "e" and src == eng:
            return
        key = (kind, src)
        if self.lastwait[eng].get(key, 0) >= val:
            return
        self.lastwait[eng][key] = val
        if kind == "d":
            c = self.ch[src]
            assert val <= c["closed"], f"wait on open batch of channel {src}"
            sem = c["sem"]
        else:
            sem = self.sem[src]
        self.ops[eng].append(lambda ifc, sem=sem, val=val: ifc.wait_ge(sem, val))

    def waits(self, eng, *toks):
        for t in toks:
            self.wait(eng, t)

    def mark(self, eng):
        sem = self.sem[eng]
        prev = self.cnt[eng]
        if eng == "gpsimd" and prev > 0:
            self.ops[eng].append(lambda ifc, sem=sem, v=prev: ifc.wait_ge(sem, v))
        self.cnt[eng] += 1
        self.ops[eng].append(lambda ifc: ifc.drain())
        self.ops[eng].append(lambda ifc, sem=sem: ifc.sem_inc(sem, 1))
        return ("e", eng, self.cnt[eng])

    def barrier(self, engs=ENGS):
        toks = {e: self.mark(e) for e in engs}
        for e in engs:
            for o in engs:
                if o != e:
                    self.wait(e, toks[o])

    def drain_all_channels(self, eng):
        for name, c in self.ch.items():
            if c["total"] > 0:
                self.ops[eng].append(
                    lambda ifc, sem=c["sem"], v=c["total"]: ifc.wait_ge(sem, v))

    def finalize(self, block):
        for e in ENGS:
            if not self.ops[e]:
                continue
            deco = getattr(block, e)
            ops = self.ops[e]

            def body(ifc, ops=ops):
                for cl in ops:
                    cl(ifc)
            deco(body)



def emit_ln(p, al, N, x32c, ready_tok, out16c=None, out32c=None, affine=None,
            tmp32=None):
    nc = p.nc
    psA = al["psA"][0:1, 0:N]
    psB = al["psB"][0:1, 0:N]
    p.waits("scalar", ready_tok)
    cons = [None, None]
    for k in range(NKD):
        b = k % 2
        xbt = al["xbt"][:, b * al["ln_n"]: b * al["ln_n"] + N]
        p.waits("scalar", cons[b])
        tk = p.run("scalar", lambda s, o=xbt, i=x32c[k]: s.copy(o, i), inc=True)
        p.waits("tensor", tk)
        p.run("tensor", lambda t, o=psA, w=al["ones_col16"], r=xbt,
              s=(k == 0), e=(k == NKD - 1): t.matmul(o, w[:], r, start=s, stop=e))
        cons[b] = p.mark("tensor")
    for k in range(NKD):
        b = k % 2
        sqt = al["sqt"][:, b * al["ln_n"]: b * al["ln_n"] + N]
        p.waits("scalar", cons[b])
        tk = p.run("scalar", lambda s, o=sqt, i=x32c[k]: s.activation(o, i, AF.Square), inc=True)
        p.waits("tensor", tk)
        p.run("tensor", lambda t, o=psB, w=al["ones_col16"], r=sqt,
              s=(k == 0), e=(k == NKD - 1): t.matmul(o, w[:], r, start=s, stop=e))
        cons[b] = p.mark("tensor")
    tok_mm = cons[(NKD - 1) % 2]

    m = al["st_m"][:, 0:N]
    q = al["st_q"][:, 0:N]
    var = al["st_var"][:, 0:N]
    sd = al["st_sd"][:, 0:N]
    r_ = al["st_r"][:, 0:N]
    nmr = al["st_nmr"][:, 0:N]
    t1 = al["st_t1"][:, 0:N]
    t2 = al["st_t2"][:, 0:N]
    p.waits("vector", tok_mm)
    p.run("vector", lambda v, o=m, i=psA: v.tensor_scalar(o, i, 1.0 / D, None, AL.mult))
    p.run("vector", lambda v, o=q, i=psB: v.tensor_scalar(o, i, 1.0 / D, None, AL.mult))
    p.run("vector", lambda v, o=t1, a=m: v.tensor_tensor(o, a, a, AL.mult))
    p.run("vector", lambda v, o=var, a=q, b=t1: v.tensor_tensor(o, a, b, AL.subtract))
    tv = p.run("vector", lambda v, o=t2, i=var: v.tensor_scalar(o, i, EPS, None, AL.add), inc=True)
    p.waits("scalar", tv)
    ts_ = p.run("scalar", lambda s, o=sd, i=t2: s.activation(o, i, AF.Sqrt), inc=True)
    p.waits("vector", ts_)
    p.run("vector", lambda v, o=r_, i=sd: v.reciprocal(o, i))
    p.run("vector", lambda v, o=t1, a=m, b=r_: v.tensor_tensor(o, a, b, AL.mult))
    tv2 = p.run("vector", lambda v, o=nmr, i=t1: v.tensor_scalar(o, i, -1.0, None, AL.mult), inc=True)

    psR = al["psR"][:, 0:N]
    psN = al["psN"][:, 0:N]
    p.waits("tensor", tv2)
    p.run("tensor", lambda t, o=psR, w=al["ones_row32"], r=r_: t.matmul(o, w[:], r, start=True, stop=True))
    tok_rep = p.run("tensor", lambda t, o=psN, w=al["ones_row32"], r=nmr: t.matmul(o, w[:], r, start=True, stop=True), inc=True)

    p.waits("vector", tok_rep)
    for k in range(NKD):
        p.run("vector", lambda v, o=tmp32[0:128, 0:N], a=x32c[k], b=psR: v.tensor_tensor(o, a, b, AL.mult))
        if out32c is not None:
            p.run("vector", lambda v, o=out32c[k], a=tmp32[0:128, 0:N], b=psN: v.tensor_tensor(o, a, b, AL.add))
        if out16c is not None and affine is None:
            src = out32c[k] if out32c is not None else None
            if src is not None:
                p.run("vector", lambda v, o=out16c[k], i=src: v.tensor_copy(o, i))
            else:
                p.run("vector", lambda v, o=out16c[k], a=tmp32[0:128, 0:N], b=psN: v.tensor_tensor(o, a, b, AL.add))
        if out16c is not None and affine is not None:
            wcol, bcol = affine
            p.run("vector", lambda v, o=al["tmp32b"][0:128, 0:N], a=tmp32[0:128, 0:N], b=psN: v.tensor_tensor(o, a, b, AL.add))
            p.run("vector", lambda v, o=out16c[k], i=al["tmp32b"][0:128, 0:N], w=wcol[k], b=bcol[k]:
                  v.tensor_scalar(o, i, w, b, AL.mult, AL.add))
    return p.mark("vector")


def emit_moe(p, al, N, v16c, vtok, x32c, xtok, out32c, scale, pre_cb=None, post_cb=None):
    nc = p.nc
    nst = (N + 127) // 128
    psR = al["psRt"]
    lgs = al["lgs"]
    mx = al["mx"]
    wsc = al["wsc"]
    sm = al["sm"]
    maskb = al["maskb"]
    p.waits("tensor", vtok)
    for st in range(nst):
        Ns = min(128, N - 128 * st)
        out = psR[0:Ns, st * 8:(st + 1) * 8]
        for k in range(NKD):
            p.run("tensor", lambda t, o=out, w=v16c[k][:, st * 128: st * 128 + Ns],
                  r=al["wrT"][:, k * 8:(k + 1) * 8], s=(k == 0):
                  t.matmul(o, w, r, start=s, stop=False))
        p.run("tensor", lambda t, o=out, w=al["ones_row16"][:, 0:Ns], r=al["brT"][:]:
              t.matmul(o, w, r, start=False, stop=True))
    tok_r = p.mark("tensor")
    p.waits("scalar", tok_r)
    ts_ = None
    for st in range(nst):
        Ns = min(128, N - 128 * st)
        ts_ = p.run("scalar", lambda s, o=lgs[0:Ns, st * 8:(st + 1) * 8],
                    i=psR[0:Ns, st * 8:(st + 1) * 8]: s.copy(o, i), inc=True)
    p.waits("vector", ts_)
    for st in range(nst):
        Ns = min(128, N - 128 * st)
        p.run("vector", lambda v, o=mx[0:Ns, st:st + 1], i=lgs[0:Ns, st * 8:(st + 1) * 8]:
              v.reduce_max(o, i, axis=AX))
    for st in range(nst):
        Ns = min(128, N - 128 * st)
        p.run("vector", lambda v, o=al["nmx"][0:Ns, st:st + 1], i=mx[0:Ns, st:st + 1]:
              v.tensor_scalar(o, i, -1.0, None, AL.mult))
    tv = p.mark("vector")
    p.waits("scalar", tv)
    for st in range(nst):
        Ns = min(128, N - 128 * st)
        p.run("scalar", lambda s, o=al["exps"][0:Ns, st * 8:(st + 1) * 8], i=lgs[0:Ns, st * 8:(st + 1) * 8],
              b=al["nmx"][0:Ns, st:st + 1]: s.activation(o, i, AF.Exp, bias=b))
    ts2 = p.mark("scalar")
    p.waits("vector", ts2)
    for st in range(nst):
        Ns = min(128, N - 128 * st)
        p.run("vector", lambda v, o=sm[0:Ns, st:st + 1], i=al["exps"][0:Ns, st * 8:(st + 1) * 8]:
              v.reduce_sum(o, i, axis=AX))
    for st in range(nst):
        Ns = min(128, N - 128 * st)
        p.run("vector", lambda v, o=al["w0"][0:Ns, st:st + 1], i=sm[0:Ns, st:st + 1]: v.reciprocal(o, i))
    for st in range(nst):
        Ns = min(128, N - 128 * st)
        p.run("vector", lambda v, o=wsc[0:Ns, st:st + 1], i=al["w0"][0:Ns, st:st + 1]:
              v.tensor_scalar(o, i, scale / R, None, AL.mult))
    for st in range(nst):
        Ns = min(128, N - 128 * st)
        p.run("vector", lambda v, o=maskb[0:Ns, st * 8:(st + 1) * 8], i=lgs[0:Ns, st * 8:(st + 1) * 8],
              a=mx[0:Ns, st:st + 1], b=wsc[0:Ns, st:st + 1]:
              v.tensor_scalar(o, i, a, b, AL.is_equal, AL.mult))
    tv2 = p.mark("vector")
    psT = al["psT"]
    p.waits("tensor", tv2)
    for st in range(nst):
        Ns = min(128, N - 128 * st)
        p.run("tensor", lambda t, o=psT[0:8, st * 128: st * 128 + Ns],
              i=maskb[0:Ns, st * 8:(st + 1) * 8], id_=al["ident"], n=Ns:
              t.transpose(o, i, id_[0:n, 0:n]))
    tok_t = p.mark("tensor")
    p.waits("scalar", tok_t)
    ts3 = p.run("scalar", lambda s, o=al["maskT"][0:8, 0:N], i=psT[0:8, 0:N]: s.copy(o, i), inc=True)
    psM = al["psM"][:, 0:N]
    psH = al["psH"][:, 0:N]
    hm = al["hm16"]
    tok_hm = []
    for c in range(4):
        p.waits("tensor", ts3)
        tmr = p.run("tensor", lambda t, o=psM, w=al["repm"][:, c * 128:(c + 1) * 128],
              r=al["maskT"][0:8, 0:N]: t.matmul(o, w, r, start=True, stop=True), inc=True)
        p.waits("scalar", tmr)
        tmc = p.run("scalar", lambda s, o=al["mrep16"][:, 0:N], i=psM: s.copy(o, i), inc=True)
        for k in range(NKD):
            p.run("tensor", lambda t, o=psH, w=al["aT"][:, (k * 4 + c) * 128:(k * 4 + c) * 128 + 128],
                  r=v16c[k], s=(k == 0), e=(k == NKD - 1): t.matmul(o, w, r, start=s, stop=e))
        tmh = p.mark("tensor")
        p.waits("scalar", tmh)
        tch = p.run("scalar", lambda s, o=al["h16"][:, 0:N], i=psH: s.copy(o, i), inc=True)
        p.waits("vector", tch, tmc)
        thm = p.run("vector", lambda v, o=hm[:, c * al["ffn_n"]: c * al["ffn_n"] + N],
              a=al["h16"][:, 0:N], b=al["mrep16"][:, 0:N]: v.tensor_tensor(o, a, b, AL.mult), inc=True)
        tok_hm.append(thm)
        p.waits("tensor", thm, tch, tmc)
    psS = al["psS"][:, 0:N]
    p.waits("tensor", tok_hm[-1], xtok)
    out_tok = None
    for mchunk in range(NKD):
        for c in range(4):
            p.run("tensor", lambda t, o=psS, w=al["bT"][:, (c * NKD + mchunk) * 128:(c * NKD + mchunk) * 128 + 128],
                  r=hm[:, c * al["ffn_n"]: c * al["ffn_n"] + N], s=(c == 0), e=(c == 3):
                  t.matmul(o, w, r, start=s, stop=e))
        tms = p.mark("tensor")
        p.waits("vector", tms)
        if pre_cb is not None:
            p.waits("vector", pre_cb(mchunk))
        out_tok = p.run("vector", lambda v, o=out32c[mchunk], a=x32c[mchunk], b=psS:
                        v.tensor_tensor(o, a, b, AL.add), inc=True)
        p.waits("tensor", out_tok)
        if post_cb is not None:
            post_cb(mchunk, out_tok)
    return out_tok


def host_f32(a):
    return np.ascontiguousarray(np.asarray(a, dtype=np.float32))


def colsT(vec, nchunk):
    return np.ascontiguousarray(np.asarray(vec, dtype=np.float32).reshape(nchunk, 128).T)



def build_front(nc=None, p=None, io=None, finalize=True, mid_cb=None):
    if nc is None:
        nc = bass.Bass("TRN2", target_bir_lowering=False, debug=False)
    dram = nc.dram_tensor
    if io is None:
        io = {}
        for nm, shp in (("x_in", [D, TPC + 3]), ("halo_mask", [128, 3]),
                        ("w_in_T", [D, 2 * DI]), ("w_xp_T", [DI, 160]),
                        ("a_T", [D, E * R]), ("b_T", [E * R, D]), ("wr_T", [D, E]),
                        ("br_r", [1, E]), ("conv_w_t", [128, NKI * 4]),
                        ("conv_b_t", [128, NKI]), ("ln1_w_t", [128, NKD]),
                        ("ln1_b_t", [128, NKD]), ("repm_in", [8, 512])):
            io[nm] = dram(nm, shp, F32, kind="ExternalInput").ap()
        io["xc_o"] = dram("xc_o", [DI, TPC], BF16, kind="ExternalOutput").ap()
        io["z_o"] = dram("z_o", [DI, TPC], BF16, kind="ExternalOutput").ap()
        io["xd_o"] = dram("xd_o", [160, TPC], BF16, kind="ExternalOutput").ap()
        io["x1_o"] = dram("x1_o", [D, TPC], F32, kind="ExternalOutput").ap()
        io["u_o"] = dram("u_o", [D, TPC], F32, kind="ExternalOutput").ap()
    (x_in, halo_mask, w_in_T, w_xp_T, a_T, b_T, wr_T, br_r, conv_w_t, conv_b_t,
     ln1_w_t, ln1_b_t, repm_in, xc_o, z_o, xd_o, x1_o, u_o) = (
        io["x_in"], io["halo_mask"], io["w_in_T"], io["w_xp_T"], io["a_T"],
        io["b_T"], io["wr_T"], io["br_r"], io["conv_w_t"], io["conv_b_t"],
        io["ln1_w_t"], io["ln1_b_t"], io["repm_in"], io["xc_o"], io["z_o"],
        io["xd_o"], io["x1_o"], io["u_o"])

    w_in_b = dram("f_w_in_b", [D, 2 * DI], BF16)
    w_xp_b = dram("f_w_xp_b", [DI, 160], BF16)
    a_b = dram("f_a_b", [D, E * R], BF16)
    b_b = dram("f_b_b", [E * R, D], BF16)
    wr_b = dram("f_wr_b", [D, E], BF16)
    br_b = dram("f_br_b", [1, E], BF16)
    t_s = dram("f_t_s", [D, TPC + 3], BF16).ap()

    PF = "f_"
    es = contextlib.ExitStack()
    with es:
        if p is None:
            p = Prog(nc, es)
        p.dma("prep", "gpsimd", w_in_b.ap(), w_in_T)
        p.dma("prep", "gpsimd", w_xp_b.ap(), w_xp_T)
        p.dma("prep", "gpsimd", a_b.ap(), a_T)
        p.dma("prep", "gpsimd", b_b.ap(), b_T)
        p.dma("prep", "gpsimd", wr_b.ap(), wr_T)
        p.dma("prep", "gpsimd", al_repm_dummy := None or br_b.ap(), br_r)
        tok_prep = p.close("prep")

        N = TT_A
        with contextlib.ExitStack() as e0:
            sb0 = lambda name, shape, dt=F32: e0.enter_context(nc.sbuf_tensor(PF + "p0" + name, shape, dt))[:]
            pst0 = lambda name, shape, dt=F32: e0.enter_context(nc.psum_tensor(PF + name, shape, dt))[:]
            al = {}
            bufA = sb0("bufA", [128, NKD * N])
            bufB = sb0("bufB", [128, NKD * N])
            al["xbt"] = sb0("xbt", [128, 2 * N], BF16)
            al["sqt"] = sb0("sqt", [128, 2 * N], BF16)
            al["ln_n"] = N
            vt16 = sb0("vt16", [128, NKD * N], BF16)
            al["tmp32"] = sb0("tmp32", [128, N])
            al["tmp32b"] = sb0("tmp32b", [128, N])
            for nm in ("st_m", "st_q", "st_var", "st_sd", "st_r", "st_nmr", "st_t1", "st_t2"):
                al[nm] = sb0(nm, [1, N])
            al["ones_col16"] = e0.enter_context(nc.sbuf_tensor(PF + "p0ones_c", [128, 1], BF16))
            al["ones_row32"] = e0.enter_context(nc.sbuf_tensor(PF + "p0ones_r", [1, 128], F32))
            al["ones_row16"] = sb0("ones_r16", [1, 128], BF16)
            al["ident"] = e0.enter_context(nc.sbuf_tensor(PF + "p0ident", [128, 128], BF16))
            zeros128 = e0.enter_context(nc.sbuf_tensor(PF + "p0zeros", [128, 128], BF16))
            al["lgs"] = sb0("lgs", [128, 32])
            al["mx"] = sb0("mx", [128, 4])
            al["nmx"] = sb0("nmx", [128, 4])
            al["sm"] = sb0("sm", [128, 4])
            al["wsc"] = sb0("wsc", [128, 4])
            al["w0"] = sb0("w0", [128, 4])
            al["exps"] = sb0("exps", [128, 32])
            al["maskb"] = sb0("maskb", [128, 32], BF16)
            al["maskT"] = sb0("maskT", [8, N], BF16)
            al["repm"] = sb0("repm", [8, 512], BF16)
            al["mrep16"] = sb0("mrep16", [128, N], BF16)
            al["h16"] = sb0("h16", [128, N], BF16)
            al["hm16"] = sb0("hm16", [128, 4 * N], BF16)
            al["ffn_n"] = N
            al["aT"] = sb0("aT", [128, NKD * 4 * 128], BF16)
            al["bT"] = sb0("bT", [128, 4 * NKD * 128], BF16)
            al["wrT"] = sb0("wrT", [128, NKD * 8], BF16)
            al["brT"] = e0.enter_context(nc.sbuf_tensor(PF + "p0brT", [1, 8], BF16))
            al["psA"] = pst0("p0psA", [1, 512])
            al["psB"] = pst0("p0psB", [1, 512])
            al["psR"] = pst0("p0psR", [128, 512])
            al["psN"] = pst0("p0psN", [128, 512])
            al["psS"] = pst0("p0psS", [128, 512])
            al["psRt"] = al["psS"][:, 0:32]
            al["psM"] = al["psR"]
            al["psH"] = al["psN"]
            psX0 = pst0("p0psX", [128, 512])
            al["psT"] = psX0[0:8, :].bitcast(BF16)
            ln1w = sb0("ln1w", [128, NKD])
            ln1b = sb0("ln1b", [128, NKD])
            hmask = sb0("hmask", [128, 3])
            tok_z = p.run("vector", lambda v: v.memset(zeros128[:], 0.0), inc=True)
            p.waits("gpsimd", tok_z)
            tok_id = p.run("gpsimd", lambda g: g.affine_select(
                out=al["ident"][:], in_=zeros128[:], compare_op=AL.not_equal,
                fill=1.0, base=0, pattern=[[-1, 128]], channel_multiplier=1), inc=True)
            tok_c1 = p.run("gpsimd", lambda g: g.memset(al["ones_col16"][:], 1.0), inc=True)
            tok_c2 = p.run("gpsimd", lambda g: g.memset(al["ones_row32"][:], 1.0), inc=True)
            tok_c3 = p.run("gpsimd", lambda g: g.memset(al["ones_row16"][:], 1.0), inc=True)
            p.waits("sync", tok_prep)
            for k in range(NKD):
                p.dma("res", "sync", al["aT"][:, k * 512:(k + 1) * 512],
                      a_b.ap()[k * 128:(k + 1) * 128, :])
            for c in range(4):
                p.dma("res", "sync", al["bT"][:, c * 2048:(c + 1) * 2048],
                      b_b.ap()[c * 128:(c + 1) * 128, :])
            for k in range(NKD):
                p.dma("res", "sync", al["wrT"][:, k * 8:(k + 1) * 8],
                      wr_b.ap()[k * 128:(k + 1) * 128, :])
            p.dma("res", "sync", al["brT"][:], br_b.ap())
            p.dma("res", "sync", ln1w, ln1_w_t)
            p.dma("res", "sync", ln1b, ln1_b_t)
            p.dma("res", "sync", hmask, halo_mask)
            tok_res = p.close("res")
            p.dma("prep0", "gpsimd", al["repm"], repm_in)
            tok_rp = p.close("prep0")
            for e in ENGS:
                p.waits(e, tok_res, tok_rp, tok_id, tok_c1, tok_c2, tok_c3)
            p.barrier()
            ln1w_c = [ln1w[:, k:k + 1] for k in range(NKD)]
            ln1b_c = [ln1b[:, k:k + 1] for k in range(NKD)]
            store_toks = {}

            def p0_pass(N_, xoff, halo_pass):
                xc = [bufA[:, k * TT_A: k * TT_A + N_] for k in range(NKD)]
                x1c = [bufB[:, k * TT_A: k * TT_A + N_] for k in range(NKD)]
                vc = [vt16[:, k * TT_A: k * TT_A + N_] for k in range(NKD)]
                p.waits("vector", store_toks.get("x1"), store_toks.get("u"), store_toks.get("t"))
                p.waits("sync", store_toks.get("x1"), store_toks.get("u"), store_toks.get("t"),
                        store_toks.get("tln"))
                for k in range(NKD):
                    p.dma("x", "sync", xc[k], x_in[k * 128:(k + 1) * 128, xoff: xoff + N_])
                tok_x = p.close("x")
                tok_v = emit_ln(p, al, N_, xc, tok_x, out16c=vc, tmp32=al["tmp32"])
                tok_x1 = emit_moe(p, al, N_, vc, tok_v, xc, tok_x, x1c, 0.5)
                if not halo_pass:
                    p.waits("sync", tok_x1)
                    for k in range(NKD):
                        p.dma("ox1", "sync", x1_o[k * 128:(k + 1) * 128, xoff - 3: xoff - 3 + N_], x1c[k])
                    store_toks["x1"] = p.close("ox1")
                uc = xc
                tok_u = emit_ln(p, al, N_, x1c, tok_x1, out32c=uc, tmp32=al["tmp32"])
                if not halo_pass:
                    p.waits("sync", tok_u)
                    for k in range(NKD):
                        p.dma("ou", "sync", u_o[k * 128:(k + 1) * 128, xoff - 3: xoff - 3 + N_], uc[k])
                    store_toks["u"] = p.close("ou")
                tok_t = emit_ln(p, al, N_, uc, tok_u, out16c=vc, affine=(ln1w_c, ln1b_c),
                                tmp32=al["tmp32"])
                if halo_pass:
                    p.waits("vector", tok_t)
                    for k in range(NKD):
                        tok_t = p.run("vector", lambda v, o=al["tmp32"][:, 0:3],
                                      a=vt16[:, k * TT_A: k * TT_A + 3]:
                                      v.tensor_tensor(o, a, hmask, AL.mult), inc=True)
                        tok_t = p.run("vector", lambda v, o=vt16[:, k * TT_A: k * TT_A + 3],
                                      i=al["tmp32"][:, 0:3]: v.tensor_copy(o, i), inc=True)
                    p.waits("sync", tok_t)
                    for k in range(NKD):
                        p.dma("ot", "sync", t_s[k * 128:(k + 1) * 128, 0:3],
                              vt16[:, k * TT_A: k * TT_A + 3])
                    store_toks["t"] = p.close("ot")
                else:
                    p.waits("sync", tok_t)
                    for k in range(NKD):
                        p.dma("ot", "sync", t_s[k * 128:(k + 1) * 128, xoff: xoff + N_], vc[k])
                    store_toks["t"] = p.close("ot")
                store_toks["tln"] = tok_t

            p0_pass(4, 0, True)
            for t in range(TPC // N):
                p0_pass(N, 3 + t * N, False)
            for name in list(p.ch):
                if p.ch[name]["open"]:
                    p.close(name)
            p.drain_all_channels("sync")
            p.barrier()

        PT = 2048
        with contextlib.ExitStack() as e1:
            sb1 = lambda name, shape, dt=F32: e1.enter_context(nc.sbuf_tensor(PF + "p1" + name, shape, dt))[:]
            pst1 = lambda name, shape, dt=F32: e1.enter_context(nc.psum_tensor(PF + name, shape, dt))[:]
            t16 = sb1("t16", [128, NKD * (PT + 3)], BF16)
            wbuf = [sb1(f"wbuf{b}", [128, NKD * 512], BF16) for b in range(2)]
            work = sb1("work", [128, 4 * (PT + 3)], BF16)
            xc16 = sb1("xc16", [128, 4 * PT], BF16)
            z16 = sb1("z16", [128, 2 * PT], BF16)
            cva = sb1("cva", [128, PT])
            cvb = sb1("cvb", [128, PT])
            convw = sb1("convw", [128, NKI * 4])
            convb = sb1("convb", [128, NKI])
            psMM = [pst1("p1psMM", [128, 512]), pst1("p1psMM2", [128, 512])]
            p.dma("res1", "sync", convw, conv_w_t)
            p.dma("res1", "sync", convb, conv_b_t)
            tok_res1 = p.close("res1")
            for e in ("sync", "tensor", "vector", "scalar"):
                p.waits(e, tok_res1)
            wfree = [None, None]
            evac = [None, None]
            ldone = {}
            for t in range(TPC // PT):
                xoff = t * PT
                p.waits("sync", ldone.get("t_cons"))
                for k in range(NKD):
                    p.dma("tl", "sync", t16[:, k * (PT + 3): k * (PT + 3) + PT + 3],
                          t_s[k * 128:(k + 1) * 128, xoff: xoff + PT + 3])
                tok_t = p.close("tl")
                p.waits("tensor", tok_t)
                for m_abs in range(2 * NKI):
                    g, mc = m_abs // 4, m_abs % 4
                    par = g % 2
                    if mc == 0:
                        p.waits("sync", wfree[par])
                        for k in range(NKD):
                            p.dma(f"w{par}", "sync", wbuf[par][:, k * 512:(k + 1) * 512],
                                  w_in_b.ap()[k * 128:(k + 1) * 128, g * 512:(g + 1) * 512])
                        tok_w = p.close(f"w{par}")
                        p.waits("tensor", tok_w)
                    is_xm = m_abs < NKI
                    cslot = m_abs % 4
                    zslot = m_abs % 2
                    npieces = PT // 512
                    dst = work[:, cslot * (PT + 3): cslot * (PT + 3) + PT + 3] if is_xm else None
                    if is_xm:
                        p.waits("scalar", ldone.get(("work", cslot)))
                    else:
                        p.waits("scalar", ldone.get(("z", zslot)))
                    total = PT + 3 if is_xm else PT
                    toff = 0 if is_xm else 3
                    for h in range((total + 511) // 512):
                        c0 = h * 512
                        cn = min(512, total - c0)
                        bank = (m_abs * 5 + h) % 2
                        ps = psMM[bank]
                        p.waits("tensor", evac[bank])
                        for k in range(NKD):
                            p.run("tensor", lambda tt, o=ps[:, 0:cn],
                                  w=wbuf[par][:, k * 512 + mc * 128: k * 512 + mc * 128 + 128],
                                  r=t16[:, k * (PT + 3) + toff + c0: k * (PT + 3) + toff + c0 + cn],
                                  st=(k == 0), e_=(k == NKD - 1): tt.matmul(o, w, r, start=st, stop=e_))
                        tmm = p.mark("tensor")
                        p.waits("scalar", tmm)
                        if is_xm:
                            tev = p.run("scalar", lambda sc, o=dst[:, c0:c0 + cn], i=ps[:, 0:cn]:
                                        sc.copy(o, i), inc=True)
                        else:
                            tev = p.run("scalar", lambda sc, o=z16[:, zslot * PT + c0: zslot * PT + c0 + cn],
                                        i=ps[:, 0:cn]: sc.copy(o, i), inc=True)
                        evac[bank] = tev
                    if is_xm:
                        cw = [convw[:, m_abs * 4 + kk: m_abs * 4 + kk + 1] for kk in range(4)]
                        p.waits("vector", tev, ldone.get(("cv", 0)))
                        p.run("vector", lambda v, o=cva[:, 0:PT], i=dst[:, 0:PT], w=cw[0]:
                              v.tensor_scalar(o, i, w, None, AL.mult))
                        for kk in range(1, 4):
                            src, dsx = (cva, cvb) if kk % 2 == 1 else (cvb, cva)
                            p.run("vector", lambda v, o=dsx[:, 0:PT], i=dst[:, kk:kk + PT], w=cw[kk], b=src[:, 0:PT]:
                                  v.scalar_tensor_tensor(o, i, w, b, AL.mult, AL.add))
                        tok_cv = p.mark("vector")
                        ldone[("work", cslot)] = tok_cv
                        xslot = m_abs % 4
                        p.waits("scalar", tok_cv, ldone.get(("xc", xslot)))
                        tok_xc = p.run("scalar", lambda sc, o=xc16[:, xslot * PT: xslot * PT + PT],
                                       i=cvb[:, 0:PT], b=convb[:, m_abs:m_abs + 1]:
                                       sc.activation(o, i, AF.Silu, bias=b), inc=True)
                        ldone[("cv", 0)] = tok_xc
                        p.waits("sync", tok_xc)
                        p.dma(f"oxc{xslot}", "sync",
                              xc_o[m_abs * 128:(m_abs + 1) * 128, xoff: xoff + PT],
                              xc16[:, xslot * PT: xslot * PT + PT])
                        ldone[("xc", xslot)] = p.close(f"oxc{xslot}")
                    else:
                        p.waits("sync", tev)
                        p.dma(f"oz{zslot}", "sync",
                              z_o[(m_abs - NKI) * 128:(m_abs - NKI + 1) * 128, xoff: xoff + PT],
                              z16[:, zslot * PT:(zslot + 1) * PT])
                        ldone[("z", zslot)] = p.close(f"oz{zslot}")
                    if mc == 3:
                        wfree[par] = p.mark("tensor")
                ldone["t_cons"] = p.mark("tensor")
            for name in list(p.ch):
                if p.ch[name]["open"]:
                    p.close(name)
            p.drain_all_channels("sync")
            p.barrier()

        if mid_cb is not None:
            mid_cb()
        with contextlib.ExitStack() as e2:
            sb2 = lambda name, shape, dt=F32: e2.enter_context(nc.sbuf_tensor(PF + "p2" + name, shape, dt))[:]
            pst2 = lambda name, shape, dt=F32: e2.enter_context(nc.psum_tensor(PF + name, shape, dt))[:]
            xpT = sb2("xpT", [128, NKI * 160], BF16)
            xcr = sb2("xcr", [128, 2 * 512], BF16)
            xd16 = sb2("xd16", [128, 2 * 512], BF16)
            xd16b = sb2("xd16b", [32, 2 * 512], BF16)
            psD = pst2("p2psD", [128, 512])
            psD2 = pst2("p2psD2", [32, 512])
            for kk in range(NKI):
                p.dma("res2", "sync", xpT[:, kk * 160:(kk + 1) * 160],
                      w_xp_b.ap()[kk * 128:(kk + 1) * 128, :])
            tok_res2 = p.close("res2")
            p.waits("tensor", tok_res2)
            ldone = {}
            for piece in range(TPC // 512):
                xoff = piece * 512
                psl = piece % 2
                for k in range(NKI):
                    ksl = k % 2
                    p.waits("sync", ldone.get(("xcr", ksl)))
                    p.dma("xcl", "sync", xcr[:, ksl * 512:(ksl + 1) * 512],
                          xc_o[k * 128:(k + 1) * 128, xoff: xoff + 512])
                    tok_xc = p.close("xcl")
                    p.waits("tensor", tok_xc, ldone.get("evac"))
                    p.run("tensor", lambda tt, o=psD[:, 0:512], w=xpT[:, k * 160: k * 160 + 128],
                          r=xcr[:, ksl * 512:(ksl + 1) * 512], st=(k == 0), e_=(k == NKI - 1):
                          tt.matmul(o, w, r, start=st, stop=e_))
                    p.run("tensor", lambda tt, o=psD2[:, 0:512], w=xpT[:, k * 160 + 128: k * 160 + 160],
                          r=xcr[:, ksl * 512:(ksl + 1) * 512], st=(k == 0), e_=(k == NKI - 1):
                          tt.matmul(o, w, r, start=st, stop=e_))
                    ldone[("xcr", ksl)] = p.mark("tensor")
                tok_mm = ldone[("xcr", (NKI - 1) % 2)]
                p.waits("scalar", tok_mm, ldone.get(("xd", psl)))
                p.run("scalar", lambda sc, o=xd16[:, psl * 512:(psl + 1) * 512], i=psD:
                      sc.copy(o, i[:, 0:512]))
                tok_xd = p.run("scalar", lambda sc, o=xd16b[:, psl * 512:(psl + 1) * 512], i=psD2:
                               sc.copy(o, i[:, 0:512]), inc=True)
                ldone["evac"] = tok_xd
                p.waits("sync", tok_xd)
                p.dma(f"oxd{psl}", "sync", xd_o[0:128, xoff: xoff + 512],
                      xd16[:, psl * 512:(psl + 1) * 512])
                p.dma(f"oxd{psl}", "sync", xd_o[128:160, xoff: xoff + 512],
                      xd16b[:, psl * 512:(psl + 1) * 512])
                ldone[("xd", psl)] = p.close(f"oxd{psl}")

        if finalize:
            for name in list(p.ch):
                if p.ch[name]["open"]:
                    p.close(name)
            p.drain_all_channels("sync")
            block = es.enter_context(nc.Block())
            p.finalize(block)
    return nc



def build_scan(nc=None, p=None, io=None, finalize=True):
    if nc is None:
        nc = bass.Bass("TRN2", target_bir_lowering=False, debug=False)
    dram = nc.dram_tensor
    if io is None:
        io = {}
        xc_in = dram("xc_in", [CPC, TOK], BF16, kind="ExternalInput").ap()
        z_in = dram("z_in", [CPC, TOK], BF16, kind="ExternalInput").ap()
        xd_in = dram("xd_in", [160, TOK], BF16, kind="ExternalInput").ap()
        io["wdt_j"] = dram("wdt_j", [DTR, CPC], F32, kind="ExternalInput").ap()
        io["db_j"] = dram("db_j", [128, CPC // 128], F32, kind="ExternalInput").ap()
        io["alog_j"] = dram("alog_j", [CPC, DS], F32, kind="ExternalInput").ap()
        io["dsk_j"] = dram("dsk_j", [128, CPC // 128], F32, kind="ExternalInput").ap()
        io["identb_j"] = dram("identb_j", [128, 128], BF16, kind="ExternalInput").ap()
        yg_ = dram("yg_o", [CPC, TOK], BF16, kind="ExternalOutput").ap()
        io["xc_ap"] = lambda r0, nr, co, ln: xc_in[r0:r0 + nr, co:co + ln]
        io["z_ap"] = lambda r0, nr, co, ln: z_in[r0:r0 + nr, co:co + ln]
        io["xd_ap"] = lambda r0, nr, co, ln: xd_in[r0:r0 + nr, co:co + ln]
        io["yg_ap"] = lambda r0, nr, co, ln: yg_[r0:r0 + nr, co:co + ln]
    wdt_j, db_j, alog_j, dsk_j = io["wdt_j"], io["db_j"], io["alog_j"], io["dsk_j"]
    identb_j = io["identb_j"]
    xc_ap, z_ap, xd_ap, yg_ap = io["xc_ap"], io["z_ap"], io["xd_ap"], io["yg_ap"]

    ND = CPC // 128
    NCT = SCAN_CTS
    RESET_CT = L_ // LC
    PF = "s_"
    es = contextlib.ExitStack()
    with es:
        sb = lambda name, shape, dt=F32: es.enter_context(nc.sbuf_tensor(PF + name, shape, dt))[:]
        pst = lambda name, shape, dt=F32: es.enter_context(nc.psum_tensor(PF + name, shape, dt))[:]
        wdt16 = sb("wdt16", [128, CPC], BF16)
        atmp = sb("atmp", [128, ND * DS])
        acol = sb("acol", [128, ND * DS])
        db = sb("db", [128, ND])
        dsk = sb("dsk", [128, ND])
        dtr16 = sb("dtr16", [128, 2 * LC], BF16)
        Bs = sb("Bs", [128, 3 * LC], BF16)
        Cs = sb("Cs", [128, 3 * LC], BF16)
        xcd = sb("xcd", [128, 2 * ND * LC], BF16)
        zd = sb("zd", [128, 2 * ND * LC], BF16)
        dt32 = sb("dt32", [128, ND * LC])
        dtx = sb("dtx", [128, ND * LC], BF16)
        Pt = sb("Pt", [128, ND * LC])
        Qt = sb("Qt", [128, ND * LC])
        Rt = sb("Rt", [128, ND * LC])
        dA = [sb(f"dA{i}", [128, LC]) for i in range(3)]
        u_all = [sb(f"u_all{i}", [128, ND * LC], BF16) for i in range(2)]
        h8 = sb("h8", [128, 8 * LC], BF16)
        prod_all = sb("prod_all", [128, ND * LC], BF16)
        prod2 = sb("prod2", [128, ND * LC], BF16)
        yacc = [sb(f"yacc{i}", [128, LC], BF16) for i in range(2)]
        identb = sb("identb", [128, 128], BF16)
        t1g = sb("t1g", [128, 2 * LC], BF16)
        szg = sb("szg", [128, 2 * LC], BF16)
        ygd = sb("ygd", [128, ND * LC], BF16)
        state_all = sb("state_all", [128, DS * ND])
        psdt = [pst(f"psdt{i}", [128, 512]) for i in range(2)]
        psy = [pst(f"psy{i}", [128, LC]) for i in range(3)]

        p = p if p is not None else Prog(nc, es)
        p.dma("prep", "gpsimd", wdt16, wdt_j)
        tok_prep = p.close("prep")
        for d in range(ND):
            p.dma("res", "sync", atmp[:, d * DS:(d + 1) * DS],
                  alog_j[d * 128:(d + 1) * 128, :])
        p.dma("res", "sync", db, db_j)
        p.dma("res", "sync", dsk, dsk_j)
        p.dma("res", "sync", identb, identb_j)
        tok_res = p.close("res")
        p.waits("scalar", tok_res)
        tok_e = p.run("scalar", lambda sc: sc.activation(acol, atmp, AF.Exp), inc=True)
        p.waits("vector", tok_e)
        tok_a = p.run("vector", lambda v: v.tensor_scalar(atmp, acol, -1.0, None, AL.mult), inc=True)
        for e in ENGS:
            p.waits(e, tok_prep, tok_res, tok_a)
        p.barrier()

        prev_cons = {}
        slotS = [None, None, None]
        hcons = [None, None]
        bcons = [None, None, None]
        ucons = [None, None]
        pcons = [None, None]
        ygate_read = [None] * 3
        oy_tok = [None] * ND
        for ct in range(NCT):
            co = ct * LC
            pc = ct % 2
            dsl = lambda d: slice(d * LC, (d + 1) * LC)
            xsl = lambda d: slice((pc * ND + d) * LC, (pc * ND + d + 1) * LC)
            for tk in (prev_cons.get(("ld", pc)) or []):
                p.waits("sync", tk)
            p.dma(f"dtr{pc}", "sync", dtr16[:, pc * LC:(pc + 1) * LC], xd_ap(0, DTR, co, LC))
            tok_dtr = p.close(f"dtr{pc}")
            for d in range(ND):
                p.dma(f"xcz{pc}", "sync", xcd[:, xsl(d)], xc_ap(d * 128, 128, co, LC))
                p.dma(f"xcz{pc}", "sync", zd[:, xsl(d)], z_ap(d * 128, 128, co, LC))
            tok_xcz = p.close(f"xcz{pc}")

            p.waits("tensor", tok_dtr, prev_cons.get("dtpre"))
            for d in range(ND):
                for h in range(LC // 512):
                    bank = (d * (LC // 512) + h) % 2
                    p.waits("tensor", prev_cons.get(("psdt", bank)))
                    t_ = p.run("tensor", lambda tt, o=psdt[bank][:, 0:512],
                               w=wdt16[:, d * 128:(d + 1) * 128],
                               r=dtr16[:, pc * LC + h * 512: pc * LC + (h + 1) * 512]:
                               tt.matmul(o, w, r, start=True, stop=True), inc=True)
                    p.waits("scalar", t_)
                    s1 = p.run("scalar", lambda sc, o=Pt[:, d * LC + h * 512: d * LC + (h + 1) * 512],
                               i=psdt[bank][:, 0:512],
                               b=db[:, d:d + 1]: sc.add(o, i, b), inc=True)
                    prev_cons[("psdt", bank)] = s1
            for d in range(ND):
                p.run("scalar", lambda sc, o=Qt[:, dsl(d)], i=Pt[:, dsl(d)]:
                      sc.activation(o, i, AF.Abs))
            t2_ = p.mark("scalar")
            p.waits("vector", t2_)
            for d in range(ND):
                p.run("vector", lambda v, o=Rt[:, dsl(d)], i=Qt[:, dsl(d)]:
                      v.tensor_scalar(o, i, -1.0, None, AL.mult))
            t3_ = p.mark("vector")
            p.waits("scalar", t3_)
            for d in range(ND):
                p.run("scalar", lambda sc, o=Qt[:, dsl(d)], i=Rt[:, dsl(d)]:
                      sc.activation(o, i, AF.Exp))
            t4_ = p.mark("scalar")
            p.waits("vector", t4_)
            for d in range(ND):
                p.run("vector", lambda v, o=Rt[:, dsl(d)], i=Qt[:, dsl(d)]:
                      v.tensor_scalar(o, i, 1.0, None, AL.add))
            t5_ = p.mark("vector")
            p.waits("scalar", t5_)
            for d in range(ND):
                p.run("scalar", lambda sc, o=Qt[:, dsl(d)], i=Rt[:, dsl(d)]:
                      sc.activation(o, i, AF.Ln))
            t6_ = p.mark("scalar")
            p.waits("vector", t6_)
            for d in range(ND):
                p.run("vector", lambda v, o=Rt[:, dsl(d)], i=Pt[:, dsl(d)]:
                      v.tensor_scalar(o, i, 0.0, None, AL.max))
            for d in range(ND):
                p.run("vector", lambda v, o=dt32[:, dsl(d)], a=Qt[:, dsl(d)], b=Rt[:, dsl(d)]:
                      v.tensor_tensor(o, a, b, AL.add))
            p.waits("vector", tok_xcz)
            for d in range(ND):
                p.run("vector", lambda v, o=dtx[:, dsl(d)], a=dt32[:, dsl(d)], b=xcd[:, xsl(d)]:
                      v.tensor_tensor(o, a, b, AL.mult))
            tok_dt = p.mark("vector")
            prev_cons["dtpre"] = tok_dt
            p.waits("scalar", tok_dt)

            for s_ in range(DS):
                bs = s_ % 3
                usl = s_ % 2
                ysl = s_ % 2
                hbase = (s_ % 2) * ND
                p.waits("sync", bcons[bs])
                p.dma(f"bc{bs}", "sync", Bs[:, bs * LC:(bs + 1) * LC],
                      xd_ap(DTR + s_, 1, co, LC).partition_broadcast(128))
                p.dma(f"bc{bs}", "sync", Cs[:, bs * LC:(bs + 1) * LC],
                      xd_ap(DTR + DS + s_, 1, co, LC).partition_broadcast(128))
                tok_bc = p.close(f"bc{bs}")
                p.waits("vector", ucons[usl], tok_bc)
                tu = p.run("vector", lambda v, o=u_all[usl], a=dtx,
                           b=Bs[:, bs * LC:(bs + 1) * LC].unsqueeze(1).broadcast_to([128, ND, LC]):
                           v.tensor_tensor(o.rearrange("p (n l) -> p n l", l=LC), a.rearrange("p (n l) -> p n l", l=LC), b, AL.mult), inc=True)
                vlast = None
                for d in range(ND):
                    step = s_ * ND + d
                    sl = step % 3
                    hsl = hbase + d
                    p.waits("scalar", slotS[sl])
                    tda = p.run("scalar", lambda sc, o=dA[sl], i=dt32[:, dsl(d)],
                                a=atmp[:, d * DS + s_: d * DS + s_ + 1]:
                                sc.activation(o, i, AF.Exp, scale=a), inc=True)
                    if ct == 0 or ct == RESET_CT:
                        init = 0.0
                    else:
                        init = state_all[:, s_ * ND + d: s_ * ND + d + 1]
                    p.waits("vector", tda, tu, hcons[s_ % 2])
                    vlast = p.run("vector", lambda v, o=h8[:, hsl * LC:(hsl + 1) * LC],
                                  a=dA[sl], b=u_all[usl][:, d * LC:(d + 1) * LC], i=init:
                                  v.tensor_tensor_scan(o, a, b, i, AL.mult, AL.add), inc=True)
                    slotS[sl] = vlast
                pp = s_ % 2
                prodb = prod_all if pp == 0 else prod2
                p.waits("vector", pcons[pp])
                tpm = p.run("vector", lambda v, o=prodb,
                            a=h8[:, hbase * LC: (hbase + ND) * LC],
                            c=Cs[:, bs * LC:(bs + 1) * LC].unsqueeze(1).broadcast_to([128, ND, LC]):
                            v.tensor_tensor(o.rearrange("p (n l) -> p n l", l=LC),
                                            a.rearrange("p (n l) -> p n l", l=LC), c, AL.mult), inc=True)
                ucons[usl] = tpm
                bcons[bs] = tpm
                p.waits("tensor", tpm)
                if s_ == 0:
                    p.waits("tensor", ygate_read[0], ygate_read[1], ygate_read[2])
                for d in range(3):
                    for hh in range(LC // 512):
                        tmm = p.run("tensor", lambda tt, o=psy[d][:, hh * 512:(hh + 1) * 512],
                                    w=identb,
                                    r=prodb[:, d * LC + hh * 512: d * LC + (hh + 1) * 512],
                                    st=(s_ == 0), e_=(s_ == DS - 1):
                                    tt.matmul(o, w, r, start=st, stop=e_),
                                    inc=(d == 2 and hh == LC // 512 - 1))
                pcons[pp] = tmm
                if s_ == 0:
                    p.run("vector", lambda v, o=yacc[ysl], i=prodb[:, 3 * LC: 4 * LC]:
                          v.tensor_copy(o, i))
                else:
                    p.run("vector", lambda v, o=yacc[ysl], a=yacc[1 - ysl],
                          b=prodb[:, 3 * LC: 4 * LC]:
                          v.tensor_tensor(o, a, b, AL.add))
                p.waits("scalar", vlast)
                tcap = p.run("scalar", lambda sc, o=state_all[:, s_ * ND: s_ * ND + ND],
                             i=h8.rearrange("p (n l) -> p n l", l=LC)[:, hbase: hbase + ND, LC - 1]:
                             sc.copy(o, i), inc=True)
                hcons[s_ % 2] = tcap

            for d in range(ND):
                zsl = d % 2
                p.waits("scalar", prev_cons.get(("sz", zsl)))
                tsz = p.run("scalar", lambda sc, o=szg[:, zsl * LC:(zsl + 1) * LC],
                            i=zd[:, xsl(d)]: sc.activation(o, i, AF.Silu), inc=True)
                ysrc = psy[d] if d < 3 else yacc[(DS - 1) % 2]
                p.waits("vector", tsz, oy_tok[d], pcons[(DS - 1) % 2])
                tg1 = p.run("vector", lambda v, o=t1g[:, zsl * LC:(zsl + 1) * LC], i=xcd[:, xsl(d)],
                            a=dsk[:, d:d + 1], b=ysrc:
                            v.scalar_tensor_tensor(o, i, a, b, AL.mult, AL.add), inc=(d < 3))
                if d < 3:
                    ygate_read[d] = tg1
                tyg = p.run("vector", lambda v, o=ygd[:, dsl(d)], a=t1g[:, zsl * LC:(zsl + 1) * LC],
                            b=szg[:, zsl * LC:(zsl + 1) * LC]:
                            v.tensor_tensor(o, a, b, AL.mult), inc=True)
                prev_cons[("sz", zsl)] = tyg
                p.waits("sync", tyg)
                p.dma(f"oy{d}", "sync", yg_ap(d * 128, 128, co, LC), ygd[:, dsl(d)])
                oy_tok[d] = p.close(f"oy{d}")
            prev_cons[("ld", pc)] = [p.mark("vector"), p.mark("gpsimd"), p.mark("tensor")]

        if finalize:
            for name in list(p.ch):
                if p.ch[name]["open"]:
                    p.close(name)
            p.drain_all_channels("sync")
            block = es.enter_context(nc.Block())
            p.finalize(block)
    return nc


def build_back(nc=None, p=None, io=None, finalize=True):
    if nc is None:
        nc = bass.Bass("TRN2", target_bir_lowering=False, debug=False)
    dram = nc.dram_tensor
    if io is None:
        io = {}
        yg_in = dram("yg_in", [DI, TPC], BF16, kind="ExternalInput").ap()
        io["ygb_ap"] = lambda r0, nr, xoff, n: yg_in[r0:r0 + nr, xoff:xoff + n]
        for nm, shp in (("x1_in", [D, TPC]), ("u_in", [D, TPC])):
            io[nm] = dram(nm, shp, F32, kind="ExternalInput").ap()
        for nm, shp in (("w_out_T", [DI, D]), ("w_ffp_T", [D, H]), ("w_ffo_T", [DI, D]),
                        ("a_T", [D, E * R]), ("b_T", [E * R, D]), ("wr_T", [D, E]),
                        ("br_r", [1, E]), ("ln2_w_t", [128, NKD]), ("ln2_b_t", [128, NKD]),
                        ("repm_in", [8, 512])):
            io[nm] = dram(nm, shp, F32, kind="ExternalInput").ap()
        io["xo_o"] = dram("xo_o", [D, TPC], F32, kind="ExternalOutput").ap()
    (ygb_ap, x1_in, u_in, w_out_T, w_ffp_T, w_ffo_T, a_T, b_T, wr_T, br_r,
     ln2_w_t, ln2_b_t, repm_in, xo_o) = (
        io["ygb_ap"], io["x1_in"], io["u_in"], io["w_out_T"], io["w_ffp_T"],
        io["w_ffo_T"], io["a_T"], io["b_T"], io["wr_T"], io["br_r"],
        io["ln2_w_t"], io["ln2_b_t"], io["repm_in"], io["xo_o"])

    if "w_out_b" in io:
        (w_out_b, w_ffp_b, w_ffo_b, a_b, b_b, wr_b, br_b) = (
            io["w_out_b"], io["w_ffp_b"], io["w_ffo_b"], io["a_b"], io["b_b"],
            io["wr_b"], io["br_b"])
        prep_here = False
    else:
        w_out_b = dram("k_w_out_b", [DI, D], BF16)
        w_ffp_b = dram("k_w_ffp_b", [D, H], BF16)
        w_ffo_b = dram("k_w_ffo_b", [DI, D], BF16)
        a_b = dram("k_a_b", [D, E * R], BF16)
        b_b = dram("k_b_b", [E * R, D], BF16)
        wr_b = dram("k_wr_b", [D, E], BF16)
        br_b = dram("k_br_b", [1, E], BF16)
        prep_here = True
    v_s = dram("k_v_s", [D, TPC], F32).ap()
    t2_s = dram("k_t2_s", [D, TPC], BF16).ap()
    ff_s = dram("k_ff_s", [DI, TPC], BF16).ap()
    x2_s = dram("k_x2_s", [D, TPC], F32).ap()

    PF = "k_"
    es = contextlib.ExitStack()
    with es:
        sb = lambda name, shape, dt=F32: es.enter_context(nc.sbuf_tensor(PF + name, shape, dt))[:]
        pst = lambda name, shape, dt=F32: es.enter_context(nc.psum_tensor(PF + name, shape, dt))[:]
        if p is None:
            p = Prog(nc, es)
        if prep_here:
            p.dma("prep", "gpsimd", w_out_b.ap(), w_out_T)
            p.dma("prep", "gpsimd", w_ffp_b.ap(), w_ffp_T)
            p.dma("prep", "gpsimd", w_ffo_b.ap(), w_ffo_T)
            p.dma("prep", "gpsimd", a_b.ap(), a_T)
            p.dma("prep", "gpsimd", b_b.ap(), b_T)
            p.dma("prep", "gpsimd", wr_b.ap(), wr_T)
            p.dma("prep", "gpsimd", br_b.ap(), br_r)
            tok_prep = p.close("prep")
        else:
            tok_prep = io["prepk_tok"]

        PT1 = 1024
        with contextlib.ExitStack() as e1:
            sb1 = lambda name, shape, dt=F32: e1.enter_context(nc.sbuf_tensor(PF + "p1" + name, shape, dt))[:]
            pst1 = lambda name, shape, dt=F32: e1.enter_context(nc.psum_tensor(PF + name, shape, dt))[:]
            ygt = sb1("ygt", [128, NKI * PT1], BF16)
            ut = sb1("ut", [128, 2 * PT1])
            vt = sb1("vt", [128, 2 * PT1])
            vb = sb1("vb", [128, 2 * PT1], BF16)
            vrb = sb1("vrb", [128, 2 * 512])
            sqb = sb1("sqb", [128, 2 * PT1], BF16)
            t2t = sb1("t2t", [128, 2 * PT1], BF16)
            tmpA = sb1("tmpA", [128, PT1])
            tmpB = sb1("tmpB", [128, PT1])
            wbuf = [sb1(f"wbuf{b}", [128, NKI * 512], BF16) for b in range(2)]
            stats = {nm: sb1(nm, [1, PT1]) for nm in
                     ("m", "q", "var", "sd", "r", "nmr", "t1", "t2x")}
            ones_c = e1.enter_context(nc.sbuf_tensor(PF + "p1ones_c", [128, 1], BF16))
            ones_r = e1.enter_context(nc.sbuf_tensor(PF + "p1ones_r", [1, 128], F32))
            ln2w = sb1("ln2w", [128, NKD])
            ln2b = sb1("ln2b", [128, NKD])
            psMM = [pst1("p1psMM", [128, 512]), pst1("p1psMM2", [128, 512])]
            psA = pst1("p1psA", [1, 1024])
            psB = pst1("p1psB", [1, 1024])
            psR = pst1("p1psR", [128, 512])
            psN = pst1("p1psN", [128, 512])
            tok_c = p.run("vector", lambda v: v.memset(ones_c[:], 1.0), inc=True)
            tok_c2 = p.run("vector", lambda v: v.memset(ones_r[:], 1.0), inc=True)
            p.dma("res", "sync", ln2w, ln2_w_t)
            p.dma("res", "sync", ln2b, ln2_b_t)
            tok_res = p.close("res")
            for e in ENGS:
                p.waits(e, tok_prep, tok_res, tok_c, tok_c2)
            wfree = [None, None]
            evac = [None, None]
            ldone = {}
            for t in range(BACK_SPAN // PT1):
                xoff = t * PT1
                p.waits("sync", ldone.get("yg_cons"))
                for k in range(NKI):
                    p.dma("yg", "sync", ygt[:, k * PT1:(k + 1) * PT1],
                          ygb_ap(k * 128, 128, xoff, PT1))
                tok_yg = p.close("yg")
                p.waits("tensor", tok_yg)
                for m in range(NKD):
                    par = m % 2
                    g, mc = m // 4, m % 4
                    if mc == 0:
                        p.waits("sync", wfree[par_g := (g % 2)])
                        for k in range(NKI):
                            p.dma(f"w{par_g}", "sync", wbuf[par_g][:, k * 512:(k + 1) * 512],
                                  w_out_b.ap()[k * 128:(k + 1) * 128, g * 512:(g + 1) * 512])
                        tok_w = p.close(f"w{par_g}")
                        p.waits("tensor", tok_w)
                    usl = m % 2
                    for _tk in (ldone.get(("u", usl)) or []):
                        p.waits("sync", _tk)
                    p.dma("u", "sync", ut[:, usl * PT1:(usl + 1) * PT1],
                          u_in[m * 128:(m + 1) * 128, xoff: xoff + PT1])
                    tok_u = p.close("u")
                    vsl = m % 2
                    for h in range(PT1 // 512):
                        bank = (m * (PT1 // 512) + h) % 2
                        ps = psMM[bank]
                        p.waits("tensor", evac[bank])
                        for k in range(NKI):
                            p.run("tensor", lambda tt, o=ps[:, 0:512],
                                  w=wbuf[g % 2][:, k * 512 + (m % 4) * 128: k * 512 + (m % 4) * 128 + 128],
                                  r=ygt[:, k * PT1 + h * 512: k * PT1 + (h + 1) * 512],
                                  st=(k == 0), e_=(k == NKI - 1):
                                  tt.matmul(o, w, r, start=st, stop=e_))
                        tmm = p.mark("tensor")
                        p.waits("vector", tmm, tok_u)
                        tv = p.run("vector", lambda v, o=vt[:, vsl * PT1 + h * 512: vsl * PT1 + (h + 1) * 512],
                                   a=ut[:, usl * PT1 + h * 512: usl * PT1 + (h + 1) * 512],
                                   b=ps[:, 0:512]: v.tensor_tensor(o, a, b, AL.add), inc=True)
                        evac[bank] = tv
                    p.waits("sync", tv)
                    p.dma("v", "sync", v_s[m * 128:(m + 1) * 128, xoff: xoff + PT1],
                          vt[:, vsl * PT1:(vsl + 1) * PT1])
                    vstok = p.close("v")
                    ldone[("u", usl)] = [tv, vstok]
                    p.waits("scalar", tv)
                    tvb = p.run("scalar", lambda sc, o=vb[:, vsl * PT1:(vsl + 1) * PT1],
                                i=vt[:, vsl * PT1:(vsl + 1) * PT1]: sc.copy(o, i), inc=True)
                    tsq = p.run("scalar", lambda sc, o=sqb[:, vsl * PT1:(vsl + 1) * PT1],
                                i=vt[:, vsl * PT1:(vsl + 1) * PT1]: sc.activation(o, i, AF.Square), inc=True)
                    p.waits("tensor", tvb, tsq)
                    for h2 in range(PT1 // 512):
                        p.run("tensor", lambda tt, o=psA[0:1, h2 * 512:(h2 + 1) * 512], w=ones_c[:],
                              r=vb[:, vsl * PT1 + h2 * 512: vsl * PT1 + (h2 + 1) * 512],
                              st=(m == 0), e_=(m == NKD - 1):
                              tt.matmul(o, w, r, start=st, stop=e_, skip_group_check=True))
                        p.run("tensor", lambda tt, o=psB[0:1, h2 * 512:(h2 + 1) * 512], w=ones_c[:],
                              r=sqb[:, vsl * PT1 + h2 * 512: vsl * PT1 + (h2 + 1) * 512],
                              st=(m == 0), e_=(m == NKD - 1):
                              tt.matmul(o, w, r, start=st, stop=e_, skip_group_check=True))
                    ldone[("v", vsl)] = p.mark("tensor")
                    if mc == 3:
                        wfree[g % 2] = ldone[("v", vsl)]
                    if m == NKD - 1:
                        ldone["yg_cons"] = ldone[("v", vsl)]
                tok_mm = p.mark("tensor")
                st_ = stats
                p.waits("vector", tok_mm)
                p.run("vector", lambda v: v.tensor_scalar(st_["m"][:, 0:PT1], psA[0:1, 0:PT1], 1.0 / D, None, AL.mult))
                p.run("vector", lambda v: v.tensor_scalar(st_["q"][:, 0:PT1], psB[0:1, 0:PT1], 1.0 / D, None, AL.mult))
                p.run("vector", lambda v: v.tensor_tensor(st_["t1"][:, 0:PT1], st_["m"][:, 0:PT1], st_["m"][:, 0:PT1], AL.mult))
                p.run("vector", lambda v: v.tensor_tensor(st_["var"][:, 0:PT1], st_["q"][:, 0:PT1], st_["t1"][:, 0:PT1], AL.subtract))
                tv1 = p.run("vector", lambda v: v.tensor_scalar(st_["t2x"][:, 0:PT1], st_["var"][:, 0:PT1], EPS, None, AL.add), inc=True)
                p.waits("scalar", tv1)
                ts1 = p.run("scalar", lambda sc: sc.activation(st_["sd"][:, 0:PT1], st_["t2x"][:, 0:PT1], AF.Sqrt), inc=True)
                p.waits("vector", ts1)
                p.run("vector", lambda v: v.reciprocal(st_["r"][:, 0:PT1], st_["sd"][:, 0:PT1]))
                p.run("vector", lambda v: v.tensor_tensor(st_["t1"][:, 0:PT1], st_["m"][:, 0:PT1], st_["r"][:, 0:PT1], AL.mult))
                tv2 = p.run("vector", lambda v: v.tensor_scalar(st_["nmr"][:, 0:PT1], st_["t1"][:, 0:PT1], -1.0, None, AL.mult), inc=True)
                p.waits("tensor", tv2)
                p.waits("sync", vstok)
                for h in range(PT1 // 512):
                    p.run("tensor", lambda tt, o=psR[:, 0:512], r=st_["r"][0:1, h * 512:(h + 1) * 512]:
                          tt.matmul(o, ones_r[:], r, start=True, stop=True))
                    p.run("tensor", lambda tt, o=psN[:, 0:512], r=st_["nmr"][0:1, h * 512:(h + 1) * 512]:
                          tt.matmul(o, ones_r[:], r, start=True, stop=True))
                    trp = p.mark("tensor")
                    for m in range(NKD):
                        vsl2 = m % 2
                        p.waits("sync", ldone.get(("vr", vsl2)))
                        p.dma("vr", "sync", vrb[:, vsl2 * 512:(vsl2 + 1) * 512],
                              v_s[m * 128:(m + 1) * 128, xoff + h * 512: xoff + (h + 1) * 512])
                        tok_vr = p.close("vr")
                        p.waits("vector", trp, tok_vr, ldone.get(("t2s", vsl2)))
                        p.run("vector", lambda v, o=tmpA[:, 0:512], a=vrb[:, vsl2 * 512:(vsl2 + 1) * 512],
                              b=psR[:, 0:512]: v.tensor_tensor(o, a, b, AL.mult))
                        p.run("vector", lambda v, o=tmpB[:, 0:512], a=tmpA[:, 0:512],
                              b=psN[:, 0:512]: v.tensor_tensor(o, a, b, AL.add))
                        tt2 = p.run("vector", lambda v, o=t2t[:, vsl2 * 512:(vsl2 + 1) * 512], i=tmpB[:, 0:512],
                                    w=ln2w[:, m:m + 1], b=ln2b[:, m:m + 1]:
                                    v.tensor_scalar(o, i, w, b, AL.mult, AL.add), inc=True)
                        ldone[("vr", vsl2)] = tt2
                        p.waits("sync", tt2)
                        p.dma(f"t2{vsl2}", "sync",
                              t2_s[m * 128:(m + 1) * 128, xoff + h * 512: xoff + (h + 1) * 512],
                              t2t[:, vsl2 * 512:(vsl2 + 1) * 512])
                        ldone[("t2s", vsl2)] = p.close(f"t2{vsl2}")
                    p.waits("tensor", p.mark("vector"))
        for name in list(p.ch):
            if p.ch[name]["open"]:
                p.close(name)
        p.drain_all_channels("sync")
        p.barrier()

        PT2 = 2048
        with contextlib.ExitStack() as e2:
            sb2 = lambda name, shape, dt=F32: e2.enter_context(nc.sbuf_tensor(PF + "p2" + name, shape, dt))[:]
            pst2 = lambda name, shape, dt=F32: e2.enter_context(nc.psum_tensor(PF + name, shape, dt))[:]
            t2r = sb2("t2r", [128, NKD * PT2], BF16)
            a16 = sb2("a16", [128, 2 * 512], BF16)
            sg = sb2("sg", [128, 2 * 512], BF16)
            ff16 = sb2("ff16", [128, 2 * 512], BF16)
            wbufa = [sb2(f"wbufa{b}", [128, NKD * 512], BF16) for b in range(2)]
            wbufg = [sb2(f"wbufg{b}", [128, NKD * 512], BF16) for b in range(2)]
            psMM = [pst2("p2psMM", [128, 512]), pst2("p2psMM2", [128, 512])]
            evac = [None, None]
            wfree = [None, None, None, None]
            ldone = {}
            for t in range(BACK_SPAN // PT2):
                xoff = t * PT2
                for k in range(NKD):
                    p.waits("sync", ldone.get(("t2", k)))
                    p.dma("t2r", "sync", t2r[:, k * PT2:(k + 1) * PT2],
                          t2_s[k * 128:(k + 1) * 128, xoff: xoff + PT2])
                tok_t2 = p.close("t2r")
                for m in range(NKI):
                    gw, mw = m // 4, m % 4
                    par = gw % 2
                    if mw == 0:
                        p.waits("sync", wfree[par])
                        for k in range(NKD):
                            p.dma(f"wa{par}", "sync", wbufa[par][:, k * 512:(k + 1) * 512],
                                  w_ffp_b.ap()[k * 128:(k + 1) * 128, gw * 512:(gw + 1) * 512])
                        for k in range(NKD):
                            p.dma(f"wa{par}", "sync", wbufg[par][:, k * 512:(k + 1) * 512],
                                  w_ffp_b.ap()[k * 128:(k + 1) * 128, DI + gw * 512: DI + (gw + 1) * 512])
                        tok_w = p.close(f"wa{par}")
                    p.waits("tensor", tok_w, tok_t2)
                    for h in range(PT2 // 512):
                        bank = h % 2
                        psa = psMM[bank]
                        psg = psMM[1 - bank]
                        p.waits("tensor", evac[0], evac[1])
                        for k in range(NKD):
                            p.run("tensor", lambda tt, o=psa[:, 0:512],
                                  w=wbufa[par][:, k * 512 + mw * 128: k * 512 + mw * 128 + 128],
                                  r=t2r[:, k * PT2 + h * 512: k * PT2 + (h + 1) * 512],
                                  st=(k == 0), e_=(k == NKD - 1): tt.matmul(o, w, r, start=st, stop=e_))
                        tma = p.mark("tensor")
                        for k in range(NKD):
                            p.run("tensor", lambda tt, o=psg[:, 0:512],
                                  w=wbufg[par][:, k * 512 + mw * 128: k * 512 + mw * 128 + 128],
                                  r=t2r[:, k * PT2 + h * 512: k * PT2 + (h + 1) * 512],
                                  st=(k == 0), e_=(k == NKD - 1): tt.matmul(o, w, r, start=st, stop=e_))
                        tmg = p.mark("tensor")
                        hsl = h % 2
                        p.waits("scalar", tma, tmg, ldone.get(("ff", hsl)))
                        ta = p.run("scalar", lambda sc, o=a16[:, hsl * 512:(hsl + 1) * 512], i=psa:
                                   sc.copy(o, i[:, 0:512]), inc=True)
                        tg = p.run("scalar", lambda sc, o=sg[:, hsl * 512:(hsl + 1) * 512], i=psg:
                                   sc.activation(o, i[:, 0:512], AF.Silu), inc=True)
                        evac[0] = ta
                        evac[1] = tg
                        p.waits("vector", tg)
                        tf = p.run("vector", lambda v, o=ff16[:, hsl * 512:(hsl + 1) * 512],
                                   a=a16[:, hsl * 512:(hsl + 1) * 512], b=sg[:, hsl * 512:(hsl + 1) * 512]:
                                   v.tensor_tensor(o, a, b, AL.mult), inc=True)
                        p.waits("sync", tf)
                        p.dma(f"ff{hsl}", "sync", ff_s[m * 128:(m + 1) * 128, xoff + h * 512: xoff + (h + 1) * 512],
                              ff16[:, hsl * 512:(hsl + 1) * 512])
                        ldone[("ff", hsl)] = p.close(f"ff{hsl}")
                    if mw == 3:
                        wfree[par] = p.mark("tensor")
        for name in list(p.ch):
            if p.ch[name]["open"]:
                p.close(name)
        p.drain_all_channels("sync")
        p.barrier()

        PT3 = 1024
        with contextlib.ExitStack() as e3:
            sb3 = lambda name, shape, dt=F32: e3.enter_context(nc.sbuf_tensor(PF + "p3" + name, shape, dt))[:]
            pst3 = lambda name, shape, dt=F32: e3.enter_context(nc.psum_tensor(PF + name, shape, dt))[:]
            ffr = sb3("ffr", [128, NKI * PT3], BF16)
            x1t = sb3("x1t", [128, 2 * PT3])
            vtt = sb3("vtt", [128, 2 * PT3])
            e32 = sb3("e32", [128, 2 * PT3])
            x2t = sb3("x2t", [128, 2 * PT3])
            wbuf = [sb3(f"wbuf{b}", [128, NKI * 512], BF16) for b in range(2)]
            psMM = [pst3("p3psMM", [128, 512]), pst3("p3psMM2", [128, 512])]
            evac = [None, None]
            wfree = [None, None]
            ldone = {}
            for t in range(BACK_SPAN // PT3):
                xoff = t * PT3
                for k in range(NKI):
                    p.waits("sync", ldone.get(("ff", k)))
                    p.dma("ffr", "sync", ffr[:, k * PT3:(k + 1) * PT3],
                          ff_s[k * 128:(k + 1) * 128, xoff: xoff + PT3])
                tok_ff = p.close("ffr")
                for m in range(NKD):
                    gw, mw = m // 4, m % 4
                    par = gw % 2
                    if mw == 0:
                        p.waits("sync", wfree[par])
                        for k in range(NKI):
                            p.dma(f"w{par}", "sync", wbuf[par][:, k * 512:(k + 1) * 512],
                                  w_ffo_b.ap()[k * 128:(k + 1) * 128, gw * 512:(gw + 1) * 512])
                        tok_w = p.close(f"w{par}")
                    msl = m % 2
                    p.waits("sync", ldone.get(("x2", msl)))
                    p.dma("x1l", "sync", x1t[:, msl * PT3:(msl + 1) * PT3],
                          x1_in[m * 128:(m + 1) * 128, xoff: xoff + PT3])
                    p.dma("x1l", "sync", vtt[:, msl * PT3:(msl + 1) * PT3],
                          v_s[m * 128:(m + 1) * 128, xoff: xoff + PT3])
                    tok_x1 = p.close("x1l")
                    p.waits("tensor", tok_w, tok_ff)
                    tx2 = None
                    for h in range(PT3 // 512):
                        bank = h % 2
                        ps = psMM[bank]
                        p.waits("tensor", evac[bank])
                        for k in range(NKI):
                            p.run("tensor", lambda tt, o=ps[:, 0:512],
                                  w=wbuf[par][:, k * 512 + mw * 128: k * 512 + mw * 128 + 128],
                                  r=ffr[:, k * PT3 + h * 512: k * PT3 + (h + 1) * 512],
                                  st=(k == 0), e_=(k == NKI - 1): tt.matmul(o, w, r, start=st, stop=e_))
                        tmm = p.mark("tensor")
                        p.waits("vector", tmm, tok_x1)
                        p.run("vector", lambda v, o=e32[:, msl * PT3 + h * 512: msl * PT3 + (h + 1) * 512],
                              a=vtt[:, msl * PT3 + h * 512: msl * PT3 + (h + 1) * 512],
                              b=ps[:, 0:512]: v.tensor_tensor(o, a, b[:, 0:512], AL.add))
                        tx2 = p.run("vector", lambda v, o=x2t[:, msl * PT3 + h * 512: msl * PT3 + (h + 1) * 512],
                                    a=e32[:, msl * PT3 + h * 512: msl * PT3 + (h + 1) * 512],
                                    b=x1t[:, msl * PT3 + h * 512: msl * PT3 + (h + 1) * 512]:
                                    v.tensor_tensor(o, a, b, AL.add), inc=True)
                        evac[bank] = tx2
                    p.waits("sync", tx2)
                    p.dma(f"x2{msl}", "sync", x2_s[m * 128:(m + 1) * 128, xoff: xoff + PT3],
                          x2t[:, msl * PT3:(msl + 1) * PT3])
                    ldone[("x2", msl)] = p.close(f"x2{msl}")
                    if mw == 3:
                        wfree[par] = p.mark("tensor")
        for name in list(p.ch):
            if p.ch[name]["open"]:
                p.close(name)
        p.drain_all_channels("sync")
        p.barrier()

        N = 512
        with contextlib.ExitStack() as e4:
            sb4 = lambda name, shape, dt=F32: e4.enter_context(nc.sbuf_tensor(PF + "p4" + name, shape, dt))[:]
            pst4 = lambda name, shape, dt=F32: e4.enter_context(nc.psum_tensor(PF + name, shape, dt))[:]
            al = {}
            al["xbt"] = sb4("xbt", [128, 2 * N], BF16)
            al["sqt"] = sb4("sqt", [128, 2 * N], BF16)
            al["ln_n"] = N
            al["tmp32"] = sb4("tmp32", [128, N])
            al["tmp32b"] = sb4("tmp32b", [128, N])
            for nm in ("st_m", "st_q", "st_var", "st_sd", "st_r", "st_nmr", "st_t1", "st_t2"):
                al[nm] = sb4(nm, [1, N])
            al["ones_col16"] = e4.enter_context(nc.sbuf_tensor(PF + "p4ones_c", [128, 1], BF16))
            al["ones_row32"] = e4.enter_context(nc.sbuf_tensor(PF + "p4ones_r", [1, 128], F32))
            al["ones_row16"] = sb4("ones_r16", [1, 128], BF16)
            al["ident"] = e4.enter_context(nc.sbuf_tensor(PF + "p4ident", [128, 128], BF16))
            zeros128 = e4.enter_context(nc.sbuf_tensor(PF + "p4zeros", [128, 128], BF16))
            al["lgs"] = sb4("lgs", [128, 32])
            al["mx"] = sb4("mx", [128, 4])
            al["nmx"] = sb4("nmx", [128, 4])
            al["sm"] = sb4("sm", [128, 4])
            al["wsc"] = sb4("wsc", [128, 4])
            al["w0"] = sb4("w0", [128, 4])
            al["exps"] = sb4("exps", [128, 32])
            al["maskb"] = sb4("maskb", [128, 32], BF16)
            al["maskT"] = sb4("maskT", [8, N], BF16)
            al["repm"] = sb4("repm", [8, 512], BF16)
            al["mrep16"] = sb4("mrep16", [128, N], BF16)
            al["h16"] = sb4("h16", [128, N], BF16)
            al["hm16"] = sb4("hm16", [128, 4 * N], BF16)
            al["ffn_n"] = N
            al["aT"] = sb4("aT", [128, NKD * 4 * 128], BF16)
            al["bT"] = sb4("bT", [128, 4 * NKD * 128], BF16)
            al["wrT"] = sb4("wrT", [128, NKD * 8], BF16)
            al["brT"] = e4.enter_context(nc.sbuf_tensor(PF + "p4brT", [1, 8], BF16))
            al["psA"] = pst4("p4psA", [1, 512])
            al["psB"] = pst4("p4psB", [1, 512])
            al["psR"] = pst4("p4psR", [128, 512])
            al["psN"] = pst4("p4psN", [128, 512])
            al["psS"] = pst4("p4psS", [128, 512])
            al["psRt"] = al["psS"][:, 0:32]
            al["psM"] = al["psR"]
            al["psH"] = al["psN"]
            psX4 = pst4("p4psX", [128, 512])
            al["psT"] = psX4[0:8, :].bitcast(BF16)
            x2b = sb4("x2b", [128, NKD * N])
            v2b = sb4("v2b", [128, NKD * N], BF16)
            xob = sb4("xob", [128, 4 * N])
            tok_z = p.run("vector", lambda v: v.memset(zeros128[:], 0.0), inc=True)
            p.waits("gpsimd", tok_z)
            tok_id = p.run("gpsimd", lambda g: g.affine_select(
                out=al["ident"][:], in_=zeros128[:], compare_op=AL.not_equal,
                fill=1.0, base=0, pattern=[[-1, 128]], channel_multiplier=1), inc=True)
            tok_c1 = p.run("gpsimd", lambda g: g.memset(al["ones_col16"][:], 1.0), inc=True)
            tok_c2 = p.run("gpsimd", lambda g: g.memset(al["ones_row32"][:], 1.0), inc=True)
            tok_c3 = p.run("gpsimd", lambda g: g.memset(al["ones_row16"][:], 1.0), inc=True)
            for k in range(NKD):
                p.dma("res4", "sync", al["aT"][:, k * 512:(k + 1) * 512],
                      a_b.ap()[k * 128:(k + 1) * 128, :])
            for c in range(4):
                p.dma("res4", "sync", al["bT"][:, c * 2048:(c + 1) * 2048],
                      b_b.ap()[c * 128:(c + 1) * 128, :])
            for k in range(NKD):
                p.dma("res4", "sync", al["wrT"][:, k * 8:(k + 1) * 8],
                      wr_b.ap()[k * 128:(k + 1) * 128, :])
            p.dma("res4", "sync", al["brT"][:], br_b.ap())
            tok_res = p.close("res4")
            p.dma("prep4", "gpsimd", al["repm"], repm_in)
            tok_rp = p.close("prep4")
            for e in ENGS:
                p.waits(e, tok_res, tok_rp, tok_id, tok_c1, tok_c2, tok_c3)
            p.barrier()
            oslot = {}
            prev_x2 = None
            for t in range(BACK_SPAN // N):
                xoff = t * N
                x2c = [x2b[:, k * N:(k + 1) * N] for k in range(NKD)]
                v2c = [v2b[:, k * N:(k + 1) * N] for k in range(NKD)]
                xoc = [xob[:, (k % 4) * N:(k % 4 + 1) * N] for k in range(NKD)]
                p.waits("sync", prev_x2)
                for k in range(NKD):
                    p.dma("x2r", "sync", x2c[k], x2_s[k * 128:(k + 1) * 128, xoff: xoff + N])
                tok_x2 = p.close("x2r")
                tok_v2 = emit_ln(p, al, N, x2c, tok_x2, out16c=v2c, tmp32=al["tmp32"])

                def pre_cb(m, oslot=oslot, t=t):
                    return oslot.get((t, m - 4)) or oslot.get((t - 1, m + NKD - 4))

                def post_cb(m, tok, xoff=xoff, t=t, xoc=xoc):
                    p.waits("sync", tok)
                    p.dma(f"o{m % 4}", "sync", xo_o[m * 128:(m + 1) * 128, xoff: xoff + N], xoc[m])
                    oslot[(t, m)] = p.close(f"o{m % 4}")

                tok_out = emit_moe(p, al, N, v2c, tok_v2, x2c, tok_x2, xoc, 0.5,
                                   pre_cb=pre_cb, post_cb=post_cb)
                prev_x2 = tok_out

        if finalize:
            for name in list(p.ch):
                if p.ch[name]["open"]:
                    p.close(name)
            p.drain_all_channels("sync")
            block = es.enter_context(nc.Block())
            p.finalize(block)
    return nc



def build_mega():
    from concourse.aot_env import allow_ici_all_to_all
    nc = bass.Bass("TRN2", target_bir_lowering=False, debug=False)
    dram = nc.dram_tensor
    io_f = {}
    for nm, shp in (("x_in", [D, TPC + 3]), ("halo_mask", [128, 3]),
                    ("w_in_T", [D, 2 * DI]), ("w_xp_T", [DI, 160]),
                    ("a_T", [D, E * R]), ("b_T", [E * R, D]), ("wr_T", [D, E]),
                    ("br_r", [1, E]), ("conv_w_t", [128, NKI * 4]),
                    ("conv_b_t", [128, NKI]), ("ln1_w_t", [128, NKD]),
                    ("ln1_b_t", [128, NKD]), ("repm_in", [8, 512])):
        io_f[nm] = dram(nm, shp, F32, kind="ExternalInput").ap()
    io_s = {}
    for nm, shp in (("wdt_j", [DTR, CPC]), ("db_j", [128, CPC // 128]),
                    ("alog_j", [CPC, DS]), ("dsk_j", [128, CPC // 128])):
        io_s[nm] = dram(nm, shp, F32, kind="ExternalInput").ap()
    io_s["identb_j"] = dram("identb_j", [128, 128], BF16, kind="ExternalInput").ap()
    io_b = {}
    for nm, shp in (("w_out_T", [DI, D]), ("w_ffp_T", [D, H]), ("w_ffo_T", [DI, D]),
                    ("ln2_w_t", [128, NKD]), ("ln2_b_t", [128, NKD])):
        io_b[nm] = dram(nm, shp, F32, kind="ExternalInput").ap()
    io_b["a_T"] = io_f["a_T"]; io_b["b_T"] = io_f["b_T"]
    io_b["wr_T"] = io_f["wr_T"]; io_b["br_r"] = io_f["br_r"]
    io_b["repm_in"] = io_f["repm_in"]
    io_b["xo_o"] = dram("xo_o", [D, TPC], F32, kind="ExternalOutput").ap()

    xc_x = dram("xc_x", [DI, TPC], BF16)
    z_x = dram("z_x", [DI, TPC], BF16)
    xd_g = dram("xd_g", [160, TPC], BF16)
    xc_sh = dram("xc_sh", [NC * CPC, TPC], BF16)
    z_sh = dram("z_sh", [NC * CPC, TPC], BF16)
    xd_sh = dram("xd_sh", [NC * 160, TPC], BF16)
    yg_x = dram("yg_x", [NC * CPC, TPC], BF16)
    yg_sh = dram("yg_sh", [NC * CPC, TPC], BF16)
    x1_s = dram("x1_s", [D, TPC], F32)
    u_s = dram("u_s", [D, TPC], F32)

    io_f["xc_o"] = xc_x.ap()
    io_f["z_o"] = z_x.ap()
    io_f["xd_o"] = xd_g.ap()
    io_f["x1_o"] = x1_s.ap()
    io_f["u_o"] = u_s.ap()

    xcs, zs, xds, ygs = xc_sh.ap(), z_sh.ap(), xd_sh.ap(), yg_x.ap()
    io_s["xc_ap"] = lambda r0, nr, co, ln: xcs[(co // TPC) * CPC + r0:(co // TPC) * CPC + r0 + nr, co % TPC: co % TPC + ln]
    io_s["z_ap"] = lambda r0, nr, co, ln: zs[(co // TPC) * CPC + r0:(co // TPC) * CPC + r0 + nr, co % TPC: co % TPC + ln]
    io_s["xd_ap"] = lambda r0, nr, co, ln: xds[(co // TPC) * 160 + r0:(co // TPC) * 160 + r0 + nr, co % TPC: co % TPC + ln]
    io_s["yg_ap"] = lambda r0, nr, co, ln: ygs[(co // TPC) * CPC + r0:(co // TPC) * CPC + r0 + nr, co % TPC: co % TPC + ln]
    ygshr = yg_sh.ap()
    io_b["ygb_ap"] = lambda r0, nr, xoff, n: ygshr[r0:r0 + nr, xoff: xoff + n]
    io_b["x1_in"] = x1_s.ap()
    io_b["u_in"] = u_s.ap()

    es = contextlib.ExitStack()
    with es:
        p = Prog(nc, es)
        kw = {}
        for nm, shp in (("w_out_b", [DI, D]), ("w_ffp_b", [D, H]), ("w_ffo_b", [DI, D]),
                        ("a_b", [D, E * R]), ("b_b", [E * R, D]), ("wr_b", [D, E]),
                        ("br_b", [1, E])):
            kw[nm] = dram("k_" + nm, shp, BF16)
        for nm, src in (("w_out_b", "w_out_T"), ("w_ffp_b", "w_ffp_T"),
                        ("w_ffo_b", "w_ffo_T"), ("a_b", "a_T"), ("b_b", "b_T"),
                        ("wr_b", "wr_T"), ("br_b", "br_r")):
            p.dma("prepk", "gpsimd", kw[nm].ap(), io_b[src])
        io_b.update(kw)
        io_b["prepk_tok"] = p.close("prepk")
        rg = [list(range(NC))]

        def cc(kind, i, o):
            def f(g, kind=kind, i=i, o=o):
                with allow_ici_all_to_all():
                    return g.collective_compute(kind, AL.bypass, replica_groups=rg,
                                                ins=[i], outs=[o])
            return p.run("gpsimd", f, inc=True)

        def mid_cb():
            for name in list(p.ch):
                if p.ch[name]["open"]:
                    p.close(name)
            p.drain_all_channels("gpsimd")
            cc("AllToAll", xc_x.ap().opt(), xc_sh.ap().opt())
            cc("AllToAll", z_x.ap().opt(), z_sh.ap().opt())

        build_front(nc=nc, p=p, io=io_f, finalize=False, mid_cb=mid_cb)
        for name in list(p.ch):
            if p.ch[name]["open"]:
                p.close(name)
        p.barrier()
        p.drain_all_channels("gpsimd")
        tok_fwd = cc("AllGather", xd_g.ap().opt(), xd_sh.ap().opt())
        for e in ENGS:
            p.waits(e, tok_fwd)
        p.barrier()
        build_scan(nc=nc, p=p, io=io_s, finalize=False)
        for name in list(p.ch):
            if p.ch[name]["open"]:
                p.close(name)
        p.barrier()
        p.drain_all_channels("gpsimd")
        tok_bwd = cc("AllToAll", yg_x.ap().opt(), yg_sh.ap().opt())
        for e in ENGS:
            p.waits(e, tok_bwd)
        p.barrier()
        build_back(nc=nc, p=p, io=io_b, finalize=False)
        for name in list(p.ch):
            if p.ch[name]["open"]:
                p.close(name)
        p.drain_all_channels("sync")
        block = es.enter_context(nc.Block())
        p.finalize(block)
    return nc


def _mega_inputs(inputs):
    front = _front_inputs(inputs)
    wdt = host_f32(inputs["L0_attn_dt_proj_w"]).T.copy()
    db = host_f32(inputs["L0_attn_dt_proj_b"])
    alog = host_f32(inputs["L0_attn_A_log"])
    dsk = host_f32(inputs["L0_attn_D_skip"])
    w_out_T = host_f32(inputs["L0_attn_out_proj_w"]).T.copy()
    w_ffp_T = host_f32(inputs["L0_attn_ff_proj_w"]).T.copy()
    w_ffo_T = host_f32(inputs["L0_attn_ff_out_w"]).T.copy()
    ln2_w_t = colsT(inputs["L0_attn_ln2_w"], NKD)
    ln2_b_t = colsT(inputs["L0_attn_ln2_b"], NKD)
    in_maps = []
    for i in range(NC):
        im = dict(front[i])
        c0 = i * CPC
        im.update(dict(
            wdt_j=np.ascontiguousarray(wdt[:, c0:c0 + CPC]),
            db_j=np.ascontiguousarray(db[c0:c0 + CPC].reshape(CPC // 128, 128).T),
            alog_j=np.ascontiguousarray(alog[c0:c0 + CPC]),
            dsk_j=np.ascontiguousarray(dsk[c0:c0 + CPC].reshape(CPC // 128, 128).T),
            identb_j=np.eye(128, dtype=BF16_NP),
            w_out_T=w_out_T, w_ffp_T=w_ffp_T, w_ffo_T=w_ffo_T,
            ln2_w_t=ln2_w_t, ln2_b_t=ln2_b_t))
        in_maps.append(im)
    return in_maps


def _front_inputs(inputs):
    x = host_f32(inputs["L0_attn_x"]).reshape(TOK, D)
    w_in_T = host_f32(inputs["L0_attn_in_proj_w"]).T.copy()
    w_xp_T = host_f32(inputs["L0_attn_x_proj_w"]).T.copy()
    A = host_f32(inputs["L0_ffn_A"])
    Bw = host_f32(inputs["L0_ffn_B"])
    a_T = np.ascontiguousarray(A.transpose(2, 0, 1).reshape(D, E * R))
    b_T = np.ascontiguousarray(Bw.transpose(0, 2, 1).reshape(E * R, D))
    wr_T = host_f32(inputs["L0_ffn_Wr"]).T.copy()
    br_r = host_f32(inputs["L0_ffn_br"]).reshape(1, E)
    conv_w = host_f32(inputs["L0_attn_conv_w"]).reshape(DI, 4)
    conv_w_t = np.ascontiguousarray(conv_w.reshape(NKI, 128, 4).transpose(1, 0, 2).reshape(128, NKI * 4))
    conv_b_t = colsT(inputs["L0_attn_conv_b"], NKI)
    ln1_w_t = colsT(inputs["L0_attn_ln1_w"], NKD)
    ln1_b_t = colsT(inputs["L0_attn_ln1_b"], NKD)
    repm = np.zeros((8, 512), np.float32)
    for col in range(512):
        repm[col // 64, col] = 1.0
    shared = dict(w_in_T=w_in_T, w_xp_T=w_xp_T, a_T=a_T, b_T=b_T, wr_T=wr_T,
                  br_r=br_r, conv_w_t=conv_w_t, conv_b_t=conv_b_t,
                  ln1_w_t=ln1_w_t, ln1_b_t=ln1_b_t, repm_in=repm)
    in_maps = []
    for i in range(NC):
        s0 = i * TPC
        x_ext = np.zeros((TPC + 3, D), np.float32)
        if i % 4 != 0:
            x_ext[0:3] = x[s0 - 3: s0]
        x_ext[3:] = x[s0: s0 + TPC]
        hm = np.zeros((128, 3), np.float32) if i % 4 == 0 else np.ones((128, 3), np.float32)
        im = dict(shared)
        im["x_in"] = np.ascontiguousarray(x_ext.T)
        im["halo_mask"] = hm
        in_maps.append(im)
    return in_maps


def _scan_inputs(inputs, xc, z, xd):
    wdt = host_f32(inputs["L0_attn_dt_proj_w"]).T.copy()
    db = host_f32(inputs["L0_attn_dt_proj_b"])
    alog = host_f32(inputs["L0_attn_A_log"])
    dsk = host_f32(inputs["L0_attn_D_skip"])
    in_maps = []
    for j in range(NC):
        c0 = j * CPC
        in_maps.append(dict(
            xc_in=np.ascontiguousarray(xc[c0:c0 + CPC]),
            z_in=np.ascontiguousarray(z[c0:c0 + CPC]),
            xd_in=xd,
            wdt_j=np.ascontiguousarray(wdt[:, c0:c0 + CPC]),
            db_j=np.ascontiguousarray(db[c0:c0 + CPC].reshape(CPC // 128, 128).T),
            alog_j=np.ascontiguousarray(alog[c0:c0 + CPC]),
            dsk_j=np.ascontiguousarray(dsk[c0:c0 + CPC].reshape(CPC // 128, 128).T),
            identb_j=np.eye(128, dtype=BF16_NP),
        ))
    return in_maps


def _back_inputs(inputs, yg, x1s, us):
    w_out_T = host_f32(inputs["L0_attn_out_proj_w"]).T.copy()
    w_ffp_T = host_f32(inputs["L0_attn_ff_proj_w"]).T.copy()
    w_ffo_T = host_f32(inputs["L0_attn_ff_out_w"]).T.copy()
    A = host_f32(inputs["L0_ffn_A"])
    Bw = host_f32(inputs["L0_ffn_B"])
    a_T = np.ascontiguousarray(A.transpose(2, 0, 1).reshape(D, E * R))
    b_T = np.ascontiguousarray(Bw.transpose(0, 2, 1).reshape(E * R, D))
    wr_T = host_f32(inputs["L0_ffn_Wr"]).T.copy()
    br_r = host_f32(inputs["L0_ffn_br"]).reshape(1, E)
    ln2_w_t = colsT(inputs["L0_attn_ln2_w"], NKD)
    ln2_b_t = colsT(inputs["L0_attn_ln2_b"], NKD)
    repm = np.zeros((8, 512), np.float32)
    for col in range(512):
        repm[col // 64, col] = 1.0
    shared = dict(w_out_T=w_out_T, w_ffp_T=w_ffp_T, w_ffo_T=w_ffo_T, a_T=a_T,
                  b_T=b_T, wr_T=wr_T, br_r=br_r, ln2_w_t=ln2_w_t,
                  ln2_b_t=ln2_b_t, repm_in=repm)
    in_maps = []
    for i in range(NC):
        im = dict(shared)
        im["yg_in"] = np.ascontiguousarray(yg[:, i * TPC:(i + 1) * TPC])
        im["x1_in"] = x1s[i]
        im["u_in"] = us[i]
        in_maps.append(im)
    return in_maps


_NC_CACHE = {}


def kernel(**inputs):
    cores = list(range(NC))
    if MODE == "mega":
        if "mega" not in _NC_CACHE:
            _NC_CACHE["mega"] = build_mega()
        res = run_bass_kernel_spmd(_NC_CACHE["mega"], _mega_inputs(inputs), core_ids=cores)
        out = np.concatenate([np.asarray(r["xo_o"]) for r in res.results], axis=1)
        return np.ascontiguousarray(out.T).reshape(B_, L_, D)
    if "front" not in _NC_CACHE:
        _NC_CACHE["front"] = build_front()
        _NC_CACHE["scan"] = build_scan()
        _NC_CACHE["back"] = build_back()
    resA = run_bass_kernel_spmd(_NC_CACHE["front"], _front_inputs(inputs), core_ids=cores)
    xc = np.concatenate([np.asarray(r["xc_o"]) for r in resA.results], axis=1)
    z = np.concatenate([np.asarray(r["z_o"]) for r in resA.results], axis=1)
    xd = np.ascontiguousarray(np.concatenate([np.asarray(r["xd_o"]) for r in resA.results], axis=1))
    x1s = [np.asarray(r["x1_o"]) for r in resA.results]
    us = [np.asarray(r["u_o"]) for r in resA.results]
    resB = run_bass_kernel_spmd(_NC_CACHE["scan"], _scan_inputs(inputs, xc, z, xd), core_ids=cores)
    yg = np.concatenate([np.asarray(r["yg_o"]) for r in resB.results], axis=0)
    resC = run_bass_kernel_spmd(_NC_CACHE["back"], _back_inputs(inputs, yg, x1s, us), core_ids=cores)
    out = np.concatenate([np.asarray(r["xo_o"]) for r in resC.results], axis=1)
    return np.ascontiguousarray(out.T).reshape(B_, L_, D)


# version 10
# speedup vs baseline: 1.6847x; 1.0247x over previous
import contextlib
import numpy as np
import ml_dtypes
import concourse.bass as bass
import concourse.mybir as mybir
from concourse.bass_utils import run_bass_kernel_spmd

F32 = mybir.dt.float32
BF16 = mybir.dt.bfloat16
BF16_NP = ml_dtypes.bfloat16
AL = mybir.AluOpType
AF = mybir.ActivationFunctionType
AX = mybir.AxisListType.X

NC = 8
B_, L_, D, DI, DS, DTR, H = 2, 16384, 2048, 4096, 16, 128, 8192
E, R = 8, 64
TOK = B_ * L_
TPC = TOK // NC
CPC = DI // NC
NKD = D // 128
NKI = DI // 128
EPS = 1e-5
TT_A = 512
TT_C = 256
FRONT_TILES = TPC // TT_A
LC = 1024
SCAN_CTS = TOK // LC
SCAN_DEBUG = False
BACK_TILES = TPC // TT_C
MODE = "mega"
BACK_SPAN = TPC

ENGS = ("sync", "tensor", "vector", "scalar", "gpsimd")
DMAQ = ("sync", "scalar", "gpsimd")


class Prog:

    def __init__(self, nc, es):
        self.nc = nc
        self.es = es
        self.ops = {e: [] for e in ENGS}
        self.sem = {e: es.enter_context(nc.semaphore(f"es_{e}")) for e in ENGS}
        self.cnt = {e: 0 for e in ENGS}
        self.lastwait = {e: {} for e in ENGS}
        self.ch = {}

    def _channel(self, name, eng):
        if name not in self.ch:
            self.ch[name] = dict(sem=self.es.enter_context(self.nc.semaphore(f"dc_{name}")),
                                 eng=eng, total=0, closed=0, open=False)
        c = self.ch[name]
        assert c["eng"] == eng, f"channel {name} is owned by {c['eng']}"
        return c

    def dma(self, name, eng, out, in_):
        c = self._channel(name, eng)
        if not c["open"]:
            if c["closed"] > 0:
                self.ops[eng].append(
                    lambda ifc, sem=c["sem"], v=c["closed"]: ifc.wait_ge(sem, v))
            c["open"] = True
        c["total"] += 16
        sem = c["sem"]
        if eng in ("vector", "scalar", "gpsimd"):
            self.ops[eng].append(lambda ifc: ifc.drain())
        def cl(ifc, out=out, in_=in_, sem=sem):
            ifc.dma_start(out=out, in_=in_).then_inc(sem, 16)
        self.ops[eng].append(cl)

    def close(self, name):
        c = self.ch[name]
        c["open"] = False
        c["closed"] = c["total"]
        return ("d", name, c["closed"])

    def run(self, eng, fn, inc=False):
        sem = self.sem[eng]
        if eng in ("vector", "scalar", "gpsimd"):
            self.ops[eng].append(lambda ifc: ifc.drain())
        if inc:
            self.cnt[eng] += 1

            def cl(ifc, fn=fn, sem=sem):
                fn(ifc).then_inc(sem, 1)
        else:
            def cl(ifc, fn=fn):
                fn(ifc)
        self.ops[eng].append(cl)
        return ("e", eng, self.cnt[eng]) if inc else None

    def wait(self, eng, tok):
        if tok is None:
            return
        kind, src, val = tok
        if val <= 0:
            return
        if kind == "e" and src == eng:
            return
        key = (kind, src)
        if self.lastwait[eng].get(key, 0) >= val:
            return
        self.lastwait[eng][key] = val
        if kind == "d":
            c = self.ch[src]
            assert val <= c["closed"], f"wait on open batch of channel {src}"
            sem = c["sem"]
        else:
            sem = self.sem[src]
        self.ops[eng].append(lambda ifc, sem=sem, val=val: ifc.wait_ge(sem, val))

    def waits(self, eng, *toks):
        for t in toks:
            self.wait(eng, t)

    def mark(self, eng):
        sem = self.sem[eng]
        prev = self.cnt[eng]
        if eng == "gpsimd" and prev > 0:
            self.ops[eng].append(lambda ifc, sem=sem, v=prev: ifc.wait_ge(sem, v))
        self.cnt[eng] += 1
        self.ops[eng].append(lambda ifc: ifc.drain())
        self.ops[eng].append(lambda ifc, sem=sem: ifc.sem_inc(sem, 1))
        return ("e", eng, self.cnt[eng])

    def barrier(self, engs=ENGS):
        toks = {e: self.mark(e) for e in engs}
        for e in engs:
            for o in engs:
                if o != e:
                    self.wait(e, toks[o])

    def drain_all_channels(self, eng):
        for name, c in self.ch.items():
            if c["total"] > 0:
                self.ops[eng].append(
                    lambda ifc, sem=c["sem"], v=c["total"]: ifc.wait_ge(sem, v))

    def finalize(self, block):
        for e in ENGS:
            if not self.ops[e]:
                continue
            deco = getattr(block, e)
            ops = self.ops[e]

            def body(ifc, ops=ops):
                for cl in ops:
                    cl(ifc)
            deco(body)



def emit_ln(p, al, N, x32c, ready_tok, out16c=None, out32c=None, affine=None,
            tmp32=None):
    nc = p.nc
    psA = al["psA"][0:1, 0:N]
    psB = al["psB"][0:1, 0:N]
    p.waits("scalar", ready_tok)
    cons = [None, None]
    for k in range(NKD):
        b = k % 2
        xbt = al["xbt"][:, b * al["ln_n"]: b * al["ln_n"] + N]
        p.waits("scalar", cons[b])
        tk = p.run("scalar", lambda s, o=xbt, i=x32c[k]: s.copy(o, i), inc=True)
        p.waits("tensor", tk)
        p.run("tensor", lambda t, o=psA, w=al["ones_col16"], r=xbt,
              s=(k == 0), e=(k == NKD - 1): t.matmul(o, w[:], r, start=s, stop=e))
        cons[b] = p.mark("tensor")
    for k in range(NKD):
        b = k % 2
        sqt = al["sqt"][:, b * al["ln_n"]: b * al["ln_n"] + N]
        p.waits("scalar", cons[b])
        tk = p.run("scalar", lambda s, o=sqt, i=x32c[k]: s.activation(o, i, AF.Square), inc=True)
        p.waits("tensor", tk)
        p.run("tensor", lambda t, o=psB, w=al["ones_col16"], r=sqt,
              s=(k == 0), e=(k == NKD - 1): t.matmul(o, w[:], r, start=s, stop=e))
        cons[b] = p.mark("tensor")
    tok_mm = cons[(NKD - 1) % 2]

    m = al["st_m"][:, 0:N]
    q = al["st_q"][:, 0:N]
    var = al["st_var"][:, 0:N]
    sd = al["st_sd"][:, 0:N]
    r_ = al["st_r"][:, 0:N]
    nmr = al["st_nmr"][:, 0:N]
    t1 = al["st_t1"][:, 0:N]
    t2 = al["st_t2"][:, 0:N]
    p.waits("vector", tok_mm)
    p.run("vector", lambda v, o=m, i=psA: v.tensor_scalar(o, i, 1.0 / D, None, AL.mult))
    p.run("vector", lambda v, o=q, i=psB: v.tensor_scalar(o, i, 1.0 / D, None, AL.mult))
    p.run("vector", lambda v, o=t1, a=m: v.tensor_tensor(o, a, a, AL.mult))
    p.run("vector", lambda v, o=var, a=q, b=t1: v.tensor_tensor(o, a, b, AL.subtract))
    tv = p.run("vector", lambda v, o=t2, i=var: v.tensor_scalar(o, i, EPS, None, AL.add), inc=True)
    p.waits("scalar", tv)
    ts_ = p.run("scalar", lambda s, o=sd, i=t2: s.activation(o, i, AF.Sqrt), inc=True)
    p.waits("vector", ts_)
    p.run("vector", lambda v, o=r_, i=sd: v.reciprocal(o, i))
    p.run("vector", lambda v, o=t1, a=m, b=r_: v.tensor_tensor(o, a, b, AL.mult))
    tv2 = p.run("vector", lambda v, o=nmr, i=t1: v.tensor_scalar(o, i, -1.0, None, AL.mult), inc=True)

    psR = al["psR"][:, 0:N]
    psN = al["psN"][:, 0:N]
    p.waits("tensor", tv2)
    p.run("tensor", lambda t, o=psR, w=al["ones_row32"], r=r_: t.matmul(o, w[:], r, start=True, stop=True))
    tok_rep = p.run("tensor", lambda t, o=psN, w=al["ones_row32"], r=nmr: t.matmul(o, w[:], r, start=True, stop=True), inc=True)

    p.waits("vector", tok_rep)
    for k in range(NKD):
        p.run("vector", lambda v, o=tmp32[0:128, 0:N], a=x32c[k], b=psR: v.tensor_tensor(o, a, b, AL.mult))
        if out32c is not None:
            p.run("vector", lambda v, o=out32c[k], a=tmp32[0:128, 0:N], b=psN: v.tensor_tensor(o, a, b, AL.add))
        if out16c is not None and affine is None:
            src = out32c[k] if out32c is not None else None
            if src is not None:
                p.run("vector", lambda v, o=out16c[k], i=src: v.tensor_copy(o, i))
            else:
                p.run("vector", lambda v, o=out16c[k], a=tmp32[0:128, 0:N], b=psN: v.tensor_tensor(o, a, b, AL.add))
        if out16c is not None and affine is not None:
            wcol, bcol = affine
            p.run("vector", lambda v, o=al["tmp32b"][0:128, 0:N], a=tmp32[0:128, 0:N], b=psN: v.tensor_tensor(o, a, b, AL.add))
            p.run("vector", lambda v, o=out16c[k], i=al["tmp32b"][0:128, 0:N], w=wcol[k], b=bcol[k]:
                  v.tensor_scalar(o, i, w, b, AL.mult, AL.add))
    return p.mark("vector")


def emit_moe(p, al, N, v16c, vtok, x32c, xtok, out32c, scale, pre_cb=None, post_cb=None):
    nc = p.nc
    nst = (N + 127) // 128
    psR = al["psRt"]
    lgs = al["lgs"]
    mx = al["mx"]
    wsc = al["wsc"]
    sm = al["sm"]
    maskb = al["maskb"]
    p.waits("tensor", vtok)
    for st in range(nst):
        Ns = min(128, N - 128 * st)
        out = psR[0:Ns, st * 8:(st + 1) * 8]
        for k in range(NKD):
            p.run("tensor", lambda t, o=out, w=v16c[k][:, st * 128: st * 128 + Ns],
                  r=al["wrT"][:, k * 8:(k + 1) * 8], s=(k == 0):
                  t.matmul(o, w, r, start=s, stop=False))
        p.run("tensor", lambda t, o=out, w=al["ones_row16"][:, 0:Ns], r=al["brT"][:]:
              t.matmul(o, w, r, start=False, stop=True))
    tok_r = p.mark("tensor")
    p.waits("scalar", tok_r)
    ts_ = None
    for st in range(nst):
        Ns = min(128, N - 128 * st)
        ts_ = p.run("scalar", lambda s, o=lgs[0:Ns, st * 8:(st + 1) * 8],
                    i=psR[0:Ns, st * 8:(st + 1) * 8]: s.copy(o, i), inc=True)
    p.waits("vector", ts_)
    for st in range(nst):
        Ns = min(128, N - 128 * st)
        p.run("vector", lambda v, o=mx[0:Ns, st:st + 1], i=lgs[0:Ns, st * 8:(st + 1) * 8]:
              v.reduce_max(o, i, axis=AX))
    for st in range(nst):
        Ns = min(128, N - 128 * st)
        p.run("vector", lambda v, o=al["nmx"][0:Ns, st:st + 1], i=mx[0:Ns, st:st + 1]:
              v.tensor_scalar(o, i, -1.0, None, AL.mult))
    tv = p.mark("vector")
    p.waits("scalar", tv)
    for st in range(nst):
        Ns = min(128, N - 128 * st)
        p.run("scalar", lambda s, o=al["exps"][0:Ns, st * 8:(st + 1) * 8], i=lgs[0:Ns, st * 8:(st + 1) * 8],
              b=al["nmx"][0:Ns, st:st + 1]: s.activation(o, i, AF.Exp, bias=b))
    ts2 = p.mark("scalar")
    p.waits("vector", ts2)
    for st in range(nst):
        Ns = min(128, N - 128 * st)
        p.run("vector", lambda v, o=sm[0:Ns, st:st + 1], i=al["exps"][0:Ns, st * 8:(st + 1) * 8]:
              v.reduce_sum(o, i, axis=AX))
    for st in range(nst):
        Ns = min(128, N - 128 * st)
        p.run("vector", lambda v, o=al["w0"][0:Ns, st:st + 1], i=sm[0:Ns, st:st + 1]: v.reciprocal(o, i))
    for st in range(nst):
        Ns = min(128, N - 128 * st)
        p.run("vector", lambda v, o=wsc[0:Ns, st:st + 1], i=al["w0"][0:Ns, st:st + 1]:
              v.tensor_scalar(o, i, scale / R, None, AL.mult))
    for st in range(nst):
        Ns = min(128, N - 128 * st)
        p.run("vector", lambda v, o=maskb[0:Ns, st * 8:(st + 1) * 8], i=lgs[0:Ns, st * 8:(st + 1) * 8],
              a=mx[0:Ns, st:st + 1], b=wsc[0:Ns, st:st + 1]:
              v.tensor_scalar(o, i, a, b, AL.is_equal, AL.mult))
    tv2 = p.mark("vector")
    psT = al["psT"]
    p.waits("tensor", tv2)
    for st in range(nst):
        Ns = min(128, N - 128 * st)
        p.run("tensor", lambda t, o=psT[0:8, st * 128: st * 128 + Ns],
              i=maskb[0:Ns, st * 8:(st + 1) * 8], id_=al["ident"], n=Ns:
              t.transpose(o, i, id_[0:n, 0:n]))
    tok_t = p.mark("tensor")
    p.waits("scalar", tok_t)
    ts3 = p.run("scalar", lambda s, o=al["maskT"][0:8, 0:N], i=psT[0:8, 0:N]: s.copy(o, i), inc=True)
    psM = al["psM"][:, 0:N]
    psH = al["psH"][:, 0:N]
    hm = al["hm16"]
    tok_hm = []
    for c in range(4):
        p.waits("tensor", ts3)
        tmr = p.run("tensor", lambda t, o=psM, w=al["repm"][:, c * 128:(c + 1) * 128],
              r=al["maskT"][0:8, 0:N]: t.matmul(o, w, r, start=True, stop=True), inc=True)
        p.waits("scalar", tmr)
        tmc = p.run("scalar", lambda s, o=al["mrep16"][:, 0:N], i=psM: s.copy(o, i), inc=True)
        for k in range(NKD):
            p.run("tensor", lambda t, o=psH, w=al["aT"][:, (k * 4 + c) * 128:(k * 4 + c) * 128 + 128],
                  r=v16c[k], s=(k == 0), e=(k == NKD - 1): t.matmul(o, w, r, start=s, stop=e))
        tmh = p.mark("tensor")
        p.waits("scalar", tmh)
        tch = p.run("scalar", lambda s, o=al["h16"][:, 0:N], i=psH: s.copy(o, i), inc=True)
        p.waits("vector", tch, tmc)
        thm = p.run("vector", lambda v, o=hm[:, c * al["ffn_n"]: c * al["ffn_n"] + N],
              a=al["h16"][:, 0:N], b=al["mrep16"][:, 0:N]: v.tensor_tensor(o, a, b, AL.mult), inc=True)
        tok_hm.append(thm)
        p.waits("tensor", thm, tch, tmc)
    psS = al["psS"][:, 0:N]
    p.waits("tensor", tok_hm[-1], xtok)
    out_tok = None
    for mchunk in range(NKD):
        for c in range(4):
            p.run("tensor", lambda t, o=psS, w=al["bT"][:, (c * NKD + mchunk) * 128:(c * NKD + mchunk) * 128 + 128],
                  r=hm[:, c * al["ffn_n"]: c * al["ffn_n"] + N], s=(c == 0), e=(c == 3):
                  t.matmul(o, w, r, start=s, stop=e))
        tms = p.mark("tensor")
        p.waits("vector", tms)
        if pre_cb is not None:
            p.waits("vector", pre_cb(mchunk))
        out_tok = p.run("vector", lambda v, o=out32c[mchunk], a=x32c[mchunk], b=psS:
                        v.tensor_tensor(o, a, b, AL.add), inc=True)
        p.waits("tensor", out_tok)
        if post_cb is not None:
            post_cb(mchunk, out_tok)
    return out_tok


def host_f32(a):
    return np.ascontiguousarray(np.asarray(a, dtype=np.float32))


def colsT(vec, nchunk):
    return np.ascontiguousarray(np.asarray(vec, dtype=np.float32).reshape(nchunk, 128).T)



def build_front(nc=None, p=None, io=None, finalize=True, mid_cb=None):
    if nc is None:
        nc = bass.Bass("TRN2", target_bir_lowering=False, debug=False)
    dram = nc.dram_tensor
    if io is None:
        io = {}
        for nm, shp in (("x_in", [D, TPC + 3]), ("halo_mask", [128, 3]),
                        ("w_in_T", [D, 2 * DI]), ("w_xp_T", [DI, 160]),
                        ("a_T", [D, E * R]), ("b_T", [E * R, D]), ("wr_T", [D, E]),
                        ("br_r", [1, E]), ("conv_w_t", [128, NKI * 4]),
                        ("conv_b_t", [128, NKI]), ("ln1_w_t", [128, NKD]),
                        ("ln1_b_t", [128, NKD]), ("repm_in", [8, 512])):
            io[nm] = dram(nm, shp, F32, kind="ExternalInput").ap()
        io["xc_o"] = dram("xc_o", [DI, TPC], BF16, kind="ExternalOutput").ap()
        io["z_o"] = dram("z_o", [DI, TPC], BF16, kind="ExternalOutput").ap()
        io["xd_o"] = dram("xd_o", [160, TPC], BF16, kind="ExternalOutput").ap()
        io["x1_o"] = dram("x1_o", [D, TPC], F32, kind="ExternalOutput").ap()
        io["u_o"] = dram("u_o", [D, TPC], F32, kind="ExternalOutput").ap()
    (x_in, halo_mask, w_in_T, w_xp_T, a_T, b_T, wr_T, br_r, conv_w_t, conv_b_t,
     ln1_w_t, ln1_b_t, repm_in, xc_o, z_o, xd_o, x1_o, u_o) = (
        io["x_in"], io["halo_mask"], io["w_in_T"], io["w_xp_T"], io["a_T"],
        io["b_T"], io["wr_T"], io["br_r"], io["conv_w_t"], io["conv_b_t"],
        io["ln1_w_t"], io["ln1_b_t"], io["repm_in"], io["xc_o"], io["z_o"],
        io["xd_o"], io["x1_o"], io["u_o"])

    w_in_b = dram("f_w_in_b", [D, 2 * DI], BF16)
    w_xp_b = dram("f_w_xp_b", [DI, 160], BF16)
    a_b = dram("f_a_b", [D, E * R], BF16)
    b_b = dram("f_b_b", [E * R, D], BF16)
    wr_b = dram("f_wr_b", [D, E], BF16)
    br_b = dram("f_br_b", [1, E], BF16)
    t_s = dram("f_t_s", [D, TPC + 3], BF16).ap()

    PF = "f_"
    es = contextlib.ExitStack()
    with es:
        if p is None:
            p = Prog(nc, es)
        p.dma("prepm", "gpsimd", a_b.ap(), a_T)
        p.dma("prepm", "gpsimd", b_b.ap(), b_T)
        p.dma("prepm", "gpsimd", wr_b.ap(), wr_T)
        p.dma("prepm", "gpsimd", br_b.ap(), br_r)
        tok_prepm = p.close("prepm")
        p.dma("prep", "gpsimd", w_in_b.ap(), w_in_T)
        p.dma("prep", "gpsimd", w_xp_b.ap(), w_xp_T)
        tok_prep = p.close("prep")
        if io.get("post_prep_cb") is not None:
            io["post_prep_cb"]()

        N = TT_A
        with contextlib.ExitStack() as e0:
            sb0 = lambda name, shape, dt=F32: e0.enter_context(nc.sbuf_tensor(PF + "p0" + name, shape, dt))[:]
            pst0 = lambda name, shape, dt=F32: e0.enter_context(nc.psum_tensor(PF + name, shape, dt))[:]
            al = {}
            bufA = sb0("bufA", [128, NKD * N])
            bufB = sb0("bufB", [128, NKD * N])
            al["xbt"] = sb0("xbt", [128, 2 * N], BF16)
            al["sqt"] = sb0("sqt", [128, 2 * N], BF16)
            al["ln_n"] = N
            vt16 = sb0("vt16", [128, NKD * N], BF16)
            al["tmp32"] = sb0("tmp32", [128, N])
            al["tmp32b"] = sb0("tmp32b", [128, N])
            for nm in ("st_m", "st_q", "st_var", "st_sd", "st_r", "st_nmr", "st_t1", "st_t2"):
                al[nm] = sb0(nm, [1, N])
            al["ones_col16"] = e0.enter_context(nc.sbuf_tensor(PF + "p0ones_c", [128, 1], BF16))
            al["ones_row32"] = e0.enter_context(nc.sbuf_tensor(PF + "p0ones_r", [1, 128], F32))
            al["ones_row16"] = sb0("ones_r16", [1, 128], BF16)
            al["ident"] = e0.enter_context(nc.sbuf_tensor(PF + "p0ident", [128, 128], BF16))
            zeros128 = e0.enter_context(nc.sbuf_tensor(PF + "p0zeros", [128, 128], BF16))
            al["lgs"] = sb0("lgs", [128, 32])
            al["mx"] = sb0("mx", [128, 4])
            al["nmx"] = sb0("nmx", [128, 4])
            al["sm"] = sb0("sm", [128, 4])
            al["wsc"] = sb0("wsc", [128, 4])
            al["w0"] = sb0("w0", [128, 4])
            al["exps"] = sb0("exps", [128, 32])
            al["maskb"] = sb0("maskb", [128, 32], BF16)
            al["maskT"] = sb0("maskT", [8, N], BF16)
            al["repm"] = sb0("repm", [8, 512], BF16)
            al["mrep16"] = sb0("mrep16", [128, N], BF16)
            al["h16"] = sb0("h16", [128, N], BF16)
            al["hm16"] = sb0("hm16", [128, 4 * N], BF16)
            al["ffn_n"] = N
            al["aT"] = sb0("aT", [128, NKD * 4 * 128], BF16)
            al["bT"] = sb0("bT", [128, 4 * NKD * 128], BF16)
            al["wrT"] = sb0("wrT", [128, NKD * 8], BF16)
            al["brT"] = e0.enter_context(nc.sbuf_tensor(PF + "p0brT", [1, 8], BF16))
            al["psA"] = pst0("p0psA", [1, 512])
            al["psB"] = pst0("p0psB", [1, 512])
            al["psR"] = pst0("p0psR", [128, 512])
            al["psN"] = pst0("p0psN", [128, 512])
            al["psS"] = pst0("p0psS", [128, 512])
            al["psRt"] = al["psS"][:, 0:32]
            al["psM"] = al["psR"]
            al["psH"] = al["psN"]
            psX0 = pst0("p0psX", [128, 512])
            al["psT"] = psX0[0:8, :].bitcast(BF16)
            ln1w = sb0("ln1w", [128, NKD])
            ln1b = sb0("ln1b", [128, NKD])
            hmask = sb0("hmask", [128, 3])
            tok_z = p.run("vector", lambda v: v.memset(zeros128[:], 0.0), inc=True)
            p.waits("gpsimd", tok_z)
            tok_id = p.run("gpsimd", lambda g: g.affine_select(
                out=al["ident"][:], in_=zeros128[:], compare_op=AL.not_equal,
                fill=1.0, base=0, pattern=[[-1, 128]], channel_multiplier=1), inc=True)
            tok_c1 = p.run("gpsimd", lambda g: g.memset(al["ones_col16"][:], 1.0), inc=True)
            tok_c2 = p.run("gpsimd", lambda g: g.memset(al["ones_row32"][:], 1.0), inc=True)
            tok_c3 = p.run("gpsimd", lambda g: g.memset(al["ones_row16"][:], 1.0), inc=True)
            p.waits("sync", tok_prepm)
            for k in range(NKD):
                p.dma("res", "sync", al["aT"][:, k * 512:(k + 1) * 512],
                      a_b.ap()[k * 128:(k + 1) * 128, :])
            for c in range(4):
                p.dma("res", "sync", al["bT"][:, c * 2048:(c + 1) * 2048],
                      b_b.ap()[c * 128:(c + 1) * 128, :])
            for k in range(NKD):
                p.dma("res", "sync", al["wrT"][:, k * 8:(k + 1) * 8],
                      wr_b.ap()[k * 128:(k + 1) * 128, :])
            p.dma("res", "sync", al["brT"][:], br_b.ap())
            p.dma("res", "sync", ln1w, ln1_w_t)
            p.dma("res", "sync", ln1b, ln1_b_t)
            p.dma("res", "sync", hmask, halo_mask)
            tok_res = p.close("res")
            p.dma("prep0", "gpsimd", al["repm"], repm_in)
            tok_rp = p.close("prep0")
            for e in ENGS:
                p.waits(e, tok_res, tok_rp, tok_id, tok_c1, tok_c2, tok_c3)
            p.barrier()
            ln1w_c = [ln1w[:, k:k + 1] for k in range(NKD)]
            ln1b_c = [ln1b[:, k:k + 1] for k in range(NKD)]
            store_toks = {}

            def p0_pass(N_, xoff, halo_pass):
                xc = [bufA[:, k * TT_A: k * TT_A + N_] for k in range(NKD)]
                x1c = [bufB[:, k * TT_A: k * TT_A + N_] for k in range(NKD)]
                vc = [vt16[:, k * TT_A: k * TT_A + N_] for k in range(NKD)]
                p.waits("vector", store_toks.get("x1"), store_toks.get("u"), store_toks.get("t"))
                p.waits("sync", store_toks.get("x1"), store_toks.get("u"), store_toks.get("t"),
                        store_toks.get("tln"))
                for k in range(NKD):
                    p.dma("x", "sync", xc[k], x_in[k * 128:(k + 1) * 128, xoff: xoff + N_])
                tok_x = p.close("x")
                tok_v = emit_ln(p, al, N_, xc, tok_x, out16c=vc, tmp32=al["tmp32"])
                tok_x1 = emit_moe(p, al, N_, vc, tok_v, xc, tok_x, x1c, 0.5)
                if not halo_pass:
                    p.waits("scalar", tok_x1)
                    for k in range(NKD):
                        p.dma("ox1", "scalar", x1_o[k * 128:(k + 1) * 128, xoff - 3: xoff - 3 + N_], x1c[k])
                    store_toks["x1"] = p.close("ox1")
                uc = xc
                tok_u = emit_ln(p, al, N_, x1c, tok_x1, out32c=uc, tmp32=al["tmp32"])
                if not halo_pass:
                    p.waits("scalar", tok_u)
                    for k in range(NKD):
                        p.dma("ou", "scalar", u_o[k * 128:(k + 1) * 128, xoff - 3: xoff - 3 + N_], uc[k])
                    store_toks["u"] = p.close("ou")
                tok_t = emit_ln(p, al, N_, uc, tok_u, out16c=vc, affine=(ln1w_c, ln1b_c),
                                tmp32=al["tmp32"])
                if halo_pass:
                    p.waits("vector", tok_t)
                    for k in range(NKD):
                        tok_t = p.run("vector", lambda v, o=al["tmp32"][:, 0:3],
                                      a=vt16[:, k * TT_A: k * TT_A + 3]:
                                      v.tensor_tensor(o, a, hmask, AL.mult), inc=True)
                        tok_t = p.run("vector", lambda v, o=vt16[:, k * TT_A: k * TT_A + 3],
                                      i=al["tmp32"][:, 0:3]: v.tensor_copy(o, i), inc=True)
                    p.waits("scalar", tok_t)
                    for k in range(NKD):
                        p.dma("ot", "scalar", t_s[k * 128:(k + 1) * 128, 0:3],
                              vt16[:, k * TT_A: k * TT_A + 3])
                    store_toks["t"] = p.close("ot")
                else:
                    p.waits("scalar", tok_t)
                    for k in range(NKD):
                        p.dma("ot", "scalar", t_s[k * 128:(k + 1) * 128, xoff: xoff + N_], vc[k])
                    store_toks["t"] = p.close("ot")
                store_toks["tln"] = tok_t

            p0_pass(4, 0, True)
            for t in range(TPC // N):
                p0_pass(N, 3 + t * N, False)
            for name in list(p.ch):
                if p.ch[name]["open"]:
                    p.close(name)
            p.drain_all_channels("sync")
            p.barrier()

        PT = 2048
        with contextlib.ExitStack() as e1:
            sb1 = lambda name, shape, dt=F32: e1.enter_context(nc.sbuf_tensor(PF + "p1" + name, shape, dt))[:]
            pst1 = lambda name, shape, dt=F32: e1.enter_context(nc.psum_tensor(PF + name, shape, dt))[:]
            t16 = sb1("t16", [128, NKD * (PT + 3)], BF16)
            wbuf = [sb1(f"wbuf{b}", [128, NKD * 512], BF16) for b in range(2)]
            work = sb1("work", [128, 4 * (PT + 3)], BF16)
            xc16 = sb1("xc16", [128, 4 * PT], BF16)
            z16 = sb1("z16", [128, 2 * PT], BF16)
            cva = sb1("cva", [128, PT])
            cvb = sb1("cvb", [128, PT])
            convw = sb1("convw", [128, NKI * 4])
            convb = sb1("convb", [128, NKI])
            psMM = [pst1("p1psMM", [128, 512]), pst1("p1psMM2", [128, 512])]
            p.dma("res1", "sync", convw, conv_w_t)
            p.dma("res1", "sync", convb, conv_b_t)
            tok_res1 = p.close("res1")
            p.waits("sync", tok_prep)
            for e in ("sync", "tensor", "vector", "scalar"):
                p.waits(e, tok_res1)
            wfree = [None, None]
            evac = [None, None]
            ldone = {}
            for t in range(TPC // PT):
                xoff = t * PT
                p.waits("sync", ldone.get("t_cons"))
                for k in range(NKD):
                    p.dma("tl", "sync", t16[:, k * (PT + 3): k * (PT + 3) + PT + 3],
                          t_s[k * 128:(k + 1) * 128, xoff: xoff + PT + 3])
                tok_t = p.close("tl")
                p.waits("tensor", tok_t)
                for m_abs in range(2 * NKI):
                    g, mc = m_abs // 4, m_abs % 4
                    par = g % 2
                    if mc == 0:
                        p.waits("sync", wfree[par])
                        for k in range(NKD):
                            p.dma(f"w{par}", "sync", wbuf[par][:, k * 512:(k + 1) * 512],
                                  w_in_b.ap()[k * 128:(k + 1) * 128, g * 512:(g + 1) * 512])
                        tok_w = p.close(f"w{par}")
                        p.waits("tensor", tok_w)
                    is_xm = m_abs < NKI
                    cslot = m_abs % 4
                    zslot = m_abs % 2
                    npieces = PT // 512
                    dst = work[:, cslot * (PT + 3): cslot * (PT + 3) + PT + 3] if is_xm else None
                    if is_xm:
                        p.waits("scalar", ldone.get(("work", cslot)))
                    else:
                        p.waits("scalar", ldone.get(("z", zslot)))
                    total = PT + 3 if is_xm else PT
                    toff = 0 if is_xm else 3
                    for h in range((total + 511) // 512):
                        c0 = h * 512
                        cn = min(512, total - c0)
                        bank = (m_abs * 5 + h) % 2
                        ps = psMM[bank]
                        p.waits("tensor", evac[bank])
                        for k in range(NKD):
                            p.run("tensor", lambda tt, o=ps[:, 0:cn],
                                  w=wbuf[par][:, k * 512 + mc * 128: k * 512 + mc * 128 + 128],
                                  r=t16[:, k * (PT + 3) + toff + c0: k * (PT + 3) + toff + c0 + cn],
                                  st=(k == 0), e_=(k == NKD - 1): tt.matmul(o, w, r, start=st, stop=e_))
                        tmm = p.mark("tensor")
                        p.waits("scalar", tmm)
                        if is_xm:
                            tev = p.run("scalar", lambda sc, o=dst[:, c0:c0 + cn], i=ps[:, 0:cn]:
                                        sc.copy(o, i), inc=True)
                        else:
                            tev = p.run("scalar", lambda sc, o=z16[:, zslot * PT + c0: zslot * PT + c0 + cn],
                                        i=ps[:, 0:cn]: sc.copy(o, i), inc=True)
                        evac[bank] = tev
                    if is_xm:
                        cw = [convw[:, m_abs * 4 + kk: m_abs * 4 + kk + 1] for kk in range(4)]
                        p.waits("vector", tev, ldone.get(("cv", 0)))
                        p.run("vector", lambda v, o=cva[:, 0:PT], i=dst[:, 0:PT], w=cw[0]:
                              v.tensor_scalar(o, i, w, None, AL.mult))
                        for kk in range(1, 4):
                            src, dsx = (cva, cvb) if kk % 2 == 1 else (cvb, cva)
                            p.run("vector", lambda v, o=dsx[:, 0:PT], i=dst[:, kk:kk + PT], w=cw[kk], b=src[:, 0:PT]:
                                  v.scalar_tensor_tensor(o, i, w, b, AL.mult, AL.add))
                        tok_cv = p.mark("vector")
                        ldone[("work", cslot)] = tok_cv
                        xslot = m_abs % 4
                        p.waits("scalar", tok_cv, ldone.get(("xc", xslot)))
                        tok_xc = p.run("scalar", lambda sc, o=xc16[:, xslot * PT: xslot * PT + PT],
                                       i=cvb[:, 0:PT], b=convb[:, m_abs:m_abs + 1]:
                                       sc.activation(o, i, AF.Silu, bias=b), inc=True)
                        ldone[("cv", 0)] = tok_xc
                        p.waits("scalar", tok_xc)
                        p.dma(f"oxc{xslot}", "scalar",
                              xc_o[m_abs * 128:(m_abs + 1) * 128, xoff: xoff + PT],
                              xc16[:, xslot * PT: xslot * PT + PT])
                        ldone[("xc", xslot)] = p.close(f"oxc{xslot}")
                    else:
                        p.waits("scalar", tev)
                        p.dma(f"oz{zslot}", "scalar",
                              z_o[(m_abs - NKI) * 128:(m_abs - NKI + 1) * 128, xoff: xoff + PT],
                              z16[:, zslot * PT:(zslot + 1) * PT])
                        ldone[("z", zslot)] = p.close(f"oz{zslot}")
                    if mc == 3:
                        wfree[par] = p.mark("tensor")
                ldone["t_cons"] = p.mark("tensor")
            for name in list(p.ch):
                if p.ch[name]["open"]:
                    p.close(name)
            p.drain_all_channels("sync")
            p.barrier()

        if mid_cb is not None:
            mid_cb()
        with contextlib.ExitStack() as e2:
            sb2 = lambda name, shape, dt=F32: e2.enter_context(nc.sbuf_tensor(PF + "p2" + name, shape, dt))[:]
            pst2 = lambda name, shape, dt=F32: e2.enter_context(nc.psum_tensor(PF + name, shape, dt))[:]
            xpT = sb2("xpT", [128, NKI * 160], BF16)
            xcr = sb2("xcr", [128, 2 * 512], BF16)
            xd16 = sb2("xd16", [128, 2 * 512], BF16)
            xd16b = sb2("xd16b", [32, 2 * 512], BF16)
            psD = pst2("p2psD", [128, 512])
            psD2 = pst2("p2psD2", [32, 512])
            for kk in range(NKI):
                p.dma("res2", "sync", xpT[:, kk * 160:(kk + 1) * 160],
                      w_xp_b.ap()[kk * 128:(kk + 1) * 128, :])
            tok_res2 = p.close("res2")
            p.waits("tensor", tok_res2)
            ldone = {}
            for piece in range(TPC // 512):
                xoff = piece * 512
                psl = piece % 2
                for k in range(NKI):
                    ksl = k % 2
                    p.waits("sync", ldone.get(("xcr", ksl)))
                    p.dma("xcl", "sync", xcr[:, ksl * 512:(ksl + 1) * 512],
                          xc_o[k * 128:(k + 1) * 128, xoff: xoff + 512])
                    tok_xc = p.close("xcl")
                    p.waits("tensor", tok_xc, ldone.get("evac"))
                    p.run("tensor", lambda tt, o=psD[:, 0:512], w=xpT[:, k * 160: k * 160 + 128],
                          r=xcr[:, ksl * 512:(ksl + 1) * 512], st=(k == 0), e_=(k == NKI - 1):
                          tt.matmul(o, w, r, start=st, stop=e_))
                    p.run("tensor", lambda tt, o=psD2[:, 0:512], w=xpT[:, k * 160 + 128: k * 160 + 160],
                          r=xcr[:, ksl * 512:(ksl + 1) * 512], st=(k == 0), e_=(k == NKI - 1):
                          tt.matmul(o, w, r, start=st, stop=e_))
                    ldone[("xcr", ksl)] = p.mark("tensor")
                tok_mm = ldone[("xcr", (NKI - 1) % 2)]
                p.waits("scalar", tok_mm, ldone.get(("xd", psl)))
                p.run("scalar", lambda sc, o=xd16[:, psl * 512:(psl + 1) * 512], i=psD:
                      sc.copy(o, i[:, 0:512]))
                tok_xd = p.run("scalar", lambda sc, o=xd16b[:, psl * 512:(psl + 1) * 512], i=psD2:
                               sc.copy(o, i[:, 0:512]), inc=True)
                ldone["evac"] = tok_xd
                p.waits("sync", tok_xd)
                p.dma(f"oxd{psl}", "sync", xd_o[0:128, xoff: xoff + 512],
                      xd16[:, psl * 512:(psl + 1) * 512])
                p.dma(f"oxd{psl}", "sync", xd_o[128:160, xoff: xoff + 512],
                      xd16b[:, psl * 512:(psl + 1) * 512])
                ldone[("xd", psl)] = p.close(f"oxd{psl}")

        if finalize:
            for name in list(p.ch):
                if p.ch[name]["open"]:
                    p.close(name)
            p.drain_all_channels("sync")
            block = es.enter_context(nc.Block())
            p.finalize(block)
    return nc



def build_scan(nc=None, p=None, io=None, finalize=True):
    if nc is None:
        nc = bass.Bass("TRN2", target_bir_lowering=False, debug=False)
    dram = nc.dram_tensor
    if io is None:
        io = {}
        xc_in = dram("xc_in", [CPC, TOK], BF16, kind="ExternalInput").ap()
        z_in = dram("z_in", [CPC, TOK], BF16, kind="ExternalInput").ap()
        xd_in = dram("xd_in", [160, TOK], BF16, kind="ExternalInput").ap()
        io["wdt_j"] = dram("wdt_j", [DTR, CPC], F32, kind="ExternalInput").ap()
        io["db_j"] = dram("db_j", [128, CPC // 128], F32, kind="ExternalInput").ap()
        io["alog_j"] = dram("alog_j", [CPC, DS], F32, kind="ExternalInput").ap()
        io["dsk_j"] = dram("dsk_j", [128, CPC // 128], F32, kind="ExternalInput").ap()
        io["identb_j"] = dram("identb_j", [128, 128], BF16, kind="ExternalInput").ap()
        yg_ = dram("yg_o", [CPC, TOK], BF16, kind="ExternalOutput").ap()
        io["xc_ap"] = lambda r0, nr, co, ln: xc_in[r0:r0 + nr, co:co + ln]
        io["z_ap"] = lambda r0, nr, co, ln: z_in[r0:r0 + nr, co:co + ln]
        io["xd_ap"] = lambda r0, nr, co, ln: xd_in[r0:r0 + nr, co:co + ln]
        io["yg_ap"] = lambda r0, nr, co, ln: yg_[r0:r0 + nr, co:co + ln]
    wdt_j, db_j, alog_j, dsk_j = io["wdt_j"], io["db_j"], io["alog_j"], io["dsk_j"]
    identb_j = io["identb_j"]
    xc_ap, z_ap, xd_ap, yg_ap = io["xc_ap"], io["z_ap"], io["xd_ap"], io["yg_ap"]

    ND = CPC // 128
    NCT = SCAN_CTS
    RESET_CT = L_ // LC
    PF = "s_"
    es = contextlib.ExitStack()
    with es:
        sb = lambda name, shape, dt=F32: es.enter_context(nc.sbuf_tensor(PF + name, shape, dt))[:]
        pst = lambda name, shape, dt=F32: es.enter_context(nc.psum_tensor(PF + name, shape, dt))[:]
        wdt16 = sb("wdt16", [128, CPC], BF16)
        atmp = sb("atmp", [128, ND * DS])
        acol = sb("acol", [128, ND * DS])
        db = sb("db", [128, ND])
        dsk = sb("dsk", [128, ND])
        dtr16 = sb("dtr16", [128, 2 * LC], BF16)
        Bs = sb("Bs", [128, 3 * LC], BF16)
        Cs = sb("Cs", [128, 3 * LC], BF16)
        xcd = sb("xcd", [128, 2 * ND * LC], BF16)
        zd = sb("zd", [128, 2 * ND * LC], BF16)
        dt32 = sb("dt32", [128, ND * LC])
        dtx = sb("dtx", [128, ND * LC], BF16)
        Pt = sb("Pt", [128, ND * LC])
        Qt = sb("Qt", [128, ND * LC])
        Rt = sb("Rt", [128, ND * LC])
        dA = [sb(f"dA{i}", [128, LC]) for i in range(3)]
        u_all = [sb(f"u_all{i}", [128, ND * LC], BF16) for i in range(2)]
        h8 = sb("h8", [128, 8 * LC], BF16)
        prod_all = sb("prod_all", [128, ND * LC], BF16)
        prod2 = sb("prod2", [128, ND * LC], BF16)
        yacc = [sb(f"yacc{i}", [128, LC], BF16) for i in range(2)]
        identb = sb("identb", [128, 128], BF16)
        t1g = sb("t1g", [128, 2 * LC], BF16)
        szg = sb("szg", [128, 2 * LC], BF16)
        ygd = sb("ygd", [128, ND * LC], BF16)
        state_all = sb("state_all", [128, DS * ND])
        psdt = [pst(f"psdt{i}", [128, 512]) for i in range(2)]
        psy = [pst(f"psy{i}", [128, LC]) for i in range(3)]

        p = p if p is not None else Prog(nc, es)
        p.dma("prep", "gpsimd", wdt16, wdt_j)
        tok_prep = p.close("prep")
        for d in range(ND):
            p.dma("res", "sync", atmp[:, d * DS:(d + 1) * DS],
                  alog_j[d * 128:(d + 1) * 128, :])
        p.dma("res", "sync", db, db_j)
        p.dma("res", "sync", dsk, dsk_j)
        p.dma("res", "sync", identb, identb_j)
        tok_res = p.close("res")
        p.waits("scalar", tok_res)
        tok_e = p.run("scalar", lambda sc: sc.activation(acol, atmp, AF.Exp), inc=True)
        p.waits("vector", tok_e)
        tok_a = p.run("vector", lambda v: v.tensor_scalar(atmp, acol, -1.0, None, AL.mult), inc=True)
        for e in ENGS:
            p.waits(e, tok_prep, tok_res, tok_a)
        p.barrier()

        prev_cons = {}
        slotS = [None, None, None]
        hcons = [None, None]
        bcons = [None, None, None]
        ucons = [None, None]
        pcons = [None, None]
        ygate_read = [None] * 3
        oy_tok = [None] * ND
        for ct in range(NCT):
            co = ct * LC
            pc = ct % 2
            dsl = lambda d: slice(d * LC, (d + 1) * LC)
            xsl = lambda d: slice((pc * ND + d) * LC, (pc * ND + d + 1) * LC)
            for tk in (prev_cons.get(("ld", pc)) or []):
                p.waits("sync", tk)
            p.dma(f"dtr{pc}", "sync", dtr16[:, pc * LC:(pc + 1) * LC], xd_ap(0, DTR, co, LC))
            tok_dtr = p.close(f"dtr{pc}")
            for d in range(ND):
                p.dma(f"xcz{pc}", "sync", xcd[:, xsl(d)], xc_ap(d * 128, 128, co, LC))
                p.dma(f"xcz{pc}", "sync", zd[:, xsl(d)], z_ap(d * 128, 128, co, LC))
            tok_xcz = p.close(f"xcz{pc}")

            p.waits("tensor", tok_dtr, prev_cons.get("dtpre"))
            for d in range(ND):
                for h in range(LC // 512):
                    bank = (d * (LC // 512) + h) % 2
                    p.waits("tensor", prev_cons.get(("psdt", bank)))
                    t_ = p.run("tensor", lambda tt, o=psdt[bank][:, 0:512],
                               w=wdt16[:, d * 128:(d + 1) * 128],
                               r=dtr16[:, pc * LC + h * 512: pc * LC + (h + 1) * 512]:
                               tt.matmul(o, w, r, start=True, stop=True), inc=True)
                    p.waits("scalar", t_)
                    s1 = p.run("scalar", lambda sc, o=Pt[:, d * LC + h * 512: d * LC + (h + 1) * 512],
                               i=psdt[bank][:, 0:512],
                               b=db[:, d:d + 1]: sc.add(o, i, b), inc=True)
                    prev_cons[("psdt", bank)] = s1
            for d in range(ND):
                p.run("scalar", lambda sc, o=Qt[:, dsl(d)], i=Pt[:, dsl(d)]:
                      sc.activation(o, i, AF.Abs))
            t2_ = p.mark("scalar")
            p.waits("vector", t2_)
            for d in range(ND):
                p.run("vector", lambda v, o=Rt[:, dsl(d)], i=Qt[:, dsl(d)]:
                      v.tensor_scalar(o, i, -1.0, None, AL.mult))
            t3_ = p.mark("vector")
            p.waits("scalar", t3_)
            for d in range(ND):
                p.run("scalar", lambda sc, o=Qt[:, dsl(d)], i=Rt[:, dsl(d)]:
                      sc.activation(o, i, AF.Exp))
            t4_ = p.mark("scalar")
            p.waits("vector", t4_)
            for d in range(ND):
                p.run("vector", lambda v, o=Rt[:, dsl(d)], i=Qt[:, dsl(d)]:
                      v.tensor_scalar(o, i, 1.0, None, AL.add))
            t5_ = p.mark("vector")
            p.waits("scalar", t5_)
            for d in range(ND):
                p.run("scalar", lambda sc, o=Qt[:, dsl(d)], i=Rt[:, dsl(d)]:
                      sc.activation(o, i, AF.Ln))
            t6_ = p.mark("scalar")
            p.waits("vector", t6_)
            for d in range(ND):
                p.run("vector", lambda v, o=Rt[:, dsl(d)], i=Pt[:, dsl(d)]:
                      v.tensor_scalar(o, i, 0.0, None, AL.max))
            for d in range(ND):
                p.run("vector", lambda v, o=dt32[:, dsl(d)], a=Qt[:, dsl(d)], b=Rt[:, dsl(d)]:
                      v.tensor_tensor(o, a, b, AL.add))
            p.waits("vector", tok_xcz)
            for d in range(ND):
                p.run("vector", lambda v, o=dtx[:, dsl(d)], a=dt32[:, dsl(d)], b=xcd[:, xsl(d)]:
                      v.tensor_tensor(o, a, b, AL.mult))
            tok_dt = p.mark("vector")
            prev_cons["dtpre"] = tok_dt
            p.waits("scalar", tok_dt)

            for s_ in range(DS):
                bs = s_ % 3
                usl = s_ % 2
                ysl = s_ % 2
                hbase = (s_ % 2) * ND
                p.waits("sync", bcons[bs])
                p.dma(f"bc{bs}", "sync", Bs[:, bs * LC:(bs + 1) * LC],
                      xd_ap(DTR + s_, 1, co, LC).partition_broadcast(128))
                p.dma(f"bc{bs}", "sync", Cs[:, bs * LC:(bs + 1) * LC],
                      xd_ap(DTR + DS + s_, 1, co, LC).partition_broadcast(128))
                tok_bc = p.close(f"bc{bs}")
                p.waits("vector", ucons[usl], tok_bc)
                tu = p.run("vector", lambda v, o=u_all[usl], a=dtx,
                           b=Bs[:, bs * LC:(bs + 1) * LC].unsqueeze(1).broadcast_to([128, ND, LC]):
                           v.tensor_tensor(o.rearrange("p (n l) -> p n l", l=LC), a.rearrange("p (n l) -> p n l", l=LC), b, AL.mult), inc=True)
                vlast = None
                for d in range(ND):
                    step = s_ * ND + d
                    sl = step % 3
                    hsl = hbase + d
                    p.waits("scalar", slotS[sl])
                    tda = p.run("scalar", lambda sc, o=dA[sl], i=dt32[:, dsl(d)],
                                a=atmp[:, d * DS + s_: d * DS + s_ + 1]:
                                sc.activation(o, i, AF.Exp, scale=a), inc=True)
                    if ct == 0 or ct == RESET_CT:
                        init = 0.0
                    else:
                        init = state_all[:, s_ * ND + d: s_ * ND + d + 1]
                    p.waits("vector", tda, tu, hcons[s_ % 2])
                    vlast = p.run("vector", lambda v, o=h8[:, hsl * LC:(hsl + 1) * LC],
                                  a=dA[sl], b=u_all[usl][:, d * LC:(d + 1) * LC], i=init:
                                  v.tensor_tensor_scan(o, a, b, i, AL.mult, AL.add), inc=True)
                    slotS[sl] = vlast
                pp = s_ % 2
                prodb = prod_all if pp == 0 else prod2
                p.waits("vector", pcons[pp])
                tpm = p.run("vector", lambda v, o=prodb,
                            a=h8[:, hbase * LC: (hbase + ND) * LC],
                            c=Cs[:, bs * LC:(bs + 1) * LC].unsqueeze(1).broadcast_to([128, ND, LC]):
                            v.tensor_tensor(o.rearrange("p (n l) -> p n l", l=LC),
                                            a.rearrange("p (n l) -> p n l", l=LC), c, AL.mult), inc=True)
                ucons[usl] = tpm
                bcons[bs] = tpm
                p.waits("tensor", tpm)
                if s_ == 0:
                    p.waits("tensor", ygate_read[0], ygate_read[1], ygate_read[2])
                for d in range(3):
                    for hh in range(LC // 512):
                        tmm = p.run("tensor", lambda tt, o=psy[d][:, hh * 512:(hh + 1) * 512],
                                    w=identb,
                                    r=prodb[:, d * LC + hh * 512: d * LC + (hh + 1) * 512],
                                    st=(s_ == 0), e_=(s_ == DS - 1):
                                    tt.matmul(o, w, r, start=st, stop=e_),
                                    inc=(d == 2 and hh == LC // 512 - 1))
                pcons[pp] = tmm
                if s_ == 0:
                    p.run("vector", lambda v, o=yacc[ysl], i=prodb[:, 3 * LC: 4 * LC]:
                          v.tensor_copy(o, i))
                else:
                    p.run("vector", lambda v, o=yacc[ysl], a=yacc[1 - ysl],
                          b=prodb[:, 3 * LC: 4 * LC]:
                          v.tensor_tensor(o, a, b, AL.add))
                p.waits("scalar", vlast)
                tcap = p.run("scalar", lambda sc, o=state_all[:, s_ * ND: s_ * ND + ND],
                             i=h8.rearrange("p (n l) -> p n l", l=LC)[:, hbase: hbase + ND, LC - 1]:
                             sc.copy(o, i), inc=True)
                hcons[s_ % 2] = tcap

            for d in range(ND):
                zsl = d % 2
                p.waits("scalar", prev_cons.get(("sz", zsl)))
                tsz = p.run("scalar", lambda sc, o=szg[:, zsl * LC:(zsl + 1) * LC],
                            i=zd[:, xsl(d)]: sc.activation(o, i, AF.Silu), inc=True)
                ysrc = psy[d] if d < 3 else yacc[(DS - 1) % 2]
                p.waits("vector", tsz, oy_tok[d], pcons[(DS - 1) % 2])
                tg1 = p.run("vector", lambda v, o=t1g[:, zsl * LC:(zsl + 1) * LC], i=xcd[:, xsl(d)],
                            a=dsk[:, d:d + 1], b=ysrc:
                            v.scalar_tensor_tensor(o, i, a, b, AL.mult, AL.add), inc=(d < 3))
                if d < 3:
                    ygate_read[d] = tg1
                tyg = p.run("vector", lambda v, o=ygd[:, dsl(d)], a=t1g[:, zsl * LC:(zsl + 1) * LC],
                            b=szg[:, zsl * LC:(zsl + 1) * LC]:
                            v.tensor_tensor(o, a, b, AL.mult), inc=True)
                prev_cons[("sz", zsl)] = tyg
                p.waits("sync", tyg)
                p.dma(f"oy{d}", "sync", yg_ap(d * 128, 128, co, LC), ygd[:, dsl(d)])
                oy_tok[d] = p.close(f"oy{d}")
            prev_cons[("ld", pc)] = [p.mark("vector"), p.mark("gpsimd"), p.mark("tensor")]

        if finalize:
            for name in list(p.ch):
                if p.ch[name]["open"]:
                    p.close(name)
            p.drain_all_channels("sync")
            block = es.enter_context(nc.Block())
            p.finalize(block)
    return nc


def build_back(nc=None, p=None, io=None, finalize=True):
    if nc is None:
        nc = bass.Bass("TRN2", target_bir_lowering=False, debug=False)
    dram = nc.dram_tensor
    if io is None:
        io = {}
        yg_in = dram("yg_in", [DI, TPC], BF16, kind="ExternalInput").ap()
        io["ygb_ap"] = lambda r0, nr, xoff, n: yg_in[r0:r0 + nr, xoff:xoff + n]
        for nm, shp in (("x1_in", [D, TPC]), ("u_in", [D, TPC])):
            io[nm] = dram(nm, shp, F32, kind="ExternalInput").ap()
        for nm, shp in (("w_out_T", [DI, D]), ("w_ffp_T", [D, H]), ("w_ffo_T", [DI, D]),
                        ("a_T", [D, E * R]), ("b_T", [E * R, D]), ("wr_T", [D, E]),
                        ("br_r", [1, E]), ("ln2_w_t", [128, NKD]), ("ln2_b_t", [128, NKD]),
                        ("repm_in", [8, 512])):
            io[nm] = dram(nm, shp, F32, kind="ExternalInput").ap()
        io["xo_o"] = dram("xo_o", [D, TPC], F32, kind="ExternalOutput").ap()
    (ygb_ap, x1_in, u_in, w_out_T, w_ffp_T, w_ffo_T, a_T, b_T, wr_T, br_r,
     ln2_w_t, ln2_b_t, repm_in, xo_o) = (
        io["ygb_ap"], io["x1_in"], io["u_in"], io["w_out_T"], io["w_ffp_T"],
        io["w_ffo_T"], io["a_T"], io["b_T"], io["wr_T"], io["br_r"],
        io["ln2_w_t"], io["ln2_b_t"], io["repm_in"], io["xo_o"])

    if "w_out_b" in io:
        (w_out_b, w_ffp_b, w_ffo_b, a_b, b_b, wr_b, br_b) = (
            io["w_out_b"], io["w_ffp_b"], io["w_ffo_b"], io["a_b"], io["b_b"],
            io["wr_b"], io["br_b"])
        prep_here = False
    else:
        w_out_b = dram("k_w_out_b", [DI, D], BF16)
        w_ffp_b = dram("k_w_ffp_b", [D, H], BF16)
        w_ffo_b = dram("k_w_ffo_b", [DI, D], BF16)
        a_b = dram("k_a_b", [D, E * R], BF16)
        b_b = dram("k_b_b", [E * R, D], BF16)
        wr_b = dram("k_wr_b", [D, E], BF16)
        br_b = dram("k_br_b", [1, E], BF16)
        prep_here = True
    v_s = dram("k_v_s", [D, TPC], F32).ap()
    t2_s = dram("k_t2_s", [D, TPC], BF16).ap()
    ff_s = dram("k_ff_s", [DI, TPC], BF16).ap()
    x2_s = dram("k_x2_s", [D, TPC], F32).ap()

    PF = "k_"
    es = contextlib.ExitStack()
    with es:
        sb = lambda name, shape, dt=F32: es.enter_context(nc.sbuf_tensor(PF + name, shape, dt))[:]
        pst = lambda name, shape, dt=F32: es.enter_context(nc.psum_tensor(PF + name, shape, dt))[:]
        if p is None:
            p = Prog(nc, es)
        if prep_here:
            p.dma("prep", "gpsimd", w_out_b.ap(), w_out_T)
            p.dma("prep", "gpsimd", w_ffp_b.ap(), w_ffp_T)
            p.dma("prep", "gpsimd", w_ffo_b.ap(), w_ffo_T)
            p.dma("prep", "gpsimd", a_b.ap(), a_T)
            p.dma("prep", "gpsimd", b_b.ap(), b_T)
            p.dma("prep", "gpsimd", wr_b.ap(), wr_T)
            p.dma("prep", "gpsimd", br_b.ap(), br_r)
            tok_prep = p.close("prep")
        else:
            tok_prep = io["prepk_tok"]

        PT1 = 1024
        with contextlib.ExitStack() as e1:
            sb1 = lambda name, shape, dt=F32: e1.enter_context(nc.sbuf_tensor(PF + "p1" + name, shape, dt))[:]
            pst1 = lambda name, shape, dt=F32: e1.enter_context(nc.psum_tensor(PF + name, shape, dt))[:]
            ygt = sb1("ygt", [128, NKI * PT1], BF16)
            ut = sb1("ut", [128, 2 * PT1])
            vt = sb1("vt", [128, 2 * PT1])
            vb = sb1("vb", [128, 2 * PT1], BF16)
            vrb = sb1("vrb", [128, 2 * 512])
            sqb = sb1("sqb", [128, 2 * PT1], BF16)
            t2t = sb1("t2t", [128, 2 * PT1], BF16)
            tmpA = sb1("tmpA", [128, PT1])
            tmpB = sb1("tmpB", [128, PT1])
            wbuf = [sb1(f"wbuf{b}", [128, NKI * 512], BF16) for b in range(2)]
            stats = {nm: sb1(nm, [1, PT1]) for nm in
                     ("m", "q", "var", "sd", "r", "nmr", "t1", "t2x")}
            ones_c = e1.enter_context(nc.sbuf_tensor(PF + "p1ones_c", [128, 1], BF16))
            ones_r = e1.enter_context(nc.sbuf_tensor(PF + "p1ones_r", [1, 128], F32))
            ln2w = sb1("ln2w", [128, NKD])
            ln2b = sb1("ln2b", [128, NKD])
            psMM = [pst1("p1psMM", [128, 512]), pst1("p1psMM2", [128, 512])]
            psA = pst1("p1psA", [1, 1024])
            psB = pst1("p1psB", [1, 1024])
            psR = pst1("p1psR", [128, 512])
            psN = pst1("p1psN", [128, 512])
            tok_c = p.run("vector", lambda v: v.memset(ones_c[:], 1.0), inc=True)
            tok_c2 = p.run("vector", lambda v: v.memset(ones_r[:], 1.0), inc=True)
            p.dma("res", "sync", ln2w, ln2_w_t)
            p.dma("res", "sync", ln2b, ln2_b_t)
            tok_res = p.close("res")
            for e in ENGS:
                p.waits(e, tok_prep, tok_res, tok_c, tok_c2)
            wfree = [None, None]
            evac = [None, None]
            ldone = {}
            for t in range(BACK_SPAN // PT1):
                xoff = t * PT1
                p.waits("sync", ldone.get("yg_cons"))
                for k in range(NKI):
                    p.dma("yg", "sync", ygt[:, k * PT1:(k + 1) * PT1],
                          ygb_ap(k * 128, 128, xoff, PT1))
                tok_yg = p.close("yg")
                p.waits("tensor", tok_yg)
                for m in range(NKD):
                    par = m % 2
                    g, mc = m // 4, m % 4
                    if mc == 0:
                        p.waits("sync", wfree[par_g := (g % 2)])
                        for k in range(NKI):
                            p.dma(f"w{par_g}", "sync", wbuf[par_g][:, k * 512:(k + 1) * 512],
                                  w_out_b.ap()[k * 128:(k + 1) * 128, g * 512:(g + 1) * 512])
                        tok_w = p.close(f"w{par_g}")
                        p.waits("tensor", tok_w)
                    usl = m % 2
                    for _tk in (ldone.get(("u", usl)) or []):
                        p.waits("scalar", _tk)
                    p.dma("u", "scalar", ut[:, usl * PT1:(usl + 1) * PT1],
                          u_in[m * 128:(m + 1) * 128, xoff: xoff + PT1])
                    tok_u = p.close("u")
                    vsl = m % 2
                    for h in range(PT1 // 512):
                        bank = (m * (PT1 // 512) + h) % 2
                        ps = psMM[bank]
                        p.waits("tensor", evac[bank])
                        for k in range(NKI):
                            p.run("tensor", lambda tt, o=ps[:, 0:512],
                                  w=wbuf[g % 2][:, k * 512 + (m % 4) * 128: k * 512 + (m % 4) * 128 + 128],
                                  r=ygt[:, k * PT1 + h * 512: k * PT1 + (h + 1) * 512],
                                  st=(k == 0), e_=(k == NKI - 1):
                                  tt.matmul(o, w, r, start=st, stop=e_))
                        tmm = p.mark("tensor")
                        p.waits("vector", tmm, tok_u)
                        tv = p.run("vector", lambda v, o=vt[:, vsl * PT1 + h * 512: vsl * PT1 + (h + 1) * 512],
                                   a=ut[:, usl * PT1 + h * 512: usl * PT1 + (h + 1) * 512],
                                   b=ps[:, 0:512]: v.tensor_tensor(o, a, b, AL.add), inc=True)
                        evac[bank] = tv
                    p.waits("scalar", tv)
                    p.dma("v", "scalar", v_s[m * 128:(m + 1) * 128, xoff: xoff + PT1],
                          vt[:, vsl * PT1:(vsl + 1) * PT1])
                    vstok = p.close("v")
                    ldone[("u", usl)] = [tv, vstok]
                    p.waits("scalar", tv)
                    tvb = p.run("scalar", lambda sc, o=vb[:, vsl * PT1:(vsl + 1) * PT1],
                                i=vt[:, vsl * PT1:(vsl + 1) * PT1]: sc.copy(o, i), inc=True)
                    tsq = p.run("scalar", lambda sc, o=sqb[:, vsl * PT1:(vsl + 1) * PT1],
                                i=vt[:, vsl * PT1:(vsl + 1) * PT1]: sc.activation(o, i, AF.Square), inc=True)
                    p.waits("tensor", tvb, tsq)
                    for h2 in range(PT1 // 512):
                        p.run("tensor", lambda tt, o=psA[0:1, h2 * 512:(h2 + 1) * 512], w=ones_c[:],
                              r=vb[:, vsl * PT1 + h2 * 512: vsl * PT1 + (h2 + 1) * 512],
                              st=(m == 0), e_=(m == NKD - 1):
                              tt.matmul(o, w, r, start=st, stop=e_, skip_group_check=True))
                        p.run("tensor", lambda tt, o=psB[0:1, h2 * 512:(h2 + 1) * 512], w=ones_c[:],
                              r=sqb[:, vsl * PT1 + h2 * 512: vsl * PT1 + (h2 + 1) * 512],
                              st=(m == 0), e_=(m == NKD - 1):
                              tt.matmul(o, w, r, start=st, stop=e_, skip_group_check=True))
                    ldone[("v", vsl)] = p.mark("tensor")
                    if mc == 3:
                        wfree[g % 2] = ldone[("v", vsl)]
                    if m == NKD - 1:
                        ldone["yg_cons"] = ldone[("v", vsl)]
                tok_mm = p.mark("tensor")
                st_ = stats
                p.waits("vector", tok_mm)
                p.run("vector", lambda v: v.tensor_scalar(st_["m"][:, 0:PT1], psA[0:1, 0:PT1], 1.0 / D, None, AL.mult))
                p.run("vector", lambda v: v.tensor_scalar(st_["q"][:, 0:PT1], psB[0:1, 0:PT1], 1.0 / D, None, AL.mult))
                p.run("vector", lambda v: v.tensor_tensor(st_["t1"][:, 0:PT1], st_["m"][:, 0:PT1], st_["m"][:, 0:PT1], AL.mult))
                p.run("vector", lambda v: v.tensor_tensor(st_["var"][:, 0:PT1], st_["q"][:, 0:PT1], st_["t1"][:, 0:PT1], AL.subtract))
                tv1 = p.run("vector", lambda v: v.tensor_scalar(st_["t2x"][:, 0:PT1], st_["var"][:, 0:PT1], EPS, None, AL.add), inc=True)
                p.waits("scalar", tv1)
                ts1 = p.run("scalar", lambda sc: sc.activation(st_["sd"][:, 0:PT1], st_["t2x"][:, 0:PT1], AF.Sqrt), inc=True)
                p.waits("vector", ts1)
                p.run("vector", lambda v: v.reciprocal(st_["r"][:, 0:PT1], st_["sd"][:, 0:PT1]))
                p.run("vector", lambda v: v.tensor_tensor(st_["t1"][:, 0:PT1], st_["m"][:, 0:PT1], st_["r"][:, 0:PT1], AL.mult))
                tv2 = p.run("vector", lambda v: v.tensor_scalar(st_["nmr"][:, 0:PT1], st_["t1"][:, 0:PT1], -1.0, None, AL.mult), inc=True)
                p.waits("tensor", tv2)
                p.waits("sync", vstok)
                for h in range(PT1 // 512):
                    p.run("tensor", lambda tt, o=psR[:, 0:512], r=st_["r"][0:1, h * 512:(h + 1) * 512]:
                          tt.matmul(o, ones_r[:], r, start=True, stop=True))
                    p.run("tensor", lambda tt, o=psN[:, 0:512], r=st_["nmr"][0:1, h * 512:(h + 1) * 512]:
                          tt.matmul(o, ones_r[:], r, start=True, stop=True))
                    trp = p.mark("tensor")
                    for m in range(NKD):
                        vsl2 = m % 2
                        p.waits("sync", ldone.get(("vr", vsl2)))
                        p.dma("vr", "sync", vrb[:, vsl2 * 512:(vsl2 + 1) * 512],
                              v_s[m * 128:(m + 1) * 128, xoff + h * 512: xoff + (h + 1) * 512])
                        tok_vr = p.close("vr")
                        p.waits("vector", trp, tok_vr, ldone.get(("t2s", vsl2)))
                        p.run("vector", lambda v, o=tmpA[:, 0:512], a=vrb[:, vsl2 * 512:(vsl2 + 1) * 512],
                              b=psR[:, 0:512]: v.tensor_tensor(o, a, b, AL.mult))
                        p.run("vector", lambda v, o=tmpB[:, 0:512], a=tmpA[:, 0:512],
                              b=psN[:, 0:512]: v.tensor_tensor(o, a, b, AL.add))
                        tt2 = p.run("vector", lambda v, o=t2t[:, vsl2 * 512:(vsl2 + 1) * 512], i=tmpB[:, 0:512],
                                    w=ln2w[:, m:m + 1], b=ln2b[:, m:m + 1]:
                                    v.tensor_scalar(o, i, w, b, AL.mult, AL.add), inc=True)
                        ldone[("vr", vsl2)] = tt2
                        p.waits("sync", tt2)
                        p.dma(f"t2{vsl2}", "sync",
                              t2_s[m * 128:(m + 1) * 128, xoff + h * 512: xoff + (h + 1) * 512],
                              t2t[:, vsl2 * 512:(vsl2 + 1) * 512])
                        ldone[("t2s", vsl2)] = p.close(f"t2{vsl2}")
                    p.waits("tensor", p.mark("vector"))
        for name in list(p.ch):
            if p.ch[name]["open"]:
                p.close(name)
        p.drain_all_channels("sync")
        p.barrier()

        PT2 = 2048
        with contextlib.ExitStack() as e2:
            sb2 = lambda name, shape, dt=F32: e2.enter_context(nc.sbuf_tensor(PF + "p2" + name, shape, dt))[:]
            pst2 = lambda name, shape, dt=F32: e2.enter_context(nc.psum_tensor(PF + name, shape, dt))[:]
            t2r = sb2("t2r", [128, NKD * PT2], BF16)
            a16 = sb2("a16", [128, 2 * 512], BF16)
            sg = sb2("sg", [128, 2 * 512], BF16)
            ff16 = sb2("ff16", [128, 2 * 512], BF16)
            wbufa = [sb2(f"wbufa{b}", [128, NKD * 512], BF16) for b in range(2)]
            wbufg = [sb2(f"wbufg{b}", [128, NKD * 512], BF16) for b in range(2)]
            psMM = [pst2("p2psMM", [128, 512]), pst2("p2psMM2", [128, 512])]
            evac = [None, None]
            wfree = [None, None, None, None]
            ldone = {}
            for t in range(BACK_SPAN // PT2):
                xoff = t * PT2
                for k in range(NKD):
                    p.waits("scalar", ldone.get(("t2", k)))
                    p.dma("t2r", "scalar", t2r[:, k * PT2:(k + 1) * PT2],
                          t2_s[k * 128:(k + 1) * 128, xoff: xoff + PT2])
                tok_t2 = p.close("t2r")
                for m in range(NKI):
                    gw, mw = m // 4, m % 4
                    par = gw % 2
                    if mw == 0:
                        p.waits("sync", wfree[par])
                        for k in range(NKD):
                            p.dma(f"wa{par}", "sync", wbufa[par][:, k * 512:(k + 1) * 512],
                                  w_ffp_b.ap()[k * 128:(k + 1) * 128, gw * 512:(gw + 1) * 512])
                        for k in range(NKD):
                            p.dma(f"wa{par}", "sync", wbufg[par][:, k * 512:(k + 1) * 512],
                                  w_ffp_b.ap()[k * 128:(k + 1) * 128, DI + gw * 512: DI + (gw + 1) * 512])
                        tok_w = p.close(f"wa{par}")
                    p.waits("tensor", tok_w, tok_t2)
                    for h in range(PT2 // 512):
                        bank = h % 2
                        psa = psMM[bank]
                        psg = psMM[1 - bank]
                        p.waits("tensor", evac[0], evac[1])
                        for k in range(NKD):
                            p.run("tensor", lambda tt, o=psa[:, 0:512],
                                  w=wbufa[par][:, k * 512 + mw * 128: k * 512 + mw * 128 + 128],
                                  r=t2r[:, k * PT2 + h * 512: k * PT2 + (h + 1) * 512],
                                  st=(k == 0), e_=(k == NKD - 1): tt.matmul(o, w, r, start=st, stop=e_))
                        tma = p.mark("tensor")
                        for k in range(NKD):
                            p.run("tensor", lambda tt, o=psg[:, 0:512],
                                  w=wbufg[par][:, k * 512 + mw * 128: k * 512 + mw * 128 + 128],
                                  r=t2r[:, k * PT2 + h * 512: k * PT2 + (h + 1) * 512],
                                  st=(k == 0), e_=(k == NKD - 1): tt.matmul(o, w, r, start=st, stop=e_))
                        tmg = p.mark("tensor")
                        hsl = h % 2
                        p.waits("scalar", tma, tmg, ldone.get(("ff", hsl)))
                        ta = p.run("scalar", lambda sc, o=a16[:, hsl * 512:(hsl + 1) * 512], i=psa:
                                   sc.copy(o, i[:, 0:512]), inc=True)
                        tg = p.run("scalar", lambda sc, o=sg[:, hsl * 512:(hsl + 1) * 512], i=psg:
                                   sc.activation(o, i[:, 0:512], AF.Silu), inc=True)
                        evac[0] = ta
                        evac[1] = tg
                        p.waits("vector", tg)
                        tf = p.run("vector", lambda v, o=ff16[:, hsl * 512:(hsl + 1) * 512],
                                   a=a16[:, hsl * 512:(hsl + 1) * 512], b=sg[:, hsl * 512:(hsl + 1) * 512]:
                                   v.tensor_tensor(o, a, b, AL.mult), inc=True)
                        p.waits("scalar", tf)
                        p.dma(f"ff{hsl}", "scalar", ff_s[m * 128:(m + 1) * 128, xoff + h * 512: xoff + (h + 1) * 512],
                              ff16[:, hsl * 512:(hsl + 1) * 512])
                        ldone[("ff", hsl)] = p.close(f"ff{hsl}")
                    if mw == 3:
                        wfree[par] = p.mark("tensor")
        for name in list(p.ch):
            if p.ch[name]["open"]:
                p.close(name)
        p.drain_all_channels("sync")
        p.barrier()

        PT3 = 1024
        with contextlib.ExitStack() as e3:
            sb3 = lambda name, shape, dt=F32: e3.enter_context(nc.sbuf_tensor(PF + "p3" + name, shape, dt))[:]
            pst3 = lambda name, shape, dt=F32: e3.enter_context(nc.psum_tensor(PF + name, shape, dt))[:]
            ffr = sb3("ffr", [128, NKI * PT3], BF16)
            x1t = sb3("x1t", [128, 2 * PT3])
            vtt = sb3("vtt", [128, 2 * PT3])
            e32 = sb3("e32", [128, 2 * PT3])
            x2t = sb3("x2t", [128, 2 * PT3])
            wbuf = [sb3(f"wbuf{b}", [128, NKI * 512], BF16) for b in range(2)]
            psMM = [pst3("p3psMM", [128, 512]), pst3("p3psMM2", [128, 512])]
            evac = [None, None]
            wfree = [None, None]
            ldone = {}
            for t in range(BACK_SPAN // PT3):
                xoff = t * PT3
                for k in range(NKI):
                    p.waits("sync", ldone.get(("ff", k)))
                    p.dma("ffr", "sync", ffr[:, k * PT3:(k + 1) * PT3],
                          ff_s[k * 128:(k + 1) * 128, xoff: xoff + PT3])
                tok_ff = p.close("ffr")
                for m in range(NKD):
                    gw, mw = m // 4, m % 4
                    par = gw % 2
                    if mw == 0:
                        p.waits("sync", wfree[par])
                        for k in range(NKI):
                            p.dma(f"w{par}", "sync", wbuf[par][:, k * 512:(k + 1) * 512],
                                  w_ffo_b.ap()[k * 128:(k + 1) * 128, gw * 512:(gw + 1) * 512])
                        tok_w = p.close(f"w{par}")
                    msl = m % 2
                    p.waits("scalar", ldone.get(("x2", msl)))
                    p.dma("x1l", "scalar", x1t[:, msl * PT3:(msl + 1) * PT3],
                          x1_in[m * 128:(m + 1) * 128, xoff: xoff + PT3])
                    p.dma("x1l", "scalar", vtt[:, msl * PT3:(msl + 1) * PT3],
                          v_s[m * 128:(m + 1) * 128, xoff: xoff + PT3])
                    tok_x1 = p.close("x1l")
                    p.waits("tensor", tok_w, tok_ff)
                    tx2 = None
                    for h in range(PT3 // 512):
                        bank = h % 2
                        ps = psMM[bank]
                        p.waits("tensor", evac[bank])
                        for k in range(NKI):
                            p.run("tensor", lambda tt, o=ps[:, 0:512],
                                  w=wbuf[par][:, k * 512 + mw * 128: k * 512 + mw * 128 + 128],
                                  r=ffr[:, k * PT3 + h * 512: k * PT3 + (h + 1) * 512],
                                  st=(k == 0), e_=(k == NKI - 1): tt.matmul(o, w, r, start=st, stop=e_))
                        tmm = p.mark("tensor")
                        p.waits("vector", tmm, tok_x1)
                        p.run("vector", lambda v, o=e32[:, msl * PT3 + h * 512: msl * PT3 + (h + 1) * 512],
                              a=vtt[:, msl * PT3 + h * 512: msl * PT3 + (h + 1) * 512],
                              b=ps[:, 0:512]: v.tensor_tensor(o, a, b[:, 0:512], AL.add))
                        tx2 = p.run("vector", lambda v, o=x2t[:, msl * PT3 + h * 512: msl * PT3 + (h + 1) * 512],
                                    a=e32[:, msl * PT3 + h * 512: msl * PT3 + (h + 1) * 512],
                                    b=x1t[:, msl * PT3 + h * 512: msl * PT3 + (h + 1) * 512]:
                                    v.tensor_tensor(o, a, b, AL.add), inc=True)
                        evac[bank] = tx2
                    p.waits("sync", tx2)
                    p.dma(f"x2{msl}", "sync", x2_s[m * 128:(m + 1) * 128, xoff: xoff + PT3],
                          x2t[:, msl * PT3:(msl + 1) * PT3])
                    ldone[("x2", msl)] = p.close(f"x2{msl}")
                    if mw == 3:
                        wfree[par] = p.mark("tensor")
        for name in list(p.ch):
            if p.ch[name]["open"]:
                p.close(name)
        p.drain_all_channels("sync")
        p.barrier()

        N = 512
        with contextlib.ExitStack() as e4:
            sb4 = lambda name, shape, dt=F32: e4.enter_context(nc.sbuf_tensor(PF + "p4" + name, shape, dt))[:]
            pst4 = lambda name, shape, dt=F32: e4.enter_context(nc.psum_tensor(PF + name, shape, dt))[:]
            al = {}
            al["xbt"] = sb4("xbt", [128, 2 * N], BF16)
            al["sqt"] = sb4("sqt", [128, 2 * N], BF16)
            al["ln_n"] = N
            al["tmp32"] = sb4("tmp32", [128, N])
            al["tmp32b"] = sb4("tmp32b", [128, N])
            for nm in ("st_m", "st_q", "st_var", "st_sd", "st_r", "st_nmr", "st_t1", "st_t2"):
                al[nm] = sb4(nm, [1, N])
            al["ones_col16"] = e4.enter_context(nc.sbuf_tensor(PF + "p4ones_c", [128, 1], BF16))
            al["ones_row32"] = e4.enter_context(nc.sbuf_tensor(PF + "p4ones_r", [1, 128], F32))
            al["ones_row16"] = sb4("ones_r16", [1, 128], BF16)
            al["ident"] = e4.enter_context(nc.sbuf_tensor(PF + "p4ident", [128, 128], BF16))
            zeros128 = e4.enter_context(nc.sbuf_tensor(PF + "p4zeros", [128, 128], BF16))
            al["lgs"] = sb4("lgs", [128, 32])
            al["mx"] = sb4("mx", [128, 4])
            al["nmx"] = sb4("nmx", [128, 4])
            al["sm"] = sb4("sm", [128, 4])
            al["wsc"] = sb4("wsc", [128, 4])
            al["w0"] = sb4("w0", [128, 4])
            al["exps"] = sb4("exps", [128, 32])
            al["maskb"] = sb4("maskb", [128, 32], BF16)
            al["maskT"] = sb4("maskT", [8, N], BF16)
            al["repm"] = sb4("repm", [8, 512], BF16)
            al["mrep16"] = sb4("mrep16", [128, N], BF16)
            al["h16"] = sb4("h16", [128, N], BF16)
            al["hm16"] = sb4("hm16", [128, 4 * N], BF16)
            al["ffn_n"] = N
            al["aT"] = sb4("aT", [128, NKD * 4 * 128], BF16)
            al["bT"] = sb4("bT", [128, 4 * NKD * 128], BF16)
            al["wrT"] = sb4("wrT", [128, NKD * 8], BF16)
            al["brT"] = e4.enter_context(nc.sbuf_tensor(PF + "p4brT", [1, 8], BF16))
            al["psA"] = pst4("p4psA", [1, 512])
            al["psB"] = pst4("p4psB", [1, 512])
            al["psR"] = pst4("p4psR", [128, 512])
            al["psN"] = pst4("p4psN", [128, 512])
            al["psS"] = pst4("p4psS", [128, 512])
            al["psRt"] = al["psS"][:, 0:32]
            al["psM"] = al["psR"]
            al["psH"] = al["psN"]
            psX4 = pst4("p4psX", [128, 512])
            al["psT"] = psX4[0:8, :].bitcast(BF16)
            x2b = sb4("x2b", [128, NKD * N])
            v2b = sb4("v2b", [128, NKD * N], BF16)
            xob = sb4("xob", [128, 4 * N])
            tok_z = p.run("vector", lambda v: v.memset(zeros128[:], 0.0), inc=True)
            p.waits("gpsimd", tok_z)
            tok_id = p.run("gpsimd", lambda g: g.affine_select(
                out=al["ident"][:], in_=zeros128[:], compare_op=AL.not_equal,
                fill=1.0, base=0, pattern=[[-1, 128]], channel_multiplier=1), inc=True)
            tok_c1 = p.run("gpsimd", lambda g: g.memset(al["ones_col16"][:], 1.0), inc=True)
            tok_c2 = p.run("gpsimd", lambda g: g.memset(al["ones_row32"][:], 1.0), inc=True)
            tok_c3 = p.run("gpsimd", lambda g: g.memset(al["ones_row16"][:], 1.0), inc=True)
            for k in range(NKD):
                p.dma("res4", "sync", al["aT"][:, k * 512:(k + 1) * 512],
                      a_b.ap()[k * 128:(k + 1) * 128, :])
            for c in range(4):
                p.dma("res4", "sync", al["bT"][:, c * 2048:(c + 1) * 2048],
                      b_b.ap()[c * 128:(c + 1) * 128, :])
            for k in range(NKD):
                p.dma("res4", "sync", al["wrT"][:, k * 8:(k + 1) * 8],
                      wr_b.ap()[k * 128:(k + 1) * 128, :])
            p.dma("res4", "sync", al["brT"][:], br_b.ap())
            tok_res = p.close("res4")
            p.dma("prep4", "gpsimd", al["repm"], repm_in)
            tok_rp = p.close("prep4")
            for e in ENGS:
                p.waits(e, tok_res, tok_rp, tok_id, tok_c1, tok_c2, tok_c3)
            p.barrier()
            oslot = {}
            prev_x2 = None
            for t in range(BACK_SPAN // N):
                xoff = t * N
                x2c = [x2b[:, k * N:(k + 1) * N] for k in range(NKD)]
                v2c = [v2b[:, k * N:(k + 1) * N] for k in range(NKD)]
                xoc = [xob[:, (k % 4) * N:(k % 4 + 1) * N] for k in range(NKD)]
                p.waits("sync", prev_x2)
                for k in range(NKD):
                    p.dma("x2r", "sync", x2c[k], x2_s[k * 128:(k + 1) * 128, xoff: xoff + N])
                tok_x2 = p.close("x2r")
                tok_v2 = emit_ln(p, al, N, x2c, tok_x2, out16c=v2c, tmp32=al["tmp32"])

                def pre_cb(m, oslot=oslot, t=t):
                    return oslot.get((t, m - 4)) or oslot.get((t - 1, m + NKD - 4))

                def post_cb(m, tok, xoff=xoff, t=t, xoc=xoc):
                    p.waits("scalar", tok)
                    p.dma(f"o{m % 4}", "scalar", xo_o[m * 128:(m + 1) * 128, xoff: xoff + N], xoc[m])
                    oslot[(t, m)] = p.close(f"o{m % 4}")

                tok_out = emit_moe(p, al, N, v2c, tok_v2, x2c, tok_x2, xoc, 0.5,
                                   pre_cb=pre_cb, post_cb=post_cb)
                prev_x2 = tok_out

        if finalize:
            for name in list(p.ch):
                if p.ch[name]["open"]:
                    p.close(name)
            p.drain_all_channels("sync")
            block = es.enter_context(nc.Block())
            p.finalize(block)
    return nc



def build_mega():
    from concourse.aot_env import allow_ici_all_to_all
    nc = bass.Bass("TRN2", target_bir_lowering=False, debug=False)
    dram = nc.dram_tensor
    io_f = {}
    for nm, shp in (("x_in", [D, TPC + 3]), ("halo_mask", [128, 3]),
                    ("w_in_T", [D, 2 * DI]), ("w_xp_T", [DI, 160]),
                    ("a_T", [D, E * R]), ("b_T", [E * R, D]), ("wr_T", [D, E]),
                    ("br_r", [1, E]), ("conv_w_t", [128, NKI * 4]),
                    ("conv_b_t", [128, NKI]), ("ln1_w_t", [128, NKD]),
                    ("ln1_b_t", [128, NKD]), ("repm_in", [8, 512])):
        io_f[nm] = dram(nm, shp, F32, kind="ExternalInput").ap()
    io_s = {}
    for nm, shp in (("wdt_j", [DTR, CPC]), ("db_j", [128, CPC // 128]),
                    ("alog_j", [CPC, DS]), ("dsk_j", [128, CPC // 128])):
        io_s[nm] = dram(nm, shp, F32, kind="ExternalInput").ap()
    io_s["identb_j"] = dram("identb_j", [128, 128], BF16, kind="ExternalInput").ap()
    io_b = {}
    for nm, shp in (("w_out_T", [DI, D]), ("w_ffp_T", [D, H]), ("w_ffo_T", [DI, D]),
                    ("ln2_w_t", [128, NKD]), ("ln2_b_t", [128, NKD])):
        io_b[nm] = dram(nm, shp, F32, kind="ExternalInput").ap()
    io_b["a_T"] = io_f["a_T"]; io_b["b_T"] = io_f["b_T"]
    io_b["wr_T"] = io_f["wr_T"]; io_b["br_r"] = io_f["br_r"]
    io_b["repm_in"] = io_f["repm_in"]
    io_b["xo_o"] = dram("xo_o", [D, TPC], F32, kind="ExternalOutput").ap()

    xc_x = dram("xc_x", [DI, TPC], BF16)
    z_x = dram("z_x", [DI, TPC], BF16)
    xd_g = dram("xd_g", [160, TPC], BF16)
    xc_sh = dram("xc_sh", [NC * CPC, TPC], BF16)
    z_sh = dram("z_sh", [NC * CPC, TPC], BF16)
    xd_sh = dram("xd_sh", [NC * 160, TPC], BF16)
    yg_x = dram("yg_x", [NC * CPC, TPC], BF16)
    yg_sh = dram("yg_sh", [NC * CPC, TPC], BF16)
    x1_s = dram("x1_s", [D, TPC], F32)
    u_s = dram("u_s", [D, TPC], F32)

    io_f["xc_o"] = xc_x.ap()
    io_f["z_o"] = z_x.ap()
    io_f["xd_o"] = xd_g.ap()
    io_f["x1_o"] = x1_s.ap()
    io_f["u_o"] = u_s.ap()

    xcs, zs, xds, ygs = xc_sh.ap(), z_sh.ap(), xd_sh.ap(), yg_x.ap()
    io_s["xc_ap"] = lambda r0, nr, co, ln: xcs[(co // TPC) * CPC + r0:(co // TPC) * CPC + r0 + nr, co % TPC: co % TPC + ln]
    io_s["z_ap"] = lambda r0, nr, co, ln: zs[(co // TPC) * CPC + r0:(co // TPC) * CPC + r0 + nr, co % TPC: co % TPC + ln]
    io_s["xd_ap"] = lambda r0, nr, co, ln: xds[(co // TPC) * 160 + r0:(co // TPC) * 160 + r0 + nr, co % TPC: co % TPC + ln]
    io_s["yg_ap"] = lambda r0, nr, co, ln: ygs[(co // TPC) * CPC + r0:(co // TPC) * CPC + r0 + nr, co % TPC: co % TPC + ln]
    ygshr = yg_sh.ap()
    io_b["ygb_ap"] = lambda r0, nr, xoff, n: ygshr[r0:r0 + nr, xoff: xoff + n]
    io_b["x1_in"] = x1_s.ap()
    io_b["u_in"] = u_s.ap()

    es = contextlib.ExitStack()
    with es:
        p = Prog(nc, es)
        kw = {}
        for nm, shp in (("w_out_b", [DI, D]), ("w_ffp_b", [D, H]), ("w_ffo_b", [DI, D]),
                        ("a_b", [D, E * R]), ("b_b", [E * R, D]), ("wr_b", [D, E]),
                        ("br_b", [1, E])):
            kw[nm] = dram("k_" + nm, shp, BF16)
        def post_prep_cb():
            for nm, src in (("w_out_b", "w_out_T"), ("w_ffp_b", "w_ffp_T"),
                            ("w_ffo_b", "w_ffo_T"), ("a_b", "a_T"), ("b_b", "b_T"),
                            ("wr_b", "wr_T"), ("br_b", "br_r")):
                p.dma("prepk", "gpsimd", kw[nm].ap(), io_b[src])
            io_b["prepk_tok"] = p.close("prepk")
        io_f["post_prep_cb"] = post_prep_cb
        io_b.update(kw)
        rg = [list(range(NC))]

        def cc(kind, i, o):
            def f(g, kind=kind, i=i, o=o):
                with allow_ici_all_to_all():
                    return g.collective_compute(kind, AL.bypass, replica_groups=rg,
                                                ins=[i], outs=[o])
            return p.run("gpsimd", f, inc=True)

        def mid_cb():
            for name in list(p.ch):
                if p.ch[name]["open"]:
                    p.close(name)
            p.drain_all_channels("gpsimd")
            cc("AllToAll", xc_x.ap().opt(), xc_sh.ap().opt())
            cc("AllToAll", z_x.ap().opt(), z_sh.ap().opt())

        build_front(nc=nc, p=p, io=io_f, finalize=False, mid_cb=mid_cb)
        for name in list(p.ch):
            if p.ch[name]["open"]:
                p.close(name)
        p.barrier()
        p.drain_all_channels("gpsimd")
        tok_fwd = cc("AllGather", xd_g.ap().opt(), xd_sh.ap().opt())
        for e in ENGS:
            p.waits(e, tok_fwd)
        p.barrier()
        build_scan(nc=nc, p=p, io=io_s, finalize=False)
        for name in list(p.ch):
            if p.ch[name]["open"]:
                p.close(name)
        p.barrier()
        p.drain_all_channels("gpsimd")
        tok_bwd = cc("AllToAll", yg_x.ap().opt(), yg_sh.ap().opt())
        for e in ENGS:
            p.waits(e, tok_bwd)
        p.barrier()
        build_back(nc=nc, p=p, io=io_b, finalize=False)
        for name in list(p.ch):
            if p.ch[name]["open"]:
                p.close(name)
        p.drain_all_channels("sync")
        block = es.enter_context(nc.Block())
        p.finalize(block)
    return nc


def _mega_inputs(inputs):
    front = _front_inputs(inputs)
    wdt = host_f32(inputs["L0_attn_dt_proj_w"]).T.copy()
    db = host_f32(inputs["L0_attn_dt_proj_b"])
    alog = host_f32(inputs["L0_attn_A_log"])
    dsk = host_f32(inputs["L0_attn_D_skip"])
    w_out_T = host_f32(inputs["L0_attn_out_proj_w"]).T.copy()
    w_ffp_T = host_f32(inputs["L0_attn_ff_proj_w"]).T.copy()
    w_ffo_T = host_f32(inputs["L0_attn_ff_out_w"]).T.copy()
    ln2_w_t = colsT(inputs["L0_attn_ln2_w"], NKD)
    ln2_b_t = colsT(inputs["L0_attn_ln2_b"], NKD)
    in_maps = []
    for i in range(NC):
        im = dict(front[i])
        c0 = i * CPC
        im.update(dict(
            wdt_j=np.ascontiguousarray(wdt[:, c0:c0 + CPC]),
            db_j=np.ascontiguousarray(db[c0:c0 + CPC].reshape(CPC // 128, 128).T),
            alog_j=np.ascontiguousarray(alog[c0:c0 + CPC]),
            dsk_j=np.ascontiguousarray(dsk[c0:c0 + CPC].reshape(CPC // 128, 128).T),
            identb_j=np.eye(128, dtype=BF16_NP),
            w_out_T=w_out_T, w_ffp_T=w_ffp_T, w_ffo_T=w_ffo_T,
            ln2_w_t=ln2_w_t, ln2_b_t=ln2_b_t))
        in_maps.append(im)
    return in_maps


def _front_inputs(inputs):
    x = host_f32(inputs["L0_attn_x"]).reshape(TOK, D)
    w_in_T = host_f32(inputs["L0_attn_in_proj_w"]).T.copy()
    w_xp_T = host_f32(inputs["L0_attn_x_proj_w"]).T.copy()
    A = host_f32(inputs["L0_ffn_A"])
    Bw = host_f32(inputs["L0_ffn_B"])
    a_T = np.ascontiguousarray(A.transpose(2, 0, 1).reshape(D, E * R))
    b_T = np.ascontiguousarray(Bw.transpose(0, 2, 1).reshape(E * R, D))
    wr_T = host_f32(inputs["L0_ffn_Wr"]).T.copy()
    br_r = host_f32(inputs["L0_ffn_br"]).reshape(1, E)
    conv_w = host_f32(inputs["L0_attn_conv_w"]).reshape(DI, 4)
    conv_w_t = np.ascontiguousarray(conv_w.reshape(NKI, 128, 4).transpose(1, 0, 2).reshape(128, NKI * 4))
    conv_b_t = colsT(inputs["L0_attn_conv_b"], NKI)
    ln1_w_t = colsT(inputs["L0_attn_ln1_w"], NKD)
    ln1_b_t = colsT(inputs["L0_attn_ln1_b"], NKD)
    repm = np.zeros((8, 512), np.float32)
    for col in range(512):
        repm[col // 64, col] = 1.0
    shared = dict(w_in_T=w_in_T, w_xp_T=w_xp_T, a_T=a_T, b_T=b_T, wr_T=wr_T,
                  br_r=br_r, conv_w_t=conv_w_t, conv_b_t=conv_b_t,
                  ln1_w_t=ln1_w_t, ln1_b_t=ln1_b_t, repm_in=repm)
    in_maps = []
    for i in range(NC):
        s0 = i * TPC
        x_ext = np.zeros((TPC + 3, D), np.float32)
        if i % 4 != 0:
            x_ext[0:3] = x[s0 - 3: s0]
        x_ext[3:] = x[s0: s0 + TPC]
        hm = np.zeros((128, 3), np.float32) if i % 4 == 0 else np.ones((128, 3), np.float32)
        im = dict(shared)
        im["x_in"] = np.ascontiguousarray(x_ext.T)
        im["halo_mask"] = hm
        in_maps.append(im)
    return in_maps


def _scan_inputs(inputs, xc, z, xd):
    wdt = host_f32(inputs["L0_attn_dt_proj_w"]).T.copy()
    db = host_f32(inputs["L0_attn_dt_proj_b"])
    alog = host_f32(inputs["L0_attn_A_log"])
    dsk = host_f32(inputs["L0_attn_D_skip"])
    in_maps = []
    for j in range(NC):
        c0 = j * CPC
        in_maps.append(dict(
            xc_in=np.ascontiguousarray(xc[c0:c0 + CPC]),
            z_in=np.ascontiguousarray(z[c0:c0 + CPC]),
            xd_in=xd,
            wdt_j=np.ascontiguousarray(wdt[:, c0:c0 + CPC]),
            db_j=np.ascontiguousarray(db[c0:c0 + CPC].reshape(CPC // 128, 128).T),
            alog_j=np.ascontiguousarray(alog[c0:c0 + CPC]),
            dsk_j=np.ascontiguousarray(dsk[c0:c0 + CPC].reshape(CPC // 128, 128).T),
            identb_j=np.eye(128, dtype=BF16_NP),
        ))
    return in_maps


def _back_inputs(inputs, yg, x1s, us):
    w_out_T = host_f32(inputs["L0_attn_out_proj_w"]).T.copy()
    w_ffp_T = host_f32(inputs["L0_attn_ff_proj_w"]).T.copy()
    w_ffo_T = host_f32(inputs["L0_attn_ff_out_w"]).T.copy()
    A = host_f32(inputs["L0_ffn_A"])
    Bw = host_f32(inputs["L0_ffn_B"])
    a_T = np.ascontiguousarray(A.transpose(2, 0, 1).reshape(D, E * R))
    b_T = np.ascontiguousarray(Bw.transpose(0, 2, 1).reshape(E * R, D))
    wr_T = host_f32(inputs["L0_ffn_Wr"]).T.copy()
    br_r = host_f32(inputs["L0_ffn_br"]).reshape(1, E)
    ln2_w_t = colsT(inputs["L0_attn_ln2_w"], NKD)
    ln2_b_t = colsT(inputs["L0_attn_ln2_b"], NKD)
    repm = np.zeros((8, 512), np.float32)
    for col in range(512):
        repm[col // 64, col] = 1.0
    shared = dict(w_out_T=w_out_T, w_ffp_T=w_ffp_T, w_ffo_T=w_ffo_T, a_T=a_T,
                  b_T=b_T, wr_T=wr_T, br_r=br_r, ln2_w_t=ln2_w_t,
                  ln2_b_t=ln2_b_t, repm_in=repm)
    in_maps = []
    for i in range(NC):
        im = dict(shared)
        im["yg_in"] = np.ascontiguousarray(yg[:, i * TPC:(i + 1) * TPC])
        im["x1_in"] = x1s[i]
        im["u_in"] = us[i]
        in_maps.append(im)
    return in_maps


_NC_CACHE = {}


def kernel(**inputs):
    cores = list(range(NC))
    if MODE == "mega":
        if "mega" not in _NC_CACHE:
            _NC_CACHE["mega"] = build_mega()
        res = run_bass_kernel_spmd(_NC_CACHE["mega"], _mega_inputs(inputs), core_ids=cores)
        out = np.concatenate([np.asarray(r["xo_o"]) for r in res.results], axis=1)
        return np.ascontiguousarray(out.T).reshape(B_, L_, D)
    if "front" not in _NC_CACHE:
        _NC_CACHE["front"] = build_front()
        _NC_CACHE["scan"] = build_scan()
        _NC_CACHE["back"] = build_back()
    resA = run_bass_kernel_spmd(_NC_CACHE["front"], _front_inputs(inputs), core_ids=cores)
    xc = np.concatenate([np.asarray(r["xc_o"]) for r in resA.results], axis=1)
    z = np.concatenate([np.asarray(r["z_o"]) for r in resA.results], axis=1)
    xd = np.ascontiguousarray(np.concatenate([np.asarray(r["xd_o"]) for r in resA.results], axis=1))
    x1s = [np.asarray(r["x1_o"]) for r in resA.results]
    us = [np.asarray(r["u_o"]) for r in resA.results]
    resB = run_bass_kernel_spmd(_NC_CACHE["scan"], _scan_inputs(inputs, xc, z, xd), core_ids=cores)
    yg = np.concatenate([np.asarray(r["yg_o"]) for r in resB.results], axis=0)
    resC = run_bass_kernel_spmd(_NC_CACHE["back"], _back_inputs(inputs, yg, x1s, us), core_ids=cores)
    out = np.concatenate([np.asarray(r["xo_o"]) for r in resC.results], axis=1)
    return np.ascontiguousarray(out.T).reshape(B_, L_, D)


# version 11
# speedup vs baseline: 1.7138x; 1.0173x over previous
import contextlib
import numpy as np
import ml_dtypes
import concourse.bass as bass
import concourse.mybir as mybir
from concourse.bass_utils import run_bass_kernel_spmd

F32 = mybir.dt.float32
BF16 = mybir.dt.bfloat16
BF16_NP = ml_dtypes.bfloat16
AL = mybir.AluOpType
AF = mybir.ActivationFunctionType
AX = mybir.AxisListType.X

NC = 8
B_, L_, D, DI, DS, DTR, H = 2, 16384, 2048, 4096, 16, 128, 8192
E, R = 8, 64
TOK = B_ * L_
TPC = TOK // NC
CPC = DI // NC
NKD = D // 128
NKI = DI // 128
EPS = 1e-5
TT_A = 512
TT_C = 256
FRONT_TILES = TPC // TT_A
LC = 1024
SCAN_CTS = TOK // LC
SCAN_DEBUG = False
BACK_TILES = TPC // TT_C
MODE = "mega"
BACK_SPAN = TPC

ENGS = ("sync", "tensor", "vector", "scalar", "gpsimd")
DMAQ = ("sync", "scalar", "gpsimd")


class Prog:

    def __init__(self, nc, es):
        self.nc = nc
        self.es = es
        self.ops = {e: [] for e in ENGS}
        self.sem = {e: es.enter_context(nc.semaphore(f"es_{e}")) for e in ENGS}
        self.cnt = {e: 0 for e in ENGS}
        self.lastwait = {e: {} for e in ENGS}
        self.ch = {}

    def _channel(self, name, eng):
        if name not in self.ch:
            self.ch[name] = dict(sem=self.es.enter_context(self.nc.semaphore(f"dc_{name}")),
                                 eng=eng, total=0, closed=0, open=False)
        c = self.ch[name]
        assert c["eng"] == eng, f"channel {name} is owned by {c['eng']}"
        return c

    def dma(self, name, eng, out, in_):
        c = self._channel(name, eng)
        if not c["open"]:
            if c["closed"] > 0:
                self.ops[eng].append(
                    lambda ifc, sem=c["sem"], v=c["closed"]: ifc.wait_ge(sem, v))
            c["open"] = True
        c["total"] += 16
        sem = c["sem"]
        if eng in ("vector", "scalar", "gpsimd"):
            self.ops[eng].append(lambda ifc: ifc.drain())
        def cl(ifc, out=out, in_=in_, sem=sem):
            ifc.dma_start(out=out, in_=in_).then_inc(sem, 16)
        self.ops[eng].append(cl)

    def close(self, name):
        c = self.ch[name]
        c["open"] = False
        c["closed"] = c["total"]
        return ("d", name, c["closed"])

    def run(self, eng, fn, inc=False):
        sem = self.sem[eng]
        if eng in ("vector", "scalar", "gpsimd"):
            self.ops[eng].append(lambda ifc: ifc.drain())
        if inc:
            self.cnt[eng] += 1

            def cl(ifc, fn=fn, sem=sem):
                fn(ifc).then_inc(sem, 1)
        else:
            def cl(ifc, fn=fn):
                fn(ifc)
        self.ops[eng].append(cl)
        return ("e", eng, self.cnt[eng]) if inc else None

    def wait(self, eng, tok):
        if tok is None:
            return
        kind, src, val = tok
        if val <= 0:
            return
        if kind == "e" and src == eng:
            return
        key = (kind, src)
        if self.lastwait[eng].get(key, 0) >= val:
            return
        self.lastwait[eng][key] = val
        if kind == "d":
            c = self.ch[src]
            assert val <= c["closed"], f"wait on open batch of channel {src}"
            sem = c["sem"]
        else:
            sem = self.sem[src]
        self.ops[eng].append(lambda ifc, sem=sem, val=val: ifc.wait_ge(sem, val))

    def waits(self, eng, *toks):
        for t in toks:
            self.wait(eng, t)

    def mark(self, eng):
        sem = self.sem[eng]
        prev = self.cnt[eng]
        if eng == "gpsimd" and prev > 0:
            self.ops[eng].append(lambda ifc, sem=sem, v=prev: ifc.wait_ge(sem, v))
        self.cnt[eng] += 1
        self.ops[eng].append(lambda ifc: ifc.drain())
        self.ops[eng].append(lambda ifc, sem=sem: ifc.sem_inc(sem, 1))
        return ("e", eng, self.cnt[eng])

    def barrier(self, engs=ENGS):
        toks = {e: self.mark(e) for e in engs}
        for e in engs:
            for o in engs:
                if o != e:
                    self.wait(e, toks[o])

    def drain_all_channels(self, eng):
        for name, c in self.ch.items():
            if c["total"] > 0:
                self.ops[eng].append(
                    lambda ifc, sem=c["sem"], v=c["total"]: ifc.wait_ge(sem, v))

    def finalize(self, block):
        for e in ENGS:
            if not self.ops[e]:
                continue
            deco = getattr(block, e)
            ops = self.ops[e]

            def body(ifc, ops=ops):
                for cl in ops:
                    cl(ifc)
            deco(body)



def emit_ln(p, al, N, x32c, ready_tok, out16c=None, out32c=None, affine=None,
            tmp32=None):
    nc = p.nc
    psA = al["psA"][0:1, 0:N]
    psB = al["psB"][0:1, 0:N]
    p.waits("scalar", ready_tok)
    cons = [None, None]
    for k in range(NKD):
        b = k % 2
        xbt = al["xbt"][:, b * al["ln_n"]: b * al["ln_n"] + N]
        p.waits("scalar", cons[b])
        tk = p.run("scalar", lambda s, o=xbt, i=x32c[k]: s.copy(o, i), inc=True)
        p.waits("tensor", tk)
        p.run("tensor", lambda t, o=psA, w=al["ones_col16"], r=xbt,
              s=(k == 0), e=(k == NKD - 1): t.matmul(o, w[:], r, start=s, stop=e))
        cons[b] = p.mark("tensor")
    for k in range(NKD):
        b = k % 2
        sqt = al["sqt"][:, b * al["ln_n"]: b * al["ln_n"] + N]
        p.waits("scalar", cons[b])
        tk = p.run("scalar", lambda s, o=sqt, i=x32c[k]: s.activation(o, i, AF.Square), inc=True)
        p.waits("tensor", tk)
        p.run("tensor", lambda t, o=psB, w=al["ones_col16"], r=sqt,
              s=(k == 0), e=(k == NKD - 1): t.matmul(o, w[:], r, start=s, stop=e))
        cons[b] = p.mark("tensor")
    tok_mm = cons[(NKD - 1) % 2]

    m = al["st_m"][:, 0:N]
    q = al["st_q"][:, 0:N]
    var = al["st_var"][:, 0:N]
    sd = al["st_sd"][:, 0:N]
    r_ = al["st_r"][:, 0:N]
    nmr = al["st_nmr"][:, 0:N]
    t1 = al["st_t1"][:, 0:N]
    t2 = al["st_t2"][:, 0:N]
    p.waits("vector", tok_mm)
    p.run("vector", lambda v, o=m, i=psA: v.tensor_scalar(o, i, 1.0 / D, None, AL.mult))
    p.run("vector", lambda v, o=q, i=psB: v.tensor_scalar(o, i, 1.0 / D, None, AL.mult))
    p.run("vector", lambda v, o=t1, a=m: v.tensor_tensor(o, a, a, AL.mult))
    p.run("vector", lambda v, o=var, a=q, b=t1: v.tensor_tensor(o, a, b, AL.subtract))
    tv = p.run("vector", lambda v, o=t2, i=var: v.tensor_scalar(o, i, EPS, None, AL.add), inc=True)
    p.waits("scalar", tv)
    ts_ = p.run("scalar", lambda s, o=sd, i=t2: s.activation(o, i, AF.Sqrt), inc=True)
    p.waits("vector", ts_)
    p.run("vector", lambda v, o=r_, i=sd: v.reciprocal(o, i))
    p.run("vector", lambda v, o=t1, a=m, b=r_: v.tensor_tensor(o, a, b, AL.mult))
    tv2 = p.run("vector", lambda v, o=nmr, i=t1: v.tensor_scalar(o, i, -1.0, None, AL.mult), inc=True)

    psR = al["psR"][:, 0:N]
    psN = al["psN"][:, 0:N]
    p.waits("tensor", tv2)
    p.run("tensor", lambda t, o=psR, w=al["ones_row32"], r=r_: t.matmul(o, w[:], r, start=True, stop=True))
    tok_rep = p.run("tensor", lambda t, o=psN, w=al["ones_row32"], r=nmr: t.matmul(o, w[:], r, start=True, stop=True), inc=True)

    p.waits("vector", tok_rep)
    for k in range(NKD):
        p.run("vector", lambda v, o=tmp32[0:128, 0:N], a=x32c[k], b=psR: v.tensor_tensor(o, a, b, AL.mult))
        if out32c is not None:
            p.run("vector", lambda v, o=out32c[k], a=tmp32[0:128, 0:N], b=psN: v.tensor_tensor(o, a, b, AL.add))
        if out16c is not None and affine is None:
            src = out32c[k] if out32c is not None else None
            if src is not None:
                p.run("vector", lambda v, o=out16c[k], i=src: v.tensor_copy(o, i))
            else:
                p.run("vector", lambda v, o=out16c[k], a=tmp32[0:128, 0:N], b=psN: v.tensor_tensor(o, a, b, AL.add))
        if out16c is not None and affine is not None:
            wcol, bcol = affine
            p.run("vector", lambda v, o=al["tmp32b"][0:128, 0:N], a=tmp32[0:128, 0:N], b=psN: v.tensor_tensor(o, a, b, AL.add))
            p.run("vector", lambda v, o=out16c[k], i=al["tmp32b"][0:128, 0:N], w=wcol[k], b=bcol[k]:
                  v.tensor_scalar(o, i, w, b, AL.mult, AL.add))
    return p.mark("vector")


def emit_moe(p, al, N, v16c, vtok, x32c, xtok, out32c, scale, pre_cb=None, post_cb=None):
    nc = p.nc
    nst = (N + 127) // 128
    psR = al["psRt"]
    lgs = al["lgs"]
    mx = al["mx"]
    wsc = al["wsc"]
    sm = al["sm"]
    maskb = al["maskb"]
    p.waits("tensor", vtok)
    for st in range(nst):
        Ns = min(128, N - 128 * st)
        out = psR[0:Ns, st * 8:(st + 1) * 8]
        for k in range(NKD):
            p.run("tensor", lambda t, o=out, w=v16c[k][:, st * 128: st * 128 + Ns],
                  r=al["wrT"][:, k * 8:(k + 1) * 8], s=(k == 0):
                  t.matmul(o, w, r, start=s, stop=False))
        p.run("tensor", lambda t, o=out, w=al["ones_row16"][:, 0:Ns], r=al["brT"][:]:
              t.matmul(o, w, r, start=False, stop=True))
    tok_r = p.mark("tensor")
    p.waits("scalar", tok_r)
    ts_ = None
    for st in range(nst):
        Ns = min(128, N - 128 * st)
        ts_ = p.run("scalar", lambda s, o=lgs[0:Ns, st * 8:(st + 1) * 8],
                    i=psR[0:Ns, st * 8:(st + 1) * 8]: s.copy(o, i), inc=True)
    p.waits("vector", ts_)
    for st in range(nst):
        Ns = min(128, N - 128 * st)
        p.run("vector", lambda v, o=mx[0:Ns, st:st + 1], i=lgs[0:Ns, st * 8:(st + 1) * 8]:
              v.reduce_max(o, i, axis=AX))
    for st in range(nst):
        Ns = min(128, N - 128 * st)
        p.run("vector", lambda v, o=al["nmx"][0:Ns, st:st + 1], i=mx[0:Ns, st:st + 1]:
              v.tensor_scalar(o, i, -1.0, None, AL.mult))
    tv = p.mark("vector")
    p.waits("scalar", tv)
    for st in range(nst):
        Ns = min(128, N - 128 * st)
        p.run("scalar", lambda s, o=al["exps"][0:Ns, st * 8:(st + 1) * 8], i=lgs[0:Ns, st * 8:(st + 1) * 8],
              b=al["nmx"][0:Ns, st:st + 1]: s.activation(o, i, AF.Exp, bias=b))
    ts2 = p.mark("scalar")
    p.waits("vector", ts2)
    for st in range(nst):
        Ns = min(128, N - 128 * st)
        p.run("vector", lambda v, o=sm[0:Ns, st:st + 1], i=al["exps"][0:Ns, st * 8:(st + 1) * 8]:
              v.reduce_sum(o, i, axis=AX))
    for st in range(nst):
        Ns = min(128, N - 128 * st)
        p.run("vector", lambda v, o=al["w0"][0:Ns, st:st + 1], i=sm[0:Ns, st:st + 1]: v.reciprocal(o, i))
    for st in range(nst):
        Ns = min(128, N - 128 * st)
        p.run("vector", lambda v, o=wsc[0:Ns, st:st + 1], i=al["w0"][0:Ns, st:st + 1]:
              v.tensor_scalar(o, i, scale / R, None, AL.mult))
    for st in range(nst):
        Ns = min(128, N - 128 * st)
        p.run("vector", lambda v, o=maskb[0:Ns, st * 8:(st + 1) * 8], i=lgs[0:Ns, st * 8:(st + 1) * 8],
              a=mx[0:Ns, st:st + 1], b=wsc[0:Ns, st:st + 1]:
              v.tensor_scalar(o, i, a, b, AL.is_equal, AL.mult))
    tv2 = p.mark("vector")
    psT = al["psT"]
    p.waits("tensor", tv2)
    for st in range(nst):
        Ns = min(128, N - 128 * st)
        p.run("tensor", lambda t, o=psT[0:8, st * 128: st * 128 + Ns],
              i=maskb[0:Ns, st * 8:(st + 1) * 8], id_=al["ident"], n=Ns:
              t.transpose(o, i, id_[0:n, 0:n]))
    tok_t = p.mark("tensor")
    p.waits("scalar", tok_t)
    ts3 = p.run("scalar", lambda s, o=al["maskT"][0:8, 0:N], i=psT[0:8, 0:N]: s.copy(o, i), inc=True)
    psM = al["psM"][:, 0:N]
    psH = al["psH"][:, 0:N]
    hm = al["hm16"]
    tok_hm = []
    for c in range(4):
        p.waits("tensor", ts3)
        tmr = p.run("tensor", lambda t, o=psM, w=al["repm"][:, c * 128:(c + 1) * 128],
              r=al["maskT"][0:8, 0:N]: t.matmul(o, w, r, start=True, stop=True), inc=True)
        p.waits("scalar", tmr)
        tmc = p.run("scalar", lambda s, o=al["mrep16"][:, 0:N], i=psM: s.copy(o, i), inc=True)
        for k in range(NKD):
            p.run("tensor", lambda t, o=psH, w=al["aT"][:, (k * 4 + c) * 128:(k * 4 + c) * 128 + 128],
                  r=v16c[k], s=(k == 0), e=(k == NKD - 1): t.matmul(o, w, r, start=s, stop=e))
        tmh = p.mark("tensor")
        p.waits("scalar", tmh)
        tch = p.run("scalar", lambda s, o=al["h16"][:, 0:N], i=psH: s.copy(o, i), inc=True)
        p.waits("vector", tch, tmc)
        thm = p.run("vector", lambda v, o=hm[:, c * al["ffn_n"]: c * al["ffn_n"] + N],
              a=al["h16"][:, 0:N], b=al["mrep16"][:, 0:N]: v.tensor_tensor(o, a, b, AL.mult), inc=True)
        tok_hm.append(thm)
        p.waits("tensor", thm, tch, tmc)
    psS = al["psS"][:, 0:N]
    p.waits("tensor", tok_hm[-1], xtok)
    out_tok = None
    for mchunk in range(NKD):
        for c in range(4):
            p.run("tensor", lambda t, o=psS, w=al["bT"][:, (c * NKD + mchunk) * 128:(c * NKD + mchunk) * 128 + 128],
                  r=hm[:, c * al["ffn_n"]: c * al["ffn_n"] + N], s=(c == 0), e=(c == 3):
                  t.matmul(o, w, r, start=s, stop=e))
        tms = p.mark("tensor")
        p.waits("vector", tms)
        if pre_cb is not None:
            p.waits("vector", pre_cb(mchunk))
        out_tok = p.run("vector", lambda v, o=out32c[mchunk], a=x32c[mchunk], b=psS:
                        v.tensor_tensor(o, a, b, AL.add), inc=True)
        p.waits("tensor", out_tok)
        if post_cb is not None:
            post_cb(mchunk, out_tok)
    return out_tok


def host_f32(a):
    return np.ascontiguousarray(np.asarray(a, dtype=np.float32))


def colsT(vec, nchunk):
    return np.ascontiguousarray(np.asarray(vec, dtype=np.float32).reshape(nchunk, 128).T)



def build_front(nc=None, p=None, io=None, finalize=True, mid_cb=None):
    if nc is None:
        nc = bass.Bass("TRN2", target_bir_lowering=False, debug=False)
    dram = nc.dram_tensor
    if io is None:
        io = {}
        for nm, shp in (("x_in", [D, TPC + 3]), ("halo_mask", [128, 3]),
                        ("w_in_T", [D, 2 * DI]), ("w_xp_T", [DI, 160]),
                        ("a_T", [D, E * R]), ("b_T", [E * R, D]), ("wr_T", [D, E]),
                        ("br_r", [1, E]), ("conv_w_t", [128, NKI * 4]),
                        ("conv_b_t", [128, NKI]), ("ln1_w_t", [128, NKD]),
                        ("ln1_b_t", [128, NKD]), ("repm_in", [8, 512])):
            io[nm] = dram(nm, shp, F32, kind="ExternalInput").ap()
        io["xc_o"] = dram("xc_o", [DI, TPC], BF16, kind="ExternalOutput").ap()
        io["z_o"] = dram("z_o", [DI, TPC], BF16, kind="ExternalOutput").ap()
        io["xd_o"] = dram("xd_o", [160, TPC], BF16, kind="ExternalOutput").ap()
        io["x1_o"] = dram("x1_o", [D, TPC], F32, kind="ExternalOutput").ap()
        io["u_o"] = dram("u_o", [D, TPC], F32, kind="ExternalOutput").ap()
    (x_in, halo_mask, w_in_T, w_xp_T, a_T, b_T, wr_T, br_r, conv_w_t, conv_b_t,
     ln1_w_t, ln1_b_t, repm_in, xc_o, z_o, xd_o, x1_o, u_o) = (
        io["x_in"], io["halo_mask"], io["w_in_T"], io["w_xp_T"], io["a_T"],
        io["b_T"], io["wr_T"], io["br_r"], io["conv_w_t"], io["conv_b_t"],
        io["ln1_w_t"], io["ln1_b_t"], io["repm_in"], io["xc_o"], io["z_o"],
        io["xd_o"], io["x1_o"], io["u_o"])

    w_in_b = dram("f_w_in_b", [D, 2 * DI], BF16)
    w_xp_b = dram("f_w_xp_b", [DI, 160], BF16)
    a_b = dram("f_a_b", [D, E * R], BF16)
    b_b = dram("f_b_b", [E * R, D], BF16)
    wr_b = dram("f_wr_b", [D, E], BF16)
    br_b = dram("f_br_b", [1, E], BF16)
    t_s = dram("f_t_s", [D, TPC + 3], BF16).ap()

    PF = "f_"
    es = contextlib.ExitStack()
    with es:
        if p is None:
            p = Prog(nc, es)
        p.dma("prepm", "gpsimd", a_b.ap(), a_T)
        p.dma("prepm", "gpsimd", b_b.ap(), b_T)
        p.dma("prepm", "gpsimd", wr_b.ap(), wr_T)
        p.dma("prepm", "gpsimd", br_b.ap(), br_r)
        tok_prepm = p.close("prepm")
        p.dma("prep", "gpsimd", w_in_b.ap(), w_in_T)
        p.dma("prep", "gpsimd", w_xp_b.ap(), w_xp_T)
        tok_prep = p.close("prep")
        if io.get("post_prep_cb") is not None:
            io["post_prep_cb"]()

        N = TT_A
        with contextlib.ExitStack() as e0:
            sb0 = lambda name, shape, dt=F32: e0.enter_context(nc.sbuf_tensor(PF + "p0" + name, shape, dt))[:]
            pst0 = lambda name, shape, dt=F32: e0.enter_context(nc.psum_tensor(PF + name, shape, dt))[:]
            al = {}
            bufA = sb0("bufA", [128, NKD * N])
            bufB = sb0("bufB", [128, NKD * N])
            al["xbt"] = sb0("xbt", [128, 2 * N], BF16)
            al["sqt"] = sb0("sqt", [128, 2 * N], BF16)
            al["ln_n"] = N
            vt16 = sb0("vt16", [128, NKD * N], BF16)
            al["tmp32"] = sb0("tmp32", [128, N])
            al["tmp32b"] = sb0("tmp32b", [128, N])
            for nm in ("st_m", "st_q", "st_var", "st_sd", "st_r", "st_nmr", "st_t1", "st_t2"):
                al[nm] = sb0(nm, [1, N])
            al["ones_col16"] = e0.enter_context(nc.sbuf_tensor(PF + "p0ones_c", [128, 1], BF16))
            al["ones_row32"] = e0.enter_context(nc.sbuf_tensor(PF + "p0ones_r", [1, 128], F32))
            al["ones_row16"] = sb0("ones_r16", [1, 128], BF16)
            al["ident"] = e0.enter_context(nc.sbuf_tensor(PF + "p0ident", [128, 128], BF16))
            zeros128 = e0.enter_context(nc.sbuf_tensor(PF + "p0zeros", [128, 128], BF16))
            al["lgs"] = sb0("lgs", [128, 32])
            al["mx"] = sb0("mx", [128, 4])
            al["nmx"] = sb0("nmx", [128, 4])
            al["sm"] = sb0("sm", [128, 4])
            al["wsc"] = sb0("wsc", [128, 4])
            al["w0"] = sb0("w0", [128, 4])
            al["exps"] = sb0("exps", [128, 32])
            al["maskb"] = sb0("maskb", [128, 32], BF16)
            al["maskT"] = sb0("maskT", [8, N], BF16)
            al["repm"] = sb0("repm", [8, 512], BF16)
            al["mrep16"] = sb0("mrep16", [128, N], BF16)
            al["h16"] = sb0("h16", [128, N], BF16)
            al["hm16"] = sb0("hm16", [128, 4 * N], BF16)
            al["ffn_n"] = N
            al["aT"] = sb0("aT", [128, NKD * 4 * 128], BF16)
            al["bT"] = sb0("bT", [128, 4 * NKD * 128], BF16)
            al["wrT"] = sb0("wrT", [128, NKD * 8], BF16)
            al["brT"] = e0.enter_context(nc.sbuf_tensor(PF + "p0brT", [1, 8], BF16))
            al["psA"] = pst0("p0psA", [1, 512])
            al["psB"] = pst0("p0psB", [1, 512])
            al["psR"] = pst0("p0psR", [128, 512])
            al["psN"] = pst0("p0psN", [128, 512])
            al["psS"] = pst0("p0psS", [128, 512])
            al["psRt"] = al["psS"][:, 0:32]
            al["psM"] = al["psR"]
            al["psH"] = al["psN"]
            psX0 = pst0("p0psX", [128, 512])
            al["psT"] = psX0[0:8, :].bitcast(BF16)
            ln1w = sb0("ln1w", [128, NKD])
            ln1b = sb0("ln1b", [128, NKD])
            hmask = sb0("hmask", [128, 3])
            tok_z = p.run("vector", lambda v: v.memset(zeros128[:], 0.0), inc=True)
            p.waits("gpsimd", tok_z)
            tok_id = p.run("gpsimd", lambda g: g.affine_select(
                out=al["ident"][:], in_=zeros128[:], compare_op=AL.not_equal,
                fill=1.0, base=0, pattern=[[-1, 128]], channel_multiplier=1), inc=True)
            tok_c1 = p.run("gpsimd", lambda g: g.memset(al["ones_col16"][:], 1.0), inc=True)
            tok_c2 = p.run("gpsimd", lambda g: g.memset(al["ones_row32"][:], 1.0), inc=True)
            tok_c3 = p.run("gpsimd", lambda g: g.memset(al["ones_row16"][:], 1.0), inc=True)
            p.waits("sync", tok_prepm)
            for k in range(NKD):
                p.dma("res", "sync", al["aT"][:, k * 512:(k + 1) * 512],
                      a_b.ap()[k * 128:(k + 1) * 128, :])
            for c in range(4):
                p.dma("res", "sync", al["bT"][:, c * 2048:(c + 1) * 2048],
                      b_b.ap()[c * 128:(c + 1) * 128, :])
            for k in range(NKD):
                p.dma("res", "sync", al["wrT"][:, k * 8:(k + 1) * 8],
                      wr_b.ap()[k * 128:(k + 1) * 128, :])
            p.dma("res", "sync", al["brT"][:], br_b.ap())
            p.dma("res", "sync", ln1w, ln1_w_t)
            p.dma("res", "sync", ln1b, ln1_b_t)
            p.dma("res", "sync", hmask, halo_mask)
            tok_res = p.close("res")
            p.dma("prep0", "gpsimd", al["repm"], repm_in)
            tok_rp = p.close("prep0")
            for e in ENGS:
                p.waits(e, tok_res, tok_rp, tok_id, tok_c1, tok_c2, tok_c3)
            p.barrier()
            ln1w_c = [ln1w[:, k:k + 1] for k in range(NKD)]
            ln1b_c = [ln1b[:, k:k + 1] for k in range(NKD)]
            store_toks = {}

            def p0_pass(N_, xoff, halo_pass):
                xc = [bufA[:, k * TT_A: k * TT_A + N_] for k in range(NKD)]
                x1c = [bufB[:, k * TT_A: k * TT_A + N_] for k in range(NKD)]
                vc = [vt16[:, k * TT_A: k * TT_A + N_] for k in range(NKD)]
                p.waits("vector", store_toks.get("x1"), store_toks.get("u"), store_toks.get("t"))
                p.waits("sync", store_toks.get("x1"), store_toks.get("u"), store_toks.get("t"),
                        store_toks.get("tln"))
                for k in range(NKD):
                    p.dma("x", "sync", xc[k], x_in[k * 128:(k + 1) * 128, xoff: xoff + N_])
                tok_x = p.close("x")
                tok_v = emit_ln(p, al, N_, xc, tok_x, out16c=vc, tmp32=al["tmp32"])
                tok_x1 = emit_moe(p, al, N_, vc, tok_v, xc, tok_x, x1c, 0.5)
                if not halo_pass:
                    p.waits("scalar", tok_x1)
                    for k in range(NKD):
                        p.dma("ox1", "scalar", x1_o[k * 128:(k + 1) * 128, xoff - 3: xoff - 3 + N_], x1c[k])
                    store_toks["x1"] = p.close("ox1")
                uc = xc
                tok_u = emit_ln(p, al, N_, x1c, tok_x1, out32c=uc, tmp32=al["tmp32"])
                if not halo_pass:
                    p.waits("scalar", tok_u)
                    for k in range(NKD):
                        p.dma("ou", "scalar", u_o[k * 128:(k + 1) * 128, xoff - 3: xoff - 3 + N_], uc[k])
                    store_toks["u"] = p.close("ou")
                tok_t = emit_ln(p, al, N_, uc, tok_u, out16c=vc, affine=(ln1w_c, ln1b_c),
                                tmp32=al["tmp32"])
                if halo_pass:
                    p.waits("vector", tok_t)
                    for k in range(NKD):
                        tok_t = p.run("vector", lambda v, o=al["tmp32"][:, 0:3],
                                      a=vt16[:, k * TT_A: k * TT_A + 3]:
                                      v.tensor_tensor(o, a, hmask, AL.mult), inc=True)
                        tok_t = p.run("vector", lambda v, o=vt16[:, k * TT_A: k * TT_A + 3],
                                      i=al["tmp32"][:, 0:3]: v.tensor_copy(o, i), inc=True)
                    p.waits("scalar", tok_t)
                    for k in range(NKD):
                        p.dma("ot", "scalar", t_s[k * 128:(k + 1) * 128, 0:3],
                              vt16[:, k * TT_A: k * TT_A + 3])
                    store_toks["t"] = p.close("ot")
                else:
                    p.waits("scalar", tok_t)
                    for k in range(NKD):
                        p.dma("ot", "scalar", t_s[k * 128:(k + 1) * 128, xoff: xoff + N_], vc[k])
                    store_toks["t"] = p.close("ot")
                store_toks["tln"] = tok_t

            p0_pass(4, 0, True)
            for t in range(TPC // N):
                p0_pass(N, 3 + t * N, False)
            for name in list(p.ch):
                if p.ch[name]["open"]:
                    p.close(name)
            p.drain_all_channels("sync")
            p.barrier()

        PT = 2048
        with contextlib.ExitStack() as e1:
            sb1 = lambda name, shape, dt=F32: e1.enter_context(nc.sbuf_tensor(PF + "p1" + name, shape, dt))[:]
            pst1 = lambda name, shape, dt=F32: e1.enter_context(nc.psum_tensor(PF + name, shape, dt))[:]
            t16 = sb1("t16", [128, NKD * (PT + 3)], BF16)
            wbuf = [sb1(f"wbuf{b}", [128, NKD * 512], BF16) for b in range(2)]
            work = sb1("work", [128, 4 * (PT + 3)], BF16)
            xc16 = sb1("xc16", [128, 4 * PT], BF16)
            z16 = sb1("z16", [128, 2 * PT], BF16)
            cva = sb1("cva", [128, PT])
            cvb = sb1("cvb", [128, PT])
            convw = sb1("convw", [128, NKI * 4])
            convb = sb1("convb", [128, NKI])
            psMM = [pst1("p1psMM", [128, 512]), pst1("p1psMM2", [128, 512])]
            p.dma("res1", "sync", convw, conv_w_t)
            p.dma("res1", "sync", convb, conv_b_t)
            tok_res1 = p.close("res1")
            p.waits("sync", tok_prep)
            for e in ("sync", "tensor", "vector", "scalar"):
                p.waits(e, tok_res1)
            wfree = [None, None]
            evac = [None, None]
            ldone = {}
            for t in range(TPC // PT):
                xoff = t * PT
                p.waits("sync", ldone.get("t_cons"))
                for k in range(NKD):
                    p.dma("tl", "sync", t16[:, k * (PT + 3): k * (PT + 3) + PT + 3],
                          t_s[k * 128:(k + 1) * 128, xoff: xoff + PT + 3])
                tok_t = p.close("tl")
                p.waits("tensor", tok_t)
                for m_abs in range(2 * NKI):
                    g, mc = m_abs // 4, m_abs % 4
                    par = g % 2
                    if mc == 0:
                        p.waits("sync", wfree[par])
                        for k in range(NKD):
                            p.dma(f"w{par}", "sync", wbuf[par][:, k * 512:(k + 1) * 512],
                                  w_in_b.ap()[k * 128:(k + 1) * 128, g * 512:(g + 1) * 512])
                        tok_w = p.close(f"w{par}")
                        p.waits("tensor", tok_w)
                    is_xm = m_abs < NKI
                    cslot = m_abs % 4
                    zslot = m_abs % 2
                    npieces = PT // 512
                    dst = work[:, cslot * (PT + 3): cslot * (PT + 3) + PT + 3] if is_xm else None
                    if is_xm:
                        p.waits("scalar", ldone.get(("work", cslot)))
                    else:
                        p.waits("scalar", ldone.get(("z", zslot)))
                    total = PT + 3 if is_xm else PT
                    toff = 0 if is_xm else 3
                    for h in range((total + 511) // 512):
                        c0 = h * 512
                        cn = min(512, total - c0)
                        bank = (m_abs * 5 + h) % 2
                        ps = psMM[bank]
                        p.waits("tensor", evac[bank])
                        for k in range(NKD):
                            p.run("tensor", lambda tt, o=ps[:, 0:cn],
                                  w=wbuf[par][:, k * 512 + mc * 128: k * 512 + mc * 128 + 128],
                                  r=t16[:, k * (PT + 3) + toff + c0: k * (PT + 3) + toff + c0 + cn],
                                  st=(k == 0), e_=(k == NKD - 1): tt.matmul(o, w, r, start=st, stop=e_))
                        tmm = p.mark("tensor")
                        p.waits("scalar", tmm)
                        if is_xm:
                            tev = p.run("scalar", lambda sc, o=dst[:, c0:c0 + cn], i=ps[:, 0:cn]:
                                        sc.copy(o, i), inc=True)
                        else:
                            tev = p.run("scalar", lambda sc, o=z16[:, zslot * PT + c0: zslot * PT + c0 + cn],
                                        i=ps[:, 0:cn]: sc.copy(o, i), inc=True)
                        evac[bank] = tev
                    if is_xm:
                        cw = [convw[:, m_abs * 4 + kk: m_abs * 4 + kk + 1] for kk in range(4)]
                        p.waits("vector", tev, ldone.get(("cv", 0)))
                        p.run("vector", lambda v, o=cva[:, 0:PT], i=dst[:, 0:PT], w=cw[0]:
                              v.tensor_scalar(o, i, w, None, AL.mult))
                        for kk in range(1, 4):
                            src, dsx = (cva, cvb) if kk % 2 == 1 else (cvb, cva)
                            p.run("vector", lambda v, o=dsx[:, 0:PT], i=dst[:, kk:kk + PT], w=cw[kk], b=src[:, 0:PT]:
                                  v.scalar_tensor_tensor(o, i, w, b, AL.mult, AL.add))
                        tok_cv = p.mark("vector")
                        ldone[("work", cslot)] = tok_cv
                        xslot = m_abs % 4
                        p.waits("scalar", tok_cv, ldone.get(("xc", xslot)))
                        tok_xc = p.run("scalar", lambda sc, o=xc16[:, xslot * PT: xslot * PT + PT],
                                       i=cvb[:, 0:PT], b=convb[:, m_abs:m_abs + 1]:
                                       sc.activation(o, i, AF.Silu, bias=b), inc=True)
                        ldone[("cv", 0)] = tok_xc
                        p.waits("scalar", tok_xc)
                        p.dma(f"oxc{xslot}", "scalar",
                              xc_o[m_abs * 128:(m_abs + 1) * 128, xoff: xoff + PT],
                              xc16[:, xslot * PT: xslot * PT + PT])
                        ldone[("xc", xslot)] = p.close(f"oxc{xslot}")
                    else:
                        p.waits("scalar", tev)
                        p.dma(f"oz{zslot}", "scalar",
                              z_o[(m_abs - NKI) * 128:(m_abs - NKI + 1) * 128, xoff: xoff + PT],
                              z16[:, zslot * PT:(zslot + 1) * PT])
                        ldone[("z", zslot)] = p.close(f"oz{zslot}")
                    if mc == 3:
                        wfree[par] = p.mark("tensor")
                ldone["t_cons"] = p.mark("tensor")
            for name in list(p.ch):
                if p.ch[name]["open"]:
                    p.close(name)
            p.drain_all_channels("sync")
            p.barrier()

        if mid_cb is not None:
            mid_cb()
        with contextlib.ExitStack() as e2:
            sb2 = lambda name, shape, dt=F32: e2.enter_context(nc.sbuf_tensor(PF + "p2" + name, shape, dt))[:]
            pst2 = lambda name, shape, dt=F32: e2.enter_context(nc.psum_tensor(PF + name, shape, dt))[:]
            xpT = sb2("xpT", [128, NKI * 160], BF16)
            xcr = sb2("xcr", [128, 4 * 8 * 512], BF16)
            xd16 = sb2("xd16", [128, 2 * 512], BF16)
            xd16b = sb2("xd16b", [32, 2 * 512], BF16)
            psD = pst2("p2psD", [128, 512])
            psD2 = pst2("p2psD2", [32, 512])
            for kk in range(NKI):
                p.dma("res2", "sync", xpT[:, kk * 160:(kk + 1) * 160],
                      w_xp_b.ap()[kk * 128:(kk + 1) * 128, :])
            tok_res2 = p.close("res2")
            p.waits("tensor", tok_res2)
            ldone = {}
            for piece in range(TPC // 512):
                xoff = piece * 512
                psl = piece % 2
                for g in range(4):
                    ag = piece * 4 + g
                    sl = ag % 4
                    ring = "sync" if sl % 2 == 0 else "scalar"
                    p.waits(ring, ldone.get(("xcr", sl)))
                    for kk in range(8):
                        k = g * 8 + kk
                        p.dma(f"xcl{sl}", ring,
                              xcr[:, (sl * 8 + kk) * 512:(sl * 8 + kk + 1) * 512],
                              xc_o[k * 128:(k + 1) * 128, xoff: xoff + 512])
                    tok_xc = p.close(f"xcl{sl}")
                    p.waits("tensor", tok_xc, ldone.get("evac") if g == 0 else None)
                    for kk in range(8):
                        k = g * 8 + kk
                        p.run("tensor", lambda tt, o=psD[:, 0:512], w=xpT[:, k * 160: k * 160 + 128],
                              r=xcr[:, (sl * 8 + kk) * 512:(sl * 8 + kk + 1) * 512],
                              st=(k == 0), e_=(k == NKI - 1):
                              tt.matmul(o, w, r, start=st, stop=e_))
                        p.run("tensor", lambda tt, o=psD2[:, 0:512], w=xpT[:, k * 160 + 128: k * 160 + 160],
                              r=xcr[:, (sl * 8 + kk) * 512:(sl * 8 + kk + 1) * 512],
                              st=(k == 0), e_=(k == NKI - 1):
                              tt.matmul(o, w, r, start=st, stop=e_))
                    ldone[("xcr", sl)] = p.mark("tensor")
                tok_mm = ldone[("xcr", (piece * 4 + 3) % 4)]
                p.waits("scalar", tok_mm, ldone.get(("xd", psl)))
                p.run("scalar", lambda sc, o=xd16[:, psl * 512:(psl + 1) * 512], i=psD:
                      sc.copy(o, i[:, 0:512]))
                tok_xd = p.run("scalar", lambda sc, o=xd16b[:, psl * 512:(psl + 1) * 512], i=psD2:
                               sc.copy(o, i[:, 0:512]), inc=True)
                ldone["evac"] = tok_xd
                p.waits("sync", tok_xd)
                p.dma(f"oxd{psl}", "sync", xd_o[0:128, xoff: xoff + 512],
                      xd16[:, psl * 512:(psl + 1) * 512])
                p.dma(f"oxd{psl}", "sync", xd_o[128:160, xoff: xoff + 512],
                      xd16b[:, psl * 512:(psl + 1) * 512])
                ldone[("xd", psl)] = p.close(f"oxd{psl}")

        if finalize:
            for name in list(p.ch):
                if p.ch[name]["open"]:
                    p.close(name)
            p.drain_all_channels("sync")
            block = es.enter_context(nc.Block())
            p.finalize(block)
    return nc



def build_scan(nc=None, p=None, io=None, finalize=True):
    if nc is None:
        nc = bass.Bass("TRN2", target_bir_lowering=False, debug=False)
    dram = nc.dram_tensor
    if io is None:
        io = {}
        xc_in = dram("xc_in", [CPC, TOK], BF16, kind="ExternalInput").ap()
        z_in = dram("z_in", [CPC, TOK], BF16, kind="ExternalInput").ap()
        xd_in = dram("xd_in", [160, TOK], BF16, kind="ExternalInput").ap()
        io["wdt_j"] = dram("wdt_j", [DTR, CPC], F32, kind="ExternalInput").ap()
        io["db_j"] = dram("db_j", [128, CPC // 128], F32, kind="ExternalInput").ap()
        io["alog_j"] = dram("alog_j", [CPC, DS], F32, kind="ExternalInput").ap()
        io["dsk_j"] = dram("dsk_j", [128, CPC // 128], F32, kind="ExternalInput").ap()
        io["identb_j"] = dram("identb_j", [128, 128], BF16, kind="ExternalInput").ap()
        yg_ = dram("yg_o", [CPC, TOK], BF16, kind="ExternalOutput").ap()
        io["xc_ap"] = lambda r0, nr, co, ln: xc_in[r0:r0 + nr, co:co + ln]
        io["z_ap"] = lambda r0, nr, co, ln: z_in[r0:r0 + nr, co:co + ln]
        io["xd_ap"] = lambda r0, nr, co, ln: xd_in[r0:r0 + nr, co:co + ln]
        io["yg_ap"] = lambda r0, nr, co, ln: yg_[r0:r0 + nr, co:co + ln]
    wdt_j, db_j, alog_j, dsk_j = io["wdt_j"], io["db_j"], io["alog_j"], io["dsk_j"]
    identb_j = io["identb_j"]
    xc_ap, z_ap, xd_ap, yg_ap = io["xc_ap"], io["z_ap"], io["xd_ap"], io["yg_ap"]

    ND = CPC // 128
    NCT = SCAN_CTS
    RESET_CT = L_ // LC
    PF = "s_"
    es = contextlib.ExitStack()
    with es:
        sb = lambda name, shape, dt=F32: es.enter_context(nc.sbuf_tensor(PF + name, shape, dt))[:]
        pst = lambda name, shape, dt=F32: es.enter_context(nc.psum_tensor(PF + name, shape, dt))[:]
        wdt16 = sb("wdt16", [128, CPC], BF16)
        atmp = sb("atmp", [128, ND * DS])
        acol = sb("acol", [128, ND * DS])
        db = sb("db", [128, ND])
        dsk = sb("dsk", [128, ND])
        dtr16 = sb("dtr16", [128, 2 * LC], BF16)
        Bs = sb("Bs", [128, 3 * LC], BF16)
        Cs = sb("Cs", [128, 3 * LC], BF16)
        xcd = sb("xcd", [128, 2 * ND * LC], BF16)
        zd = sb("zd", [128, 2 * ND * LC], BF16)
        dt32 = sb("dt32", [128, ND * LC])
        dtx = sb("dtx", [128, ND * LC], BF16)
        Pt = sb("Pt", [128, ND * LC])
        Qt = sb("Qt", [128, ND * LC])
        Rt = sb("Rt", [128, ND * LC])
        dA = [sb(f"dA{i}", [128, LC]) for i in range(3)]
        u_all = [sb(f"u_all{i}", [128, ND * LC], BF16) for i in range(2)]
        h8 = sb("h8", [128, 8 * LC], BF16)
        prod_all = sb("prod_all", [128, ND * LC], BF16)
        prod2 = sb("prod2", [128, ND * LC], BF16)
        yacc = [sb(f"yacc{i}", [128, LC], BF16) for i in range(2)]
        identb = sb("identb", [128, 128], BF16)
        t1g = sb("t1g", [128, 2 * LC], BF16)
        szg = sb("szg", [128, 2 * LC], BF16)
        ygd = sb("ygd", [128, ND * LC], BF16)
        state_all = sb("state_all", [128, DS * ND])
        psdt = [pst(f"psdt{i}", [128, 512]) for i in range(2)]
        psy = [pst(f"psy{i}", [128, LC]) for i in range(3)]

        p = p if p is not None else Prog(nc, es)
        p.dma("prep", "gpsimd", wdt16, wdt_j)
        tok_prep = p.close("prep")
        for d in range(ND):
            p.dma("res", "sync", atmp[:, d * DS:(d + 1) * DS],
                  alog_j[d * 128:(d + 1) * 128, :])
        p.dma("res", "sync", db, db_j)
        p.dma("res", "sync", dsk, dsk_j)
        p.dma("res", "sync", identb, identb_j)
        tok_res = p.close("res")
        p.waits("scalar", tok_res)
        tok_e = p.run("scalar", lambda sc: sc.activation(acol, atmp, AF.Exp), inc=True)
        p.waits("vector", tok_e)
        tok_a = p.run("vector", lambda v: v.tensor_scalar(atmp, acol, -1.0, None, AL.mult), inc=True)
        for e in ENGS:
            p.waits(e, tok_prep, tok_res, tok_a)
        p.barrier()

        prev_cons = {}
        slotS = [None, None, None]
        hcons = [None, None]
        bcons = [None, None, None]
        ucons = [None, None]
        pcons = [None, None]
        ygate_read = [None] * 3
        oy_tok = [None] * ND
        for ct in range(NCT):
            co = ct * LC
            pc = ct % 2
            dsl = lambda d: slice(d * LC, (d + 1) * LC)
            xsl = lambda d: slice((pc * ND + d) * LC, (pc * ND + d + 1) * LC)
            for tk in (prev_cons.get(("ld", pc)) or []):
                p.waits("sync", tk)
            p.dma(f"dtr{pc}", "sync", dtr16[:, pc * LC:(pc + 1) * LC], xd_ap(0, DTR, co, LC))
            tok_dtr = p.close(f"dtr{pc}")
            for d in range(ND):
                p.dma(f"xcz{pc}", "sync", xcd[:, xsl(d)], xc_ap(d * 128, 128, co, LC))
                p.dma(f"xcz{pc}", "sync", zd[:, xsl(d)], z_ap(d * 128, 128, co, LC))
            tok_xcz = p.close(f"xcz{pc}")

            p.waits("tensor", tok_dtr, prev_cons.get("dtpre"))
            for d in range(ND):
                for h in range(LC // 512):
                    bank = (d * (LC // 512) + h) % 2
                    p.waits("tensor", prev_cons.get(("psdt", bank)))
                    t_ = p.run("tensor", lambda tt, o=psdt[bank][:, 0:512],
                               w=wdt16[:, d * 128:(d + 1) * 128],
                               r=dtr16[:, pc * LC + h * 512: pc * LC + (h + 1) * 512]:
                               tt.matmul(o, w, r, start=True, stop=True), inc=True)
                    p.waits("scalar", t_)
                    s1 = p.run("scalar", lambda sc, o=Pt[:, d * LC + h * 512: d * LC + (h + 1) * 512],
                               i=psdt[bank][:, 0:512],
                               b=db[:, d:d + 1]: sc.add(o, i, b), inc=True)
                    prev_cons[("psdt", bank)] = s1
            for d in range(ND):
                p.run("scalar", lambda sc, o=Qt[:, dsl(d)], i=Pt[:, dsl(d)]:
                      sc.activation(o, i, AF.Abs))
            t2_ = p.mark("scalar")
            p.waits("vector", t2_)
            for d in range(ND):
                p.run("vector", lambda v, o=Rt[:, dsl(d)], i=Qt[:, dsl(d)]:
                      v.tensor_scalar(o, i, -1.0, None, AL.mult))
            t3_ = p.mark("vector")
            p.waits("scalar", t3_)
            for d in range(ND):
                p.run("scalar", lambda sc, o=Qt[:, dsl(d)], i=Rt[:, dsl(d)]:
                      sc.activation(o, i, AF.Exp))
            t4_ = p.mark("scalar")
            p.waits("vector", t4_)
            for d in range(ND):
                p.run("vector", lambda v, o=Rt[:, dsl(d)], i=Qt[:, dsl(d)]:
                      v.tensor_scalar(o, i, 1.0, None, AL.add))
            t5_ = p.mark("vector")
            p.waits("scalar", t5_)
            for d in range(ND):
                p.run("scalar", lambda sc, o=Qt[:, dsl(d)], i=Rt[:, dsl(d)]:
                      sc.activation(o, i, AF.Ln))
            t6_ = p.mark("scalar")
            p.waits("vector", t6_)
            for d in range(ND):
                p.run("vector", lambda v, o=Rt[:, dsl(d)], i=Pt[:, dsl(d)]:
                      v.tensor_scalar(o, i, 0.0, None, AL.max))
            for d in range(ND):
                p.run("vector", lambda v, o=dt32[:, dsl(d)], a=Qt[:, dsl(d)], b=Rt[:, dsl(d)]:
                      v.tensor_tensor(o, a, b, AL.add))
            p.waits("vector", tok_xcz)
            for d in range(ND):
                p.run("vector", lambda v, o=dtx[:, dsl(d)], a=dt32[:, dsl(d)], b=xcd[:, xsl(d)]:
                      v.tensor_tensor(o, a, b, AL.mult))
            tok_dt = p.mark("vector")
            prev_cons["dtpre"] = tok_dt
            p.waits("scalar", tok_dt)

            for s_ in range(DS):
                bs = s_ % 3
                usl = s_ % 2
                ysl = s_ % 2
                hbase = (s_ % 2) * ND
                p.waits("sync", bcons[bs])
                p.dma(f"bc{bs}", "sync", Bs[:, bs * LC:(bs + 1) * LC],
                      xd_ap(DTR + s_, 1, co, LC).partition_broadcast(128))
                p.dma(f"bc{bs}", "sync", Cs[:, bs * LC:(bs + 1) * LC],
                      xd_ap(DTR + DS + s_, 1, co, LC).partition_broadcast(128))
                tok_bc = p.close(f"bc{bs}")
                p.waits("vector", ucons[usl], tok_bc)
                tu = p.run("vector", lambda v, o=u_all[usl], a=dtx,
                           b=Bs[:, bs * LC:(bs + 1) * LC].unsqueeze(1).broadcast_to([128, ND, LC]):
                           v.tensor_tensor(o.rearrange("p (n l) -> p n l", l=LC), a.rearrange("p (n l) -> p n l", l=LC), b, AL.mult), inc=True)
                vlast = None
                for d in range(ND):
                    step = s_ * ND + d
                    sl = step % 3
                    hsl = hbase + d
                    p.waits("scalar", slotS[sl])
                    tda = p.run("scalar", lambda sc, o=dA[sl], i=dt32[:, dsl(d)],
                                a=atmp[:, d * DS + s_: d * DS + s_ + 1]:
                                sc.activation(o, i, AF.Exp, scale=a), inc=True)
                    if ct == 0 or ct == RESET_CT:
                        init = 0.0
                    else:
                        init = state_all[:, s_ * ND + d: s_ * ND + d + 1]
                    p.waits("vector", tda, tu, hcons[s_ % 2])
                    vlast = p.run("vector", lambda v, o=h8[:, hsl * LC:(hsl + 1) * LC],
                                  a=dA[sl], b=u_all[usl][:, d * LC:(d + 1) * LC], i=init:
                                  v.tensor_tensor_scan(o, a, b, i, AL.mult, AL.add), inc=True)
                    slotS[sl] = vlast
                pp = s_ % 2
                prodb = prod_all if pp == 0 else prod2
                p.waits("vector", pcons[pp])
                tpm = p.run("vector", lambda v, o=prodb,
                            a=h8[:, hbase * LC: (hbase + ND) * LC],
                            c=Cs[:, bs * LC:(bs + 1) * LC].unsqueeze(1).broadcast_to([128, ND, LC]):
                            v.tensor_tensor(o.rearrange("p (n l) -> p n l", l=LC),
                                            a.rearrange("p (n l) -> p n l", l=LC), c, AL.mult), inc=True)
                ucons[usl] = tpm
                bcons[bs] = tpm
                p.waits("tensor", tpm)
                if s_ == 0:
                    p.waits("tensor", ygate_read[0], ygate_read[1], ygate_read[2])
                for d in range(3):
                    for hh in range(LC // 512):
                        tmm = p.run("tensor", lambda tt, o=psy[d][:, hh * 512:(hh + 1) * 512],
                                    w=identb,
                                    r=prodb[:, d * LC + hh * 512: d * LC + (hh + 1) * 512],
                                    st=(s_ == 0), e_=(s_ == DS - 1):
                                    tt.matmul(o, w, r, start=st, stop=e_),
                                    inc=(d == 2 and hh == LC // 512 - 1))
                pcons[pp] = tmm
                if s_ == 0:
                    p.run("vector", lambda v, o=yacc[ysl], i=prodb[:, 3 * LC: 4 * LC]:
                          v.tensor_copy(o, i))
                else:
                    p.run("vector", lambda v, o=yacc[ysl], a=yacc[1 - ysl],
                          b=prodb[:, 3 * LC: 4 * LC]:
                          v.tensor_tensor(o, a, b, AL.add))
                p.waits("scalar", vlast)
                tcap = p.run("scalar", lambda sc, o=state_all[:, s_ * ND: s_ * ND + ND],
                             i=h8.rearrange("p (n l) -> p n l", l=LC)[:, hbase: hbase + ND, LC - 1]:
                             sc.copy(o, i), inc=True)
                hcons[s_ % 2] = tcap

            for d in range(ND):
                zsl = d % 2
                p.waits("scalar", prev_cons.get(("sz", zsl)))
                tsz = p.run("scalar", lambda sc, o=szg[:, zsl * LC:(zsl + 1) * LC],
                            i=zd[:, xsl(d)]: sc.activation(o, i, AF.Silu), inc=True)
                ysrc = psy[d] if d < 3 else yacc[(DS - 1) % 2]
                p.waits("vector", tsz, oy_tok[d], pcons[(DS - 1) % 2])
                tg1 = p.run("vector", lambda v, o=t1g[:, zsl * LC:(zsl + 1) * LC], i=xcd[:, xsl(d)],
                            a=dsk[:, d:d + 1], b=ysrc:
                            v.scalar_tensor_tensor(o, i, a, b, AL.mult, AL.add), inc=(d < 3))
                if d < 3:
                    ygate_read[d] = tg1
                tyg = p.run("vector", lambda v, o=ygd[:, dsl(d)], a=t1g[:, zsl * LC:(zsl + 1) * LC],
                            b=szg[:, zsl * LC:(zsl + 1) * LC]:
                            v.tensor_tensor(o, a, b, AL.mult), inc=True)
                prev_cons[("sz", zsl)] = tyg
                p.waits("sync", tyg)
                p.dma(f"oy{d}", "sync", yg_ap(d * 128, 128, co, LC), ygd[:, dsl(d)])
                oy_tok[d] = p.close(f"oy{d}")
            prev_cons[("ld", pc)] = [p.mark("vector"), p.mark("gpsimd"), p.mark("tensor")]

        if finalize:
            for name in list(p.ch):
                if p.ch[name]["open"]:
                    p.close(name)
            p.drain_all_channels("sync")
            block = es.enter_context(nc.Block())
            p.finalize(block)
    return nc


def build_back(nc=None, p=None, io=None, finalize=True):
    if nc is None:
        nc = bass.Bass("TRN2", target_bir_lowering=False, debug=False)
    dram = nc.dram_tensor
    if io is None:
        io = {}
        yg_in = dram("yg_in", [DI, TPC], BF16, kind="ExternalInput").ap()
        io["ygb_ap"] = lambda r0, nr, xoff, n: yg_in[r0:r0 + nr, xoff:xoff + n]
        for nm, shp in (("x1_in", [D, TPC]), ("u_in", [D, TPC])):
            io[nm] = dram(nm, shp, F32, kind="ExternalInput").ap()
        for nm, shp in (("w_out_T", [DI, D]), ("w_ffp_T", [D, H]), ("w_ffo_T", [DI, D]),
                        ("a_T", [D, E * R]), ("b_T", [E * R, D]), ("wr_T", [D, E]),
                        ("br_r", [1, E]), ("ln2_w_t", [128, NKD]), ("ln2_b_t", [128, NKD]),
                        ("repm_in", [8, 512])):
            io[nm] = dram(nm, shp, F32, kind="ExternalInput").ap()
        io["xo_o"] = dram("xo_o", [D, TPC], F32, kind="ExternalOutput").ap()
    (ygb_ap, x1_in, u_in, w_out_T, w_ffp_T, w_ffo_T, a_T, b_T, wr_T, br_r,
     ln2_w_t, ln2_b_t, repm_in, xo_o) = (
        io["ygb_ap"], io["x1_in"], io["u_in"], io["w_out_T"], io["w_ffp_T"],
        io["w_ffo_T"], io["a_T"], io["b_T"], io["wr_T"], io["br_r"],
        io["ln2_w_t"], io["ln2_b_t"], io["repm_in"], io["xo_o"])

    if "w_out_b" in io:
        (w_out_b, w_ffp_b, w_ffo_b, a_b, b_b, wr_b, br_b) = (
            io["w_out_b"], io["w_ffp_b"], io["w_ffo_b"], io["a_b"], io["b_b"],
            io["wr_b"], io["br_b"])
        prep_here = False
    else:
        w_out_b = dram("k_w_out_b", [DI, D], BF16)
        w_ffp_b = dram("k_w_ffp_b", [D, H], BF16)
        w_ffo_b = dram("k_w_ffo_b", [DI, D], BF16)
        a_b = dram("k_a_b", [D, E * R], BF16)
        b_b = dram("k_b_b", [E * R, D], BF16)
        wr_b = dram("k_wr_b", [D, E], BF16)
        br_b = dram("k_br_b", [1, E], BF16)
        prep_here = True
    v_s = dram("k_v_s", [D, TPC], F32).ap()
    t2_s = dram("k_t2_s", [D, TPC], BF16).ap()
    ff_s = dram("k_ff_s", [DI, TPC], BF16).ap()
    x2_s = dram("k_x2_s", [D, TPC], F32).ap()

    PF = "k_"
    es = contextlib.ExitStack()
    with es:
        sb = lambda name, shape, dt=F32: es.enter_context(nc.sbuf_tensor(PF + name, shape, dt))[:]
        pst = lambda name, shape, dt=F32: es.enter_context(nc.psum_tensor(PF + name, shape, dt))[:]
        if p is None:
            p = Prog(nc, es)
        if prep_here:
            p.dma("prep", "gpsimd", w_out_b.ap(), w_out_T)
            p.dma("prep", "gpsimd", w_ffp_b.ap(), w_ffp_T)
            p.dma("prep", "gpsimd", w_ffo_b.ap(), w_ffo_T)
            p.dma("prep", "gpsimd", a_b.ap(), a_T)
            p.dma("prep", "gpsimd", b_b.ap(), b_T)
            p.dma("prep", "gpsimd", wr_b.ap(), wr_T)
            p.dma("prep", "gpsimd", br_b.ap(), br_r)
            tok_prep = p.close("prep")
        else:
            tok_prep = io["prepk_tok"]

        PT1 = 1024
        with contextlib.ExitStack() as e1:
            sb1 = lambda name, shape, dt=F32: e1.enter_context(nc.sbuf_tensor(PF + "p1" + name, shape, dt))[:]
            pst1 = lambda name, shape, dt=F32: e1.enter_context(nc.psum_tensor(PF + name, shape, dt))[:]
            ygt = sb1("ygt", [128, NKI * PT1], BF16)
            ut = sb1("ut", [128, 2 * PT1])
            vt = sb1("vt", [128, 2 * PT1])
            vb = sb1("vb", [128, 2 * PT1], BF16)
            vrb = sb1("vrb", [128, 2 * 512])
            sqb = sb1("sqb", [128, 2 * PT1], BF16)
            t2t = sb1("t2t", [128, 2 * PT1], BF16)
            tmpA = sb1("tmpA", [128, PT1])
            tmpB = sb1("tmpB", [128, PT1])
            wbuf = [sb1(f"wbuf{b}", [128, NKI * 512], BF16) for b in range(2)]
            stats = {nm: sb1(nm, [1, PT1]) for nm in
                     ("m", "q", "var", "sd", "r", "nmr", "t1", "t2x")}
            ones_c = e1.enter_context(nc.sbuf_tensor(PF + "p1ones_c", [128, 1], BF16))
            ones_r = e1.enter_context(nc.sbuf_tensor(PF + "p1ones_r", [1, 128], F32))
            ln2w = sb1("ln2w", [128, NKD])
            ln2b = sb1("ln2b", [128, NKD])
            psMM = [pst1("p1psMM", [128, 512]), pst1("p1psMM2", [128, 512])]
            psA = pst1("p1psA", [1, 1024])
            psB = pst1("p1psB", [1, 1024])
            psR = pst1("p1psR", [128, 512])
            psN = pst1("p1psN", [128, 512])
            tok_c = p.run("vector", lambda v: v.memset(ones_c[:], 1.0), inc=True)
            tok_c2 = p.run("vector", lambda v: v.memset(ones_r[:], 1.0), inc=True)
            p.dma("res", "sync", ln2w, ln2_w_t)
            p.dma("res", "sync", ln2b, ln2_b_t)
            tok_res = p.close("res")
            for e in ENGS:
                p.waits(e, tok_prep, tok_res, tok_c, tok_c2)
            wfree = [None, None]
            evac = [None, None]
            ldone = {}
            for t in range(BACK_SPAN // PT1):
                xoff = t * PT1
                p.waits("sync", ldone.get("yg_cons"))
                for k in range(NKI):
                    p.dma("yg", "sync", ygt[:, k * PT1:(k + 1) * PT1],
                          ygb_ap(k * 128, 128, xoff, PT1))
                tok_yg = p.close("yg")
                p.waits("tensor", tok_yg)
                for m in range(NKD):
                    par = m % 2
                    g, mc = m // 4, m % 4
                    if mc == 0:
                        p.waits("sync", wfree[par_g := (g % 2)])
                        for k in range(NKI):
                            p.dma(f"w{par_g}", "sync", wbuf[par_g][:, k * 512:(k + 1) * 512],
                                  w_out_b.ap()[k * 128:(k + 1) * 128, g * 512:(g + 1) * 512])
                        tok_w = p.close(f"w{par_g}")
                        p.waits("tensor", tok_w)
                    usl = m % 2
                    for _tk in (ldone.get(("u", usl)) or []):
                        p.waits("scalar", _tk)
                    p.dma("u", "scalar", ut[:, usl * PT1:(usl + 1) * PT1],
                          u_in[m * 128:(m + 1) * 128, xoff: xoff + PT1])
                    tok_u = p.close("u")
                    vsl = m % 2
                    for h in range(PT1 // 512):
                        bank = (m * (PT1 // 512) + h) % 2
                        ps = psMM[bank]
                        p.waits("tensor", evac[bank])
                        for k in range(NKI):
                            p.run("tensor", lambda tt, o=ps[:, 0:512],
                                  w=wbuf[g % 2][:, k * 512 + (m % 4) * 128: k * 512 + (m % 4) * 128 + 128],
                                  r=ygt[:, k * PT1 + h * 512: k * PT1 + (h + 1) * 512],
                                  st=(k == 0), e_=(k == NKI - 1):
                                  tt.matmul(o, w, r, start=st, stop=e_))
                        tmm = p.mark("tensor")
                        p.waits("vector", tmm, tok_u)
                        tv = p.run("vector", lambda v, o=vt[:, vsl * PT1 + h * 512: vsl * PT1 + (h + 1) * 512],
                                   a=ut[:, usl * PT1 + h * 512: usl * PT1 + (h + 1) * 512],
                                   b=ps[:, 0:512]: v.tensor_tensor(o, a, b, AL.add), inc=True)
                        evac[bank] = tv
                    p.waits("scalar", tv)
                    p.dma("v", "scalar", v_s[m * 128:(m + 1) * 128, xoff: xoff + PT1],
                          vt[:, vsl * PT1:(vsl + 1) * PT1])
                    vstok = p.close("v")
                    ldone[("u", usl)] = [tv, vstok]
                    p.waits("scalar", tv)
                    tvb = p.run("scalar", lambda sc, o=vb[:, vsl * PT1:(vsl + 1) * PT1],
                                i=vt[:, vsl * PT1:(vsl + 1) * PT1]: sc.copy(o, i), inc=True)
                    tsq = p.run("scalar", lambda sc, o=sqb[:, vsl * PT1:(vsl + 1) * PT1],
                                i=vt[:, vsl * PT1:(vsl + 1) * PT1]: sc.activation(o, i, AF.Square), inc=True)
                    p.waits("tensor", tvb, tsq)
                    for h2 in range(PT1 // 512):
                        p.run("tensor", lambda tt, o=psA[0:1, h2 * 512:(h2 + 1) * 512], w=ones_c[:],
                              r=vb[:, vsl * PT1 + h2 * 512: vsl * PT1 + (h2 + 1) * 512],
                              st=(m == 0), e_=(m == NKD - 1):
                              tt.matmul(o, w, r, start=st, stop=e_, skip_group_check=True))
                        p.run("tensor", lambda tt, o=psB[0:1, h2 * 512:(h2 + 1) * 512], w=ones_c[:],
                              r=sqb[:, vsl * PT1 + h2 * 512: vsl * PT1 + (h2 + 1) * 512],
                              st=(m == 0), e_=(m == NKD - 1):
                              tt.matmul(o, w, r, start=st, stop=e_, skip_group_check=True))
                    ldone[("v", vsl)] = p.mark("tensor")
                    if mc == 3:
                        wfree[g % 2] = ldone[("v", vsl)]
                    if m == NKD - 1:
                        ldone["yg_cons"] = ldone[("v", vsl)]
                tok_mm = p.mark("tensor")
                st_ = stats
                p.waits("vector", tok_mm)
                p.run("vector", lambda v: v.tensor_scalar(st_["m"][:, 0:PT1], psA[0:1, 0:PT1], 1.0 / D, None, AL.mult))
                p.run("vector", lambda v: v.tensor_scalar(st_["q"][:, 0:PT1], psB[0:1, 0:PT1], 1.0 / D, None, AL.mult))
                p.run("vector", lambda v: v.tensor_tensor(st_["t1"][:, 0:PT1], st_["m"][:, 0:PT1], st_["m"][:, 0:PT1], AL.mult))
                p.run("vector", lambda v: v.tensor_tensor(st_["var"][:, 0:PT1], st_["q"][:, 0:PT1], st_["t1"][:, 0:PT1], AL.subtract))
                tv1 = p.run("vector", lambda v: v.tensor_scalar(st_["t2x"][:, 0:PT1], st_["var"][:, 0:PT1], EPS, None, AL.add), inc=True)
                p.waits("scalar", tv1)
                ts1 = p.run("scalar", lambda sc: sc.activation(st_["sd"][:, 0:PT1], st_["t2x"][:, 0:PT1], AF.Sqrt), inc=True)
                p.waits("vector", ts1)
                p.run("vector", lambda v: v.reciprocal(st_["r"][:, 0:PT1], st_["sd"][:, 0:PT1]))
                p.run("vector", lambda v: v.tensor_tensor(st_["t1"][:, 0:PT1], st_["m"][:, 0:PT1], st_["r"][:, 0:PT1], AL.mult))
                tv2 = p.run("vector", lambda v: v.tensor_scalar(st_["nmr"][:, 0:PT1], st_["t1"][:, 0:PT1], -1.0, None, AL.mult), inc=True)
                p.waits("tensor", tv2)
                p.waits("sync", vstok)
                for h in range(PT1 // 512):
                    p.run("tensor", lambda tt, o=psR[:, 0:512], r=st_["r"][0:1, h * 512:(h + 1) * 512]:
                          tt.matmul(o, ones_r[:], r, start=True, stop=True))
                    p.run("tensor", lambda tt, o=psN[:, 0:512], r=st_["nmr"][0:1, h * 512:(h + 1) * 512]:
                          tt.matmul(o, ones_r[:], r, start=True, stop=True))
                    trp = p.mark("tensor")
                    for m in range(NKD):
                        vsl2 = m % 2
                        p.waits("sync", ldone.get(("vr", vsl2)))
                        p.dma("vr", "sync", vrb[:, vsl2 * 512:(vsl2 + 1) * 512],
                              v_s[m * 128:(m + 1) * 128, xoff + h * 512: xoff + (h + 1) * 512])
                        tok_vr = p.close("vr")
                        p.waits("vector", trp, tok_vr, ldone.get(("t2s", vsl2)))
                        p.run("vector", lambda v, o=tmpA[:, 0:512], a=vrb[:, vsl2 * 512:(vsl2 + 1) * 512],
                              b=psR[:, 0:512]: v.tensor_tensor(o, a, b, AL.mult))
                        p.run("vector", lambda v, o=tmpB[:, 0:512], a=tmpA[:, 0:512],
                              b=psN[:, 0:512]: v.tensor_tensor(o, a, b, AL.add))
                        tt2 = p.run("vector", lambda v, o=t2t[:, vsl2 * 512:(vsl2 + 1) * 512], i=tmpB[:, 0:512],
                                    w=ln2w[:, m:m + 1], b=ln2b[:, m:m + 1]:
                                    v.tensor_scalar(o, i, w, b, AL.mult, AL.add), inc=True)
                        ldone[("vr", vsl2)] = tt2
                        p.waits("sync", tt2)
                        p.dma(f"t2{vsl2}", "sync",
                              t2_s[m * 128:(m + 1) * 128, xoff + h * 512: xoff + (h + 1) * 512],
                              t2t[:, vsl2 * 512:(vsl2 + 1) * 512])
                        ldone[("t2s", vsl2)] = p.close(f"t2{vsl2}")
                    p.waits("tensor", p.mark("vector"))
        for name in list(p.ch):
            if p.ch[name]["open"]:
                p.close(name)
        p.drain_all_channels("sync")
        p.barrier()

        PT2 = 2048
        with contextlib.ExitStack() as e2:
            sb2 = lambda name, shape, dt=F32: e2.enter_context(nc.sbuf_tensor(PF + "p2" + name, shape, dt))[:]
            pst2 = lambda name, shape, dt=F32: e2.enter_context(nc.psum_tensor(PF + name, shape, dt))[:]
            t2r = sb2("t2r", [128, NKD * PT2], BF16)
            a16 = sb2("a16", [128, 2 * 512], BF16)
            sg = sb2("sg", [128, 2 * 512], BF16)
            ff16 = sb2("ff16", [128, 2 * 512], BF16)
            wbufa = [sb2(f"wbufa{b}", [128, NKD * 512], BF16) for b in range(2)]
            wbufg = [sb2(f"wbufg{b}", [128, NKD * 512], BF16) for b in range(2)]
            psMM = [pst2("p2psMM", [128, 512]), pst2("p2psMM2", [128, 512])]
            evac = [None, None]
            wfree = [None, None, None, None]
            ldone = {}
            for t in range(BACK_SPAN // PT2):
                xoff = t * PT2
                for k in range(NKD):
                    p.waits("scalar", ldone.get(("t2", k)))
                    p.dma("t2r", "scalar", t2r[:, k * PT2:(k + 1) * PT2],
                          t2_s[k * 128:(k + 1) * 128, xoff: xoff + PT2])
                tok_t2 = p.close("t2r")
                for m in range(NKI):
                    gw, mw = m // 4, m % 4
                    par = gw % 2
                    if mw == 0:
                        p.waits("sync", wfree[par])
                        for k in range(NKD):
                            p.dma(f"wa{par}", "sync", wbufa[par][:, k * 512:(k + 1) * 512],
                                  w_ffp_b.ap()[k * 128:(k + 1) * 128, gw * 512:(gw + 1) * 512])
                        for k in range(NKD):
                            p.dma(f"wa{par}", "sync", wbufg[par][:, k * 512:(k + 1) * 512],
                                  w_ffp_b.ap()[k * 128:(k + 1) * 128, DI + gw * 512: DI + (gw + 1) * 512])
                        tok_w = p.close(f"wa{par}")
                    p.waits("tensor", tok_w, tok_t2)
                    for h in range(PT2 // 512):
                        bank = h % 2
                        psa = psMM[bank]
                        psg = psMM[1 - bank]
                        p.waits("tensor", evac[0], evac[1])
                        for k in range(NKD):
                            p.run("tensor", lambda tt, o=psa[:, 0:512],
                                  w=wbufa[par][:, k * 512 + mw * 128: k * 512 + mw * 128 + 128],
                                  r=t2r[:, k * PT2 + h * 512: k * PT2 + (h + 1) * 512],
                                  st=(k == 0), e_=(k == NKD - 1): tt.matmul(o, w, r, start=st, stop=e_))
                        tma = p.mark("tensor")
                        for k in range(NKD):
                            p.run("tensor", lambda tt, o=psg[:, 0:512],
                                  w=wbufg[par][:, k * 512 + mw * 128: k * 512 + mw * 128 + 128],
                                  r=t2r[:, k * PT2 + h * 512: k * PT2 + (h + 1) * 512],
                                  st=(k == 0), e_=(k == NKD - 1): tt.matmul(o, w, r, start=st, stop=e_))
                        tmg = p.mark("tensor")
                        hsl = h % 2
                        p.waits("scalar", tma, tmg, ldone.get(("ff", hsl)))
                        ta = p.run("scalar", lambda sc, o=a16[:, hsl * 512:(hsl + 1) * 512], i=psa:
                                   sc.copy(o, i[:, 0:512]), inc=True)
                        tg = p.run("scalar", lambda sc, o=sg[:, hsl * 512:(hsl + 1) * 512], i=psg:
                                   sc.activation(o, i[:, 0:512], AF.Silu), inc=True)
                        evac[0] = ta
                        evac[1] = tg
                        p.waits("vector", tg)
                        tf = p.run("vector", lambda v, o=ff16[:, hsl * 512:(hsl + 1) * 512],
                                   a=a16[:, hsl * 512:(hsl + 1) * 512], b=sg[:, hsl * 512:(hsl + 1) * 512]:
                                   v.tensor_tensor(o, a, b, AL.mult), inc=True)
                        p.waits("scalar", tf)
                        p.dma(f"ff{hsl}", "scalar", ff_s[m * 128:(m + 1) * 128, xoff + h * 512: xoff + (h + 1) * 512],
                              ff16[:, hsl * 512:(hsl + 1) * 512])
                        ldone[("ff", hsl)] = p.close(f"ff{hsl}")
                    if mw == 3:
                        wfree[par] = p.mark("tensor")
        for name in list(p.ch):
            if p.ch[name]["open"]:
                p.close(name)
        p.drain_all_channels("sync")
        p.barrier()

        PT3 = 1024
        with contextlib.ExitStack() as e3:
            sb3 = lambda name, shape, dt=F32: e3.enter_context(nc.sbuf_tensor(PF + "p3" + name, shape, dt))[:]
            pst3 = lambda name, shape, dt=F32: e3.enter_context(nc.psum_tensor(PF + name, shape, dt))[:]
            ffr = sb3("ffr", [128, NKI * PT3], BF16)
            x1t = sb3("x1t", [128, 2 * PT3])
            vtt = sb3("vtt", [128, 2 * PT3])
            e32 = sb3("e32", [128, 2 * PT3])
            x2t = sb3("x2t", [128, 2 * PT3])
            wbuf = [sb3(f"wbuf{b}", [128, NKI * 512], BF16) for b in range(2)]
            psMM = [pst3("p3psMM", [128, 512]), pst3("p3psMM2", [128, 512])]
            evac = [None, None]
            wfree = [None, None]
            ldone = {}
            for t in range(BACK_SPAN // PT3):
                xoff = t * PT3
                for k in range(NKI):
                    p.waits("sync", ldone.get(("ff", k)))
                    p.dma("ffr", "sync", ffr[:, k * PT3:(k + 1) * PT3],
                          ff_s[k * 128:(k + 1) * 128, xoff: xoff + PT3])
                tok_ff = p.close("ffr")
                for m in range(NKD):
                    gw, mw = m // 4, m % 4
                    par = gw % 2
                    if mw == 0:
                        p.waits("sync", wfree[par])
                        for k in range(NKI):
                            p.dma(f"w{par}", "sync", wbuf[par][:, k * 512:(k + 1) * 512],
                                  w_ffo_b.ap()[k * 128:(k + 1) * 128, gw * 512:(gw + 1) * 512])
                        tok_w = p.close(f"w{par}")
                    msl = m % 2
                    p.waits("scalar", ldone.get(("x2", msl)))
                    p.dma("x1l", "scalar", x1t[:, msl * PT3:(msl + 1) * PT3],
                          x1_in[m * 128:(m + 1) * 128, xoff: xoff + PT3])
                    p.dma("x1l", "scalar", vtt[:, msl * PT3:(msl + 1) * PT3],
                          v_s[m * 128:(m + 1) * 128, xoff: xoff + PT3])
                    tok_x1 = p.close("x1l")
                    p.waits("tensor", tok_w, tok_ff)
                    tx2 = None
                    for h in range(PT3 // 512):
                        bank = h % 2
                        ps = psMM[bank]
                        p.waits("tensor", evac[bank])
                        for k in range(NKI):
                            p.run("tensor", lambda tt, o=ps[:, 0:512],
                                  w=wbuf[par][:, k * 512 + mw * 128: k * 512 + mw * 128 + 128],
                                  r=ffr[:, k * PT3 + h * 512: k * PT3 + (h + 1) * 512],
                                  st=(k == 0), e_=(k == NKI - 1): tt.matmul(o, w, r, start=st, stop=e_))
                        tmm = p.mark("tensor")
                        p.waits("vector", tmm, tok_x1)
                        p.run("vector", lambda v, o=e32[:, msl * PT3 + h * 512: msl * PT3 + (h + 1) * 512],
                              a=vtt[:, msl * PT3 + h * 512: msl * PT3 + (h + 1) * 512],
                              b=ps[:, 0:512]: v.tensor_tensor(o, a, b[:, 0:512], AL.add))
                        tx2 = p.run("vector", lambda v, o=x2t[:, msl * PT3 + h * 512: msl * PT3 + (h + 1) * 512],
                                    a=e32[:, msl * PT3 + h * 512: msl * PT3 + (h + 1) * 512],
                                    b=x1t[:, msl * PT3 + h * 512: msl * PT3 + (h + 1) * 512]:
                                    v.tensor_tensor(o, a, b, AL.add), inc=True)
                        evac[bank] = tx2
                    p.waits("sync", tx2)
                    p.dma(f"x2{msl}", "sync", x2_s[m * 128:(m + 1) * 128, xoff: xoff + PT3],
                          x2t[:, msl * PT3:(msl + 1) * PT3])
                    ldone[("x2", msl)] = p.close(f"x2{msl}")
                    if mw == 3:
                        wfree[par] = p.mark("tensor")
        for name in list(p.ch):
            if p.ch[name]["open"]:
                p.close(name)
        p.drain_all_channels("sync")
        p.barrier()

        N = 512
        with contextlib.ExitStack() as e4:
            sb4 = lambda name, shape, dt=F32: e4.enter_context(nc.sbuf_tensor(PF + "p4" + name, shape, dt))[:]
            pst4 = lambda name, shape, dt=F32: e4.enter_context(nc.psum_tensor(PF + name, shape, dt))[:]
            al = {}
            al["xbt"] = sb4("xbt", [128, 2 * N], BF16)
            al["sqt"] = sb4("sqt", [128, 2 * N], BF16)
            al["ln_n"] = N
            al["tmp32"] = sb4("tmp32", [128, N])
            al["tmp32b"] = sb4("tmp32b", [128, N])
            for nm in ("st_m", "st_q", "st_var", "st_sd", "st_r", "st_nmr", "st_t1", "st_t2"):
                al[nm] = sb4(nm, [1, N])
            al["ones_col16"] = e4.enter_context(nc.sbuf_tensor(PF + "p4ones_c", [128, 1], BF16))
            al["ones_row32"] = e4.enter_context(nc.sbuf_tensor(PF + "p4ones_r", [1, 128], F32))
            al["ones_row16"] = sb4("ones_r16", [1, 128], BF16)
            al["ident"] = e4.enter_context(nc.sbuf_tensor(PF + "p4ident", [128, 128], BF16))
            zeros128 = e4.enter_context(nc.sbuf_tensor(PF + "p4zeros", [128, 128], BF16))
            al["lgs"] = sb4("lgs", [128, 32])
            al["mx"] = sb4("mx", [128, 4])
            al["nmx"] = sb4("nmx", [128, 4])
            al["sm"] = sb4("sm", [128, 4])
            al["wsc"] = sb4("wsc", [128, 4])
            al["w0"] = sb4("w0", [128, 4])
            al["exps"] = sb4("exps", [128, 32])
            al["maskb"] = sb4("maskb", [128, 32], BF16)
            al["maskT"] = sb4("maskT", [8, N], BF16)
            al["repm"] = sb4("repm", [8, 512], BF16)
            al["mrep16"] = sb4("mrep16", [128, N], BF16)
            al["h16"] = sb4("h16", [128, N], BF16)
            al["hm16"] = sb4("hm16", [128, 4 * N], BF16)
            al["ffn_n"] = N
            al["aT"] = sb4("aT", [128, NKD * 4 * 128], BF16)
            al["bT"] = sb4("bT", [128, 4 * NKD * 128], BF16)
            al["wrT"] = sb4("wrT", [128, NKD * 8], BF16)
            al["brT"] = e4.enter_context(nc.sbuf_tensor(PF + "p4brT", [1, 8], BF16))
            al["psA"] = pst4("p4psA", [1, 512])
            al["psB"] = pst4("p4psB", [1, 512])
            al["psR"] = pst4("p4psR", [128, 512])
            al["psN"] = pst4("p4psN", [128, 512])
            al["psS"] = pst4("p4psS", [128, 512])
            al["psRt"] = al["psS"][:, 0:32]
            al["psM"] = al["psR"]
            al["psH"] = al["psN"]
            psX4 = pst4("p4psX", [128, 512])
            al["psT"] = psX4[0:8, :].bitcast(BF16)
            x2b = sb4("x2b", [128, NKD * N])
            v2b = sb4("v2b", [128, NKD * N], BF16)
            xob = sb4("xob", [128, 4 * N])
            tok_z = p.run("vector", lambda v: v.memset(zeros128[:], 0.0), inc=True)
            p.waits("gpsimd", tok_z)
            tok_id = p.run("gpsimd", lambda g: g.affine_select(
                out=al["ident"][:], in_=zeros128[:], compare_op=AL.not_equal,
                fill=1.0, base=0, pattern=[[-1, 128]], channel_multiplier=1), inc=True)
            tok_c1 = p.run("gpsimd", lambda g: g.memset(al["ones_col16"][:], 1.0), inc=True)
            tok_c2 = p.run("gpsimd", lambda g: g.memset(al["ones_row32"][:], 1.0), inc=True)
            tok_c3 = p.run("gpsimd", lambda g: g.memset(al["ones_row16"][:], 1.0), inc=True)
            for k in range(NKD):
                p.dma("res4", "sync", al["aT"][:, k * 512:(k + 1) * 512],
                      a_b.ap()[k * 128:(k + 1) * 128, :])
            for c in range(4):
                p.dma("res4", "sync", al["bT"][:, c * 2048:(c + 1) * 2048],
                      b_b.ap()[c * 128:(c + 1) * 128, :])
            for k in range(NKD):
                p.dma("res4", "sync", al["wrT"][:, k * 8:(k + 1) * 8],
                      wr_b.ap()[k * 128:(k + 1) * 128, :])
            p.dma("res4", "sync", al["brT"][:], br_b.ap())
            tok_res = p.close("res4")
            p.dma("prep4", "gpsimd", al["repm"], repm_in)
            tok_rp = p.close("prep4")
            for e in ENGS:
                p.waits(e, tok_res, tok_rp, tok_id, tok_c1, tok_c2, tok_c3)
            p.barrier()
            oslot = {}
            prev_x2 = None
            for t in range(BACK_SPAN // N):
                xoff = t * N
                x2c = [x2b[:, k * N:(k + 1) * N] for k in range(NKD)]
                v2c = [v2b[:, k * N:(k + 1) * N] for k in range(NKD)]
                xoc = [xob[:, (k % 4) * N:(k % 4 + 1) * N] for k in range(NKD)]
                p.waits("sync", prev_x2)
                for k in range(NKD):
                    p.dma("x2r", "sync", x2c[k], x2_s[k * 128:(k + 1) * 128, xoff: xoff + N])
                tok_x2 = p.close("x2r")
                tok_v2 = emit_ln(p, al, N, x2c, tok_x2, out16c=v2c, tmp32=al["tmp32"])

                def pre_cb(m, oslot=oslot, t=t):
                    return oslot.get((t, m - 4)) or oslot.get((t - 1, m + NKD - 4))

                def post_cb(m, tok, xoff=xoff, t=t, xoc=xoc):
                    p.waits("scalar", tok)
                    p.dma(f"o{m % 4}", "scalar", xo_o[m * 128:(m + 1) * 128, xoff: xoff + N], xoc[m])
                    oslot[(t, m)] = p.close(f"o{m % 4}")

                tok_out = emit_moe(p, al, N, v2c, tok_v2, x2c, tok_x2, xoc, 0.5,
                                   pre_cb=pre_cb, post_cb=post_cb)
                prev_x2 = tok_out

        if finalize:
            for name in list(p.ch):
                if p.ch[name]["open"]:
                    p.close(name)
            p.drain_all_channels("sync")
            block = es.enter_context(nc.Block())
            p.finalize(block)
    return nc



def build_mega():
    from concourse.aot_env import allow_ici_all_to_all
    nc = bass.Bass("TRN2", target_bir_lowering=False, debug=False)
    dram = nc.dram_tensor
    io_f = {}
    for nm, shp in (("x_in", [D, TPC + 3]), ("halo_mask", [128, 3]),
                    ("w_in_T", [D, 2 * DI]), ("w_xp_T", [DI, 160]),
                    ("a_T", [D, E * R]), ("b_T", [E * R, D]), ("wr_T", [D, E]),
                    ("br_r", [1, E]), ("conv_w_t", [128, NKI * 4]),
                    ("conv_b_t", [128, NKI]), ("ln1_w_t", [128, NKD]),
                    ("ln1_b_t", [128, NKD]), ("repm_in", [8, 512])):
        io_f[nm] = dram(nm, shp, F32, kind="ExternalInput").ap()
    io_s = {}
    for nm, shp in (("wdt_j", [DTR, CPC]), ("db_j", [128, CPC // 128]),
                    ("alog_j", [CPC, DS]), ("dsk_j", [128, CPC // 128])):
        io_s[nm] = dram(nm, shp, F32, kind="ExternalInput").ap()
    io_s["identb_j"] = dram("identb_j", [128, 128], BF16, kind="ExternalInput").ap()
    io_b = {}
    for nm, shp in (("w_out_T", [DI, D]), ("w_ffp_T", [D, H]), ("w_ffo_T", [DI, D]),
                    ("ln2_w_t", [128, NKD]), ("ln2_b_t", [128, NKD])):
        io_b[nm] = dram(nm, shp, F32, kind="ExternalInput").ap()
    io_b["a_T"] = io_f["a_T"]; io_b["b_T"] = io_f["b_T"]
    io_b["wr_T"] = io_f["wr_T"]; io_b["br_r"] = io_f["br_r"]
    io_b["repm_in"] = io_f["repm_in"]
    io_b["xo_o"] = dram("xo_o", [D, TPC], F32, kind="ExternalOutput").ap()

    xc_x = dram("xc_x", [DI, TPC], BF16)
    z_x = dram("z_x", [DI, TPC], BF16)
    xd_g = dram("xd_g", [160, TPC], BF16)
    xc_sh = dram("xc_sh", [NC * CPC, TPC], BF16)
    z_sh = dram("z_sh", [NC * CPC, TPC], BF16)
    xd_sh = dram("xd_sh", [NC * 160, TPC], BF16)
    yg_x = dram("yg_x", [NC * CPC, TPC], BF16)
    yg_sh = dram("yg_sh", [NC * CPC, TPC], BF16)
    x1_s = dram("x1_s", [D, TPC], F32)
    u_s = dram("u_s", [D, TPC], F32)

    io_f["xc_o"] = xc_x.ap()
    io_f["z_o"] = z_x.ap()
    io_f["xd_o"] = xd_g.ap()
    io_f["x1_o"] = x1_s.ap()
    io_f["u_o"] = u_s.ap()

    xcs, zs, xds, ygs = xc_sh.ap(), z_sh.ap(), xd_sh.ap(), yg_x.ap()
    io_s["xc_ap"] = lambda r0, nr, co, ln: xcs[(co // TPC) * CPC + r0:(co // TPC) * CPC + r0 + nr, co % TPC: co % TPC + ln]
    io_s["z_ap"] = lambda r0, nr, co, ln: zs[(co // TPC) * CPC + r0:(co // TPC) * CPC + r0 + nr, co % TPC: co % TPC + ln]
    io_s["xd_ap"] = lambda r0, nr, co, ln: xds[(co // TPC) * 160 + r0:(co // TPC) * 160 + r0 + nr, co % TPC: co % TPC + ln]
    io_s["yg_ap"] = lambda r0, nr, co, ln: ygs[(co // TPC) * CPC + r0:(co // TPC) * CPC + r0 + nr, co % TPC: co % TPC + ln]
    ygshr = yg_sh.ap()
    io_b["ygb_ap"] = lambda r0, nr, xoff, n: ygshr[r0:r0 + nr, xoff: xoff + n]
    io_b["x1_in"] = x1_s.ap()
    io_b["u_in"] = u_s.ap()

    es = contextlib.ExitStack()
    with es:
        p = Prog(nc, es)
        kw = {}
        for nm, shp in (("w_out_b", [DI, D]), ("w_ffp_b", [D, H]), ("w_ffo_b", [DI, D]),
                        ("a_b", [D, E * R]), ("b_b", [E * R, D]), ("wr_b", [D, E]),
                        ("br_b", [1, E])):
            kw[nm] = dram("k_" + nm, shp, BF16)
        def post_prep_cb():
            for nm, src in (("w_out_b", "w_out_T"), ("w_ffp_b", "w_ffp_T"),
                            ("w_ffo_b", "w_ffo_T"), ("a_b", "a_T"), ("b_b", "b_T"),
                            ("wr_b", "wr_T"), ("br_b", "br_r")):
                p.dma("prepk", "gpsimd", kw[nm].ap(), io_b[src])
            io_b["prepk_tok"] = p.close("prepk")
        io_f["post_prep_cb"] = post_prep_cb
        io_b.update(kw)
        rg = [list(range(NC))]

        def cc(kind, i, o):
            def f(g, kind=kind, i=i, o=o):
                with allow_ici_all_to_all():
                    return g.collective_compute(kind, AL.bypass, replica_groups=rg,
                                                ins=[i], outs=[o])
            return p.run("gpsimd", f, inc=True)

        def mid_cb():
            for name in list(p.ch):
                if p.ch[name]["open"]:
                    p.close(name)
            p.drain_all_channels("gpsimd")
            cc("AllToAll", xc_x.ap().opt(), xc_sh.ap().opt())
            cc("AllToAll", z_x.ap().opt(), z_sh.ap().opt())

        build_front(nc=nc, p=p, io=io_f, finalize=False, mid_cb=mid_cb)
        for name in list(p.ch):
            if p.ch[name]["open"]:
                p.close(name)
        p.barrier()
        p.drain_all_channels("gpsimd")
        tok_fwd = cc("AllGather", xd_g.ap().opt(), xd_sh.ap().opt())
        for e in ENGS:
            p.waits(e, tok_fwd)
        p.barrier()
        build_scan(nc=nc, p=p, io=io_s, finalize=False)
        for name in list(p.ch):
            if p.ch[name]["open"]:
                p.close(name)
        p.barrier()
        p.drain_all_channels("gpsimd")
        tok_bwd = cc("AllToAll", yg_x.ap().opt(), yg_sh.ap().opt())
        for e in ENGS:
            p.waits(e, tok_bwd)
        p.barrier()
        build_back(nc=nc, p=p, io=io_b, finalize=False)
        for name in list(p.ch):
            if p.ch[name]["open"]:
                p.close(name)
        p.drain_all_channels("sync")
        block = es.enter_context(nc.Block())
        p.finalize(block)
    return nc


def _mega_inputs(inputs):
    front = _front_inputs(inputs)
    wdt = host_f32(inputs["L0_attn_dt_proj_w"]).T.copy()
    db = host_f32(inputs["L0_attn_dt_proj_b"])
    alog = host_f32(inputs["L0_attn_A_log"])
    dsk = host_f32(inputs["L0_attn_D_skip"])
    w_out_T = host_f32(inputs["L0_attn_out_proj_w"]).T.copy()
    w_ffp_T = host_f32(inputs["L0_attn_ff_proj_w"]).T.copy()
    w_ffo_T = host_f32(inputs["L0_attn_ff_out_w"]).T.copy()
    ln2_w_t = colsT(inputs["L0_attn_ln2_w"], NKD)
    ln2_b_t = colsT(inputs["L0_attn_ln2_b"], NKD)
    in_maps = []
    for i in range(NC):
        im = dict(front[i])
        c0 = i * CPC
        im.update(dict(
            wdt_j=np.ascontiguousarray(wdt[:, c0:c0 + CPC]),
            db_j=np.ascontiguousarray(db[c0:c0 + CPC].reshape(CPC // 128, 128).T),
            alog_j=np.ascontiguousarray(alog[c0:c0 + CPC]),
            dsk_j=np.ascontiguousarray(dsk[c0:c0 + CPC].reshape(CPC // 128, 128).T),
            identb_j=np.eye(128, dtype=BF16_NP),
            w_out_T=w_out_T, w_ffp_T=w_ffp_T, w_ffo_T=w_ffo_T,
            ln2_w_t=ln2_w_t, ln2_b_t=ln2_b_t))
        in_maps.append(im)
    return in_maps


def _front_inputs(inputs):
    x = host_f32(inputs["L0_attn_x"]).reshape(TOK, D)
    w_in_T = host_f32(inputs["L0_attn_in_proj_w"]).T.copy()
    w_xp_T = host_f32(inputs["L0_attn_x_proj_w"]).T.copy()
    A = host_f32(inputs["L0_ffn_A"])
    Bw = host_f32(inputs["L0_ffn_B"])
    a_T = np.ascontiguousarray(A.transpose(2, 0, 1).reshape(D, E * R))
    b_T = np.ascontiguousarray(Bw.transpose(0, 2, 1).reshape(E * R, D))
    wr_T = host_f32(inputs["L0_ffn_Wr"]).T.copy()
    br_r = host_f32(inputs["L0_ffn_br"]).reshape(1, E)
    conv_w = host_f32(inputs["L0_attn_conv_w"]).reshape(DI, 4)
    conv_w_t = np.ascontiguousarray(conv_w.reshape(NKI, 128, 4).transpose(1, 0, 2).reshape(128, NKI * 4))
    conv_b_t = colsT(inputs["L0_attn_conv_b"], NKI)
    ln1_w_t = colsT(inputs["L0_attn_ln1_w"], NKD)
    ln1_b_t = colsT(inputs["L0_attn_ln1_b"], NKD)
    repm = np.zeros((8, 512), np.float32)
    for col in range(512):
        repm[col // 64, col] = 1.0
    shared = dict(w_in_T=w_in_T, w_xp_T=w_xp_T, a_T=a_T, b_T=b_T, wr_T=wr_T,
                  br_r=br_r, conv_w_t=conv_w_t, conv_b_t=conv_b_t,
                  ln1_w_t=ln1_w_t, ln1_b_t=ln1_b_t, repm_in=repm)
    in_maps = []
    for i in range(NC):
        s0 = i * TPC
        x_ext = np.zeros((TPC + 3, D), np.float32)
        if i % 4 != 0:
            x_ext[0:3] = x[s0 - 3: s0]
        x_ext[3:] = x[s0: s0 + TPC]
        hm = np.zeros((128, 3), np.float32) if i % 4 == 0 else np.ones((128, 3), np.float32)
        im = dict(shared)
        im["x_in"] = np.ascontiguousarray(x_ext.T)
        im["halo_mask"] = hm
        in_maps.append(im)
    return in_maps


def _scan_inputs(inputs, xc, z, xd):
    wdt = host_f32(inputs["L0_attn_dt_proj_w"]).T.copy()
    db = host_f32(inputs["L0_attn_dt_proj_b"])
    alog = host_f32(inputs["L0_attn_A_log"])
    dsk = host_f32(inputs["L0_attn_D_skip"])
    in_maps = []
    for j in range(NC):
        c0 = j * CPC
        in_maps.append(dict(
            xc_in=np.ascontiguousarray(xc[c0:c0 + CPC]),
            z_in=np.ascontiguousarray(z[c0:c0 + CPC]),
            xd_in=xd,
            wdt_j=np.ascontiguousarray(wdt[:, c0:c0 + CPC]),
            db_j=np.ascontiguousarray(db[c0:c0 + CPC].reshape(CPC // 128, 128).T),
            alog_j=np.ascontiguousarray(alog[c0:c0 + CPC]),
            dsk_j=np.ascontiguousarray(dsk[c0:c0 + CPC].reshape(CPC // 128, 128).T),
            identb_j=np.eye(128, dtype=BF16_NP),
        ))
    return in_maps


def _back_inputs(inputs, yg, x1s, us):
    w_out_T = host_f32(inputs["L0_attn_out_proj_w"]).T.copy()
    w_ffp_T = host_f32(inputs["L0_attn_ff_proj_w"]).T.copy()
    w_ffo_T = host_f32(inputs["L0_attn_ff_out_w"]).T.copy()
    A = host_f32(inputs["L0_ffn_A"])
    Bw = host_f32(inputs["L0_ffn_B"])
    a_T = np.ascontiguousarray(A.transpose(2, 0, 1).reshape(D, E * R))
    b_T = np.ascontiguousarray(Bw.transpose(0, 2, 1).reshape(E * R, D))
    wr_T = host_f32(inputs["L0_ffn_Wr"]).T.copy()
    br_r = host_f32(inputs["L0_ffn_br"]).reshape(1, E)
    ln2_w_t = colsT(inputs["L0_attn_ln2_w"], NKD)
    ln2_b_t = colsT(inputs["L0_attn_ln2_b"], NKD)
    repm = np.zeros((8, 512), np.float32)
    for col in range(512):
        repm[col // 64, col] = 1.0
    shared = dict(w_out_T=w_out_T, w_ffp_T=w_ffp_T, w_ffo_T=w_ffo_T, a_T=a_T,
                  b_T=b_T, wr_T=wr_T, br_r=br_r, ln2_w_t=ln2_w_t,
                  ln2_b_t=ln2_b_t, repm_in=repm)
    in_maps = []
    for i in range(NC):
        im = dict(shared)
        im["yg_in"] = np.ascontiguousarray(yg[:, i * TPC:(i + 1) * TPC])
        im["x1_in"] = x1s[i]
        im["u_in"] = us[i]
        in_maps.append(im)
    return in_maps


_NC_CACHE = {}


def kernel(**inputs):
    cores = list(range(NC))
    if MODE == "mega":
        if "mega" not in _NC_CACHE:
            _NC_CACHE["mega"] = build_mega()
        res = run_bass_kernel_spmd(_NC_CACHE["mega"], _mega_inputs(inputs), core_ids=cores)
        out = np.concatenate([np.asarray(r["xo_o"]) for r in res.results], axis=1)
        return np.ascontiguousarray(out.T).reshape(B_, L_, D)
    if "front" not in _NC_CACHE:
        _NC_CACHE["front"] = build_front()
        _NC_CACHE["scan"] = build_scan()
        _NC_CACHE["back"] = build_back()
    resA = run_bass_kernel_spmd(_NC_CACHE["front"], _front_inputs(inputs), core_ids=cores)
    xc = np.concatenate([np.asarray(r["xc_o"]) for r in resA.results], axis=1)
    z = np.concatenate([np.asarray(r["z_o"]) for r in resA.results], axis=1)
    xd = np.ascontiguousarray(np.concatenate([np.asarray(r["xd_o"]) for r in resA.results], axis=1))
    x1s = [np.asarray(r["x1_o"]) for r in resA.results]
    us = [np.asarray(r["u_o"]) for r in resA.results]
    resB = run_bass_kernel_spmd(_NC_CACHE["scan"], _scan_inputs(inputs, xc, z, xd), core_ids=cores)
    yg = np.concatenate([np.asarray(r["yg_o"]) for r in resB.results], axis=0)
    resC = run_bass_kernel_spmd(_NC_CACHE["back"], _back_inputs(inputs, yg, x1s, us), core_ids=cores)
    out = np.concatenate([np.asarray(r["xo_o"]) for r in resC.results], axis=1)
    return np.ascontiguousarray(out.T).reshape(B_, L_, D)
